# version 03 (final)
# baseline (speedup 1.0000x reference)
import numpy as np
from contextlib import ExitStack
import concourse.bass as bass
import concourse.mybir as mybir
from concourse.bass_utils import run_bass_kernel_spmd

F32 = mybir.dt.float32
F16 = mybir.dt.float16
BF16 = mybir.dt.bfloat16
AF = mybir.ActivationFunctionType
ALU = mybir.AluOpType
AX = mybir.AxisListType

GUARD = 2
NT = 4096
D = 1024
NCH = 8
TT = 512
MASKVAL = -60000.0

ENGINES = ("sync", "scalar", "vector", "tensor", "gpsimd")


class Sem:
    __slots__ = ("name", "n", "h")

    def __init__(self, name):
        self.name = name
        self.n = 0
        self.h = None

    def w(self, val=None):
        return (self, self.n if val is None else val)


class P:
    def __init__(self, nc):
        self.nc = nc
        self.q = {e: [] for e in ENGINES}
        self.sems = []

    def sem(self, name):
        for s in self.sems:
            if s.name == name:
                return s
        s = Sem(name)
        self.sems.append(s)
        return s

    def op(self, eng, build, waits=(), inc=None, inc_by=1):
        if inc is not None:
            inc.n += inc_by
        ws = [(s, v) for (s, v) in waits if v > 0]

        def f(e, build=build, ws=ws, inc=inc, by=inc_by):
            for (s, v) in ws:
                e.wait_ge(s.h, v)
            ins = build(e)
            if inc is not None:
                ins.then_inc(inc.h, by)
        self.q[eng].append(f)

    def wait(self, eng, *waits):
        ws = [(s, v) for (s, v) in waits if v > 0]

        def f(e, ws=ws):
            for (s, v) in ws:
                e.wait_ge(s.h, v)
        self.q[eng].append(f)

    def alloc_sems(self, ctx):
        for i, s in enumerate(self.sems):
            s.h = ctx.enter_context(self.nc.semaphore(f"{s.name}_{i}"))

    def emit(self, block):
        for eng in ENGINES:
            ops = self.q[eng]
            if ops:
                getattr(block, eng)(lambda e, ops=ops: [f(e) for f in ops])


def cdiv(a, b):
    return (a + b - 1) // b


class _NcProxy:

    def __init__(self, nc, tag):
        self._nc = nc
        self._tag = tag

    def sbuf_tensor(self, name, shape, dtype, **kw):
        return self._nc.sbuf_tensor(self._tag + name, shape, dtype, **kw)

    def psum_tensor(self, name, *a, **kw):
        return self._nc.psum_tensor(self._tag + name, *a, **kw)

    def __getattr__(self, k):
        return getattr(self._nc, k)



class ProjPipe:

    def __init__(self, p, psum_banks, name):
        self.p = p
        self.banks = psum_banks
        self.nb = len(psum_banks)
        self.sMM = p.sem(name + "_mm")
        self.sEv = p.sem(name + "_ev")
        self.bEv = self.sEv.n
        self.g = 0

    def group(self, lhsT_list, rhs_list, evac, extra_wait=(), extra_mm=(),
              evac_eng="vector", width=None, rows=None, evac_wait=()):
        g = self.g
        bank = self.banks[g % self.nb]
        ap = bank[0:rows if rows else 128, 0:width if width else bank.shape[1]]
        waits = list(extra_wait)
        if g >= self.nb:
            waits.append(self.sEv.w(self.bEv + g - self.nb + 1))
        n = len(lhsT_list)
        mms = [(lhsT_list[k], rhs_list[k]) for k in range(n)] + list(extra_mm)
        for k, (lh, rh) in enumerate(mms):
            def mk(e, lh=lh, rh=rh, f=(k == 0), l=(k == len(mms) - 1), ap=ap):
                return e.matmul(ap, lh, rh, start=f, stop=l)
            self.p.op("tensor", mk, waits=waits if k == 0 else (),
                      inc=self.sMM if k == len(mms) - 1 else None)
        mm_done = self.sMM.w()
        evacs = evac if isinstance(evac, (list, tuple)) else [evac]
        for i, ef in enumerate(evacs):
            def ev(e, ef=ef, ap=ap):
                return ef(e, ap)
            self.p.op(evac_eng, ev,
                      waits=([mm_done] + list(evac_wait)) if i == 0 else (),
                      inc=self.sEv if i == len(evacs) - 1 else None)
        self.g += 1

    def drain(self):
        return self.sEv.w()


class AttnPipe:

    def __init__(self, p, bufs, name, exp_scale=1.0):
        self.p = p
        self.b = bufs
        self.exp_scale = exp_scale
        self.u = 0
        self.sS = p.sem(name + "S")
        self.sRm = p.sem(name + "Rm")
        self.sEx = p.sem(name + "Ex")
        self.sRc = p.sem(name + "Rc")
        self.sPn = p.sem(name + "Pn")
        self.sTr = p.sem(name + "Tr")
        self.sTe = p.sem(name + "Te")
        self.sPV = p.sem(name + "PV")
        self.sEo = p.sem(name + "Eo")
        self.o = {id(s): s.n for s in (self.sS, self.sRm, self.sEx, self.sRc,
                                       self.sPn, self.sTr, self.sTe, self.sPV,
                                       self.sEo)}
        self._pending = None

    def _r(self, sem, v):
        return (sem, self.o[id(sem)] + v)

    def _part1(self, u, d):
        p, b = self.p, self.b
        si = b["psS"][u % 2]
        waits = list(d.get("wait", ()))
        if u >= 2:
            waits.append(self._r(self.sEx, u - 1))

        def mmS(e, d=d, si=si):
            return e.matmul(si[:, 0:d["jn"]], d["s_lhsT"], d["s_rhs"],
                            start=True, stop=False)
        p.op("tensor", mmS, waits=waits)

        def mmM(e, d=d, si=si):
            return e.matmul(si[:, 0:d["jn"]], d["m_lhsT"], d["m_rhs"],
                            start=False, stop=True)
        p.op("tensor", mmM, inc=self.sS)

    def _vs(self, u, d):
        p, b = self.p, self.b
        si = b["psS"][u % 2]
        sl = u % 2
        nds = b["nds"]
        nm = nds[:, sl:sl + 1]
        dn = nds[:, 2 + sl:3 + sl]
        rc = nds[:, 4 + sl:5 + sl]

        def rmx(e, d=d, si=si, nm=nm):
            return e.reduce_max(nm, si[:, 0:d["jn"]], axis=AX.X, negate=True)
        p.op("vector", rmx, waits=[self._r(self.sS, u + 1), self._r(self.sEx, u - 1)], inc=self.sRm)

        def ex(e, d=d, si=si, nm=nm, dn=dn, sl=sl):
            return e.activation(b["Pbuf"][sl][:, 0:d["jn"]], si[:, 0:d["jn"]], AF.Exp,
                                bias=nm, scale=self.exp_scale, accum_out=dn)
        p.op("scalar", ex, waits=[self._r(self.sRm, u + 1), self._r(self.sPn, u - 1),
                                  self._r(self.sRc, u - 1)], inc=self.sEx)

        def rcp(e, dn=dn, rc=rc):
            return e.reciprocal(rc, dn)
        p.op("vector", rcp, waits=[self._r(self.sEx, u + 1)], inc=self.sRc)

        def pn(e, d=d, sl=sl, rc=rc):
            return e.tensor_scalar_mul(b["Pn"][sl][:, 0:d["jn"]],
                                       b["Pbuf"][sl][:, 0:d["jn"]], rc)
        p.op("vector", pn, waits=[self._r(self.sRc, u + 1), self._r(self.sTr, u - 1)], inc=self.sPn)

    def _part2(self, u, d):
        p, b = self.p, self.b
        sl = u % 2
        for c in range(d["nj"]):
            def tr(e, d=d, sl=sl, c=c):
                return e.transpose(b["psT"][sl][:, c * 128:(c + 1) * 128],
                                   b["Pn"][sl][:, c * 128:(c + 1) * 128], d["ident"])
            p.op("tensor", tr,
                 waits=[self._r(self.sPn, u + 1), self._r(self.sTe, u - 1)] if c == 0 else (),
                 inc=self.sTr if c == d["nj"] - 1 else None)

        def evt(e, d=d, sl=sl):
            return e.tensor_copy(b["Pt"][sl][:, 0:d["jn"]], b["psT"][sl][:, 0:d["jn"]])
        p.op("vector", evt, waits=[self._r(self.sTr, u + 1), self._r(self.sPV, u - 1)], inc=self.sTe)
        oi = b["psO"][sl]
        for c in range(d["nj"]):
            def pv(e, d=d, oi=oi, c=c, sl=sl):
                return e.matmul(oi[0:64, :], d["pv_lhsT"](c),
                                b["Pt"][sl][:, c * 128:(c + 1) * 128],
                                start=(c == 0), stop=(c == d["nj"] - 1))
            p.op("tensor", pv,
                 waits=[self._r(self.sTe, u + 1), self._r(self.sEo, u - 1)] if c == 0 else (),
                 inc=self.sPV if c == d["nj"] - 1 else None)

        def evo(e, d=d, oi=oi):
            return e.tensor_copy(d["out_dst"], oi[0:64, :])
        p.op("vector", evo, waits=[self._r(self.sPV, u + 1)], inc=self.sEo)

    def add(self, units):
        for d in units:
            u = self.u
            self._part1(u, d)
            self._vs(u, d)
            if self._pending is not None:
                self._part2(u - 1, self._pending)
            self._pending = d
            self.u += 1

    def flush(self):
        if self._pending is not None:
            self._part2(self.u - 1, self._pending)
            self._pending = None
        return self.sEo.w()


def emit_ln(p, stream, bufs, sems, affine=None, shifted=False, out=None,
            pre_wait=(), eps=1e-5, ntok=NT):
    sq, psR = bufs["sq"], bufs["psR"]
    srow, qrow, rstd = bufs["srow"], bufs["qrow"], bufs["rstd"]
    rstd16, murs16 = bufs["rstd16"], bufs["murs16"]
    onesc, ones1 = bufs["onesc"], bufs["ones1"]
    nch = len(stream)
    sSq = p.sem("lnSq")
    sMM = p.sem("lnMM")
    sEv = p.sem("lnEv")
    sVs = p.sem("lnVs")
    sSc = p.sem("lnSc")
    sBc = p.sem("lnBc")
    sAp = p.sem("lnAp")
    bMM, bAp = sMM.n, sAp.n

    def sview(c, t0, t1):
        if shifted and c >= nch // 2:
            return stream[c][:, GUARD - 1 + t0:GUARD - 1 + t1]
        return stream[c][:, GUARD + t0:GUARD + t1]

    ntile = ntok // TT
    for ti in range(ntile):
        t0, t1 = ti * TT, (ti + 1) * TT
        for c in range(nch):
            g = ti * nch + c
            sqb = sq[g % 2]
            waits = list(pre_wait) if g == 0 else []
            if g >= 2:
                waits.append((sMM, bMM + 2 * (g - 1)))

            def mkq(e, c=c, t0=t0, t1=t1, sqb=sqb):
                return e.tensor_tensor(sqb[:, :], sview(c, t0, t1), sview(c, t0, t1), ALU.mult)
            p.op("vector", mkq, waits=waits, inc=sSq)
            sq_done = sSq.w()
            waits2 = list(pre_wait) if g == 0 else []
            if c == 0 and ti >= 1:
                waits2.append((sAp, bAp + ti))

            def mmS(e, c=c, t0=t0, t1=t1):
                return e.matmul(psR[0][0:1, :], onesc, sview(c, t0, t1),
                                start=(c == 0), stop=(c == nch - 1))
            p.op("tensor", mmS, waits=waits2, inc=sMM)

            def mmQ(e, c=c, sqb=sqb):
                return e.matmul(psR[1][0:1, :], onesc, sqb[:, :],
                                start=(c == 0), stop=(c == nch - 1))
            p.op("tensor", mmQ, waits=[sq_done], inc=sMM)
        mm_done = sMM.w()
        p.op("vector", lambda e: e.tensor_copy(srow[:, :], psR[0][0:1, :]),
             waits=[mm_done], inc=sVs)
        p.op("vector", lambda e: e.tensor_copy(qrow[:, :], psR[1][0:1, :]), inc=sVs)
        p.op("vector", lambda e: e.tensor_scalar_mul(srow[:, :], srow[:, :], 1.0 / D),
             waits=[sVs.w()], inc=sVs)
        p.op("vector", lambda e: e.tensor_tensor(rstd[:, :], srow[:, :], srow[:, :], ALU.mult),
             waits=[sVs.w()], inc=sVs)
        p.op("vector", lambda e: e.scalar_tensor_tensor(rstd[:, :], qrow[:, :], 1.0 / D,
                                                        rstd[:, :], ALU.mult, ALU.subtract),
             waits=[sVs.w()], inc=sVs)
        p.op("vector", lambda e: e.tensor_scalar_add(rstd[:, :], rstd[:, :], eps),
             waits=[sVs.w()], inc=sVs)
        p.op("scalar", lambda e: e.sqrt(rstd[:, :], rstd[:, :]),
             waits=[sVs.w()], inc=sSc)
        p.op("vector", lambda e: e.reciprocal(rstd[:, :], rstd[:, :]),
             waits=[sSc.w()], inc=sVs)
        p.op("vector", lambda e: e.tensor_copy(rstd16[:, :], rstd[:, :]),
             waits=[sVs.w()], inc=sVs)
        p.op("vector", lambda e: e.tensor_tensor(murs16[:, :], srow[:, :], rstd[:, :], ALU.mult),
             waits=[sVs.w()], inc=sVs)
        prep_done = sVs.w()
        def mb1(e):
            return e.matmul(psR[0][:, :], ones1, rstd16[:, :], start=True, stop=True)
        p.op("tensor", mb1, waits=[prep_done], inc=sBc)

        def mb2(e):
            return e.matmul(psR[1][:, :], ones1, murs16[:, :], start=True, stop=True)
        p.op("tensor", mb2, inc=sBc)
        bc_done = sBc.w()
        for c in range(nch):
            dslice = (stream[c][:, GUARD + t0:GUARD + t1] if out is None
                      else out[c][:, t0:t1])

            def ap1(e, c=c, t0=t0, t1=t1, dslice=dslice):
                return e.tensor_tensor(dslice, sview(c, t0, t1), psR[0][:, :], ALU.mult)
            p.op("vector", ap1, waits=[bc_done] if c == 0 else ())
            last = (c == nch - 1)

            def ap2(e, dslice=dslice):
                return e.tensor_tensor(dslice, dslice, psR[1][:, :], ALU.subtract)
            p.op("vector", ap2, inc=sAp if (last and affine is None) else None)
            if affine is not None:
                g_col, b_col = affine(c)

                def ap3(e, dslice=dslice, g_col=g_col, b_col=b_col):
                    return e.tensor_scalar(dslice, dslice, g_col, b_col, ALU.mult, ALU.add)
                p.op("vector", ap3, inc=sAp if last else None)
    return sAp.w()



def build_stage1(fz=None):
    if fz is None:
        nc = bass.Bass("TRN2", target_bir_lowering=False)
        dr = {}
        dr["xT"] = nc.dram_tensor("xT", [D, NT], F32, kind="ExternalInput").ap()
        for w in ("wq", "wk", "wv", "wp"):
            dr[w] = nc.dram_tensor(w, [D, D], F32, kind="ExternalInput").ap()
        for b in ("bq", "bk", "bv", "bp"):
            dr[b] = nc.dram_tensor(b, [128, 8], F32, kind="ExternalInput").ap()
        dr["ident"] = nc.dram_tensor("ident", [128, 128], F32, kind="ExternalInput").ap()
        dr["ones1"] = nc.dram_tensor("ones1", [1, 128], F32, kind="ExternalInput").ap()
        dr["onesc"] = nc.dram_tensor("onesc", [128, 1], F32, kind="ExternalInput").ap()
        dr["mfirst"] = nc.dram_tensor("mfirst", [128, 256], F32, kind="ExternalInput").ap()
        dr["mint"] = nc.dram_tensor("mint", [128, 384], F32, kind="ExternalInput").ap()
        dr["mlast"] = nc.dram_tensor("mlast", [128, 256], F32, kind="ExternalInput").ap()
        dr["bvr"] = nc.dram_tensor("bvr", [1, D], F32, kind="ExternalInput").ap()
        dr["out"] = nc.dram_tensor("outT", [D, NT], F32, kind="ExternalOutput").ap()
    else:
        nc, dr = _NcProxy(fz["nc"], "s1_"), fz["dr"]

    with ExitStack() as ctx:
        e = ctx.enter_context
        if fz is None:
            stream = [e(nc.sbuf_tensor(f"strm{c}", [128, GUARD + NT + 6], F16))
                      for c in range(NCH)]
        else:
            stream = fz["stream"]
        yT = [e(nc.sbuf_tensor(f"yT{c}", [128, NT], F16)) for c in range(NCH)]
        wp16 = [e(nc.sbuf_tensor(f"wp16_{c}", [128, D], F16)) for c in range(NCH)]
        bias = {b: e(nc.sbuf_tensor(b + "_s", [128, 8], F32)) for b in ("bq", "bk", "bv", "bp")}
        ident = e(nc.sbuf_tensor("ident_s", [128, 128], F16))
        ones1 = e(nc.sbuf_tensor("ones1_s", [1, 128], F16))
        onesc = e(nc.sbuf_tensor("onesc_s", [128, 1], F16))
        masks = {0: e(nc.sbuf_tensor("mfirst_s", [128, 256], F16)),
                 1: e(nc.sbuf_tensor("mint_s", [128, 384], F16)),
                 2: e(nc.sbuf_tensor("mlast_s", [128, 256], F16))}
        wsl = [[e(nc.sbuf_tensor(f"wsl{k}_{i}", [128, 3 * 128], F16)) for i in range(2)]
               for k in range(NCH)]
        qT = e(nc.sbuf_tensor("qT", [128, NT], F16))
        kT = e(nc.sbuf_tensor("kT", [128, NT], F16))
        vtok = e(nc.sbuf_tensor("vtok", [128, NT], F16))
        bvrow = e(nc.sbuf_tensor("bvrow", [1, D], F16))
        Pbuf = [e(nc.sbuf_tensor(f"Pbuf{i}", [128, 384], F32)) for i in range(2)]
        Pn = [e(nc.sbuf_tensor(f"Pn{i}", [128, 384], F16)) for i in range(2)]
        Pt = [e(nc.sbuf_tensor(f"Pt{i}", [128, 384], F16)) for i in range(2)]
        nds = e(nc.sbuf_tensor("nds", [128, 6], F32))
        sqb = [e(nc.sbuf_tensor(f"sqb{i}", [128, TT], F16)) for i in range(2)]
        srow = e(nc.sbuf_tensor("srow", [1, TT], F32))
        qrow = e(nc.sbuf_tensor("qrow", [1, TT], F32))
        rstd = e(nc.sbuf_tensor("rstd", [1, TT], F32))
        rstd16 = e(nc.sbuf_tensor("rstd16", [1, TT], F16))
        murs16 = e(nc.sbuf_tensor("murs16", [1, TT], F16))

        psA = [e(nc.psum_tensor(f"psA{i}", [128, 512], F32)) for i in range(2)]
        psS = [e(nc.psum_tensor(f"psS{i}", [128, 512], F32)) for i in range(2)]
        psT = [e(nc.psum_tensor(f"psT{i}", [128, 512], F16)) for i in range(2)]
        psO = [e(nc.psum_tensor(f"psO{i}", [64, 128], F32)) for i in range(2)]
        psR = psA

        p = fz["p"] if fz else P(nc)
        sLd = p.sem("sLd")
        sSlab = p.sem("sSlab")
        sSlabF = p.sem("sSlabF")
        sSt = p.sem("sSt")
        proj = ProjPipe(p, psA, "proj")
        attn = AttnPipe(p, dict(psS=psS, psT=psT, psO=psO, Pbuf=Pbuf, Pn=Pn,
                                Pt=Pt, nds=nds), "at")

        for c in range(NCH):
            p.op("gpsimd", lambda e, c=c: e.dma_start(
                out=stream[c][:, GUARD:GUARD + NT],
                in_=dr["xT"][c * 128:(c + 1) * 128, :]), inc=sLd, inc_by=16)
        consts = [(ident, "ident"), (ones1, "ones1"), (onesc, "onesc"),
                  (masks[0], "mfirst"), (masks[1], "mint"), (masks[2], "mlast")]
        for sb, nm in consts:
            p.op("gpsimd", lambda e, sb=sb, nm=nm: e.dma_start(out=sb[:, :], in_=dr[nm]),
                 inc=sLd, inc_by=16)
        for c in range(NCH):
            p.op("gpsimd", lambda e, c=c: e.dma_start(
                out=wp16[c][:, :], in_=dr["wp"][c * 128:(c + 1) * 128, :]),
                inc=sLd, inc_by=16)
        for b in ("bq", "bk", "bv", "bp"):
            p.op("sync", lambda e, b=b: e.dma_start(out=bias[b][:, :], in_=dr[b]),
                 inc=sLd, inc_by=16)
        p.op("gpsimd", lambda e: e.dma_start(out=bvrow[:, :], in_=dr["bvr"]),
             inc=sLd, inc_by=16)
        all_loaded = sLd.w()

        for c in range(NCH):
            p.op("vector", lambda e, c=c: e.memset(stream[c][:, 0:GUARD], 0.0),
                 waits=[all_loaded] if c == 0 else ())

        slab_free = []
        for pr in range(8):
            bufi = pr % 2
            for ck in range(NCH):
                for wi, w in enumerate(("wq", "wk", "wv")):
                    waits = []
                    if pr >= 2 and wi == 0 and ck == 0:
                        waits.append(slab_free[pr - 2])
                    p.op("gpsimd", lambda e, w=w, pr=pr, ck=ck, wi=wi, bufi=bufi: e.dma_start(
                        out=wsl[ck][bufi][:, wi * 128:(wi + 1) * 128],
                        in_=dr[w][ck * 128:(ck + 1) * 128, pr * 128:(pr + 1) * 128]),
                        waits=waits, inc=sSlab, inc_by=16)
            slab_ready = sSlab.w()

            for which, (dst, wofs, bcol, scl) in enumerate((
                    (qT, 0, "bq", 1.0 / 32.0), (kT, 128, "bk", None))):
                for ti in range(8):
                    t0, t1 = GUARD + ti * TT, GUARD + (ti + 1) * TT
                    if scl is not None:
                        def mkev(e, ap, dst=dst, ti=ti, bcol=bcol, scl=scl, pr=pr):
                            return e.tensor_scalar(dst[:, ti * TT:(ti + 1) * TT], ap,
                                                   bias[bcol][:, pr:pr + 1], scl,
                                                   ALU.add, ALU.mult)
                    else:
                        def mkev(e, ap, dst=dst, ti=ti, bcol=bcol, pr=pr):
                            return e.tensor_scalar(dst[:, ti * TT:(ti + 1) * TT], ap,
                                                   bias[bcol][:, pr:pr + 1], None, ALU.add)
                    proj.group([wsl[k][bufi][:, wofs:wofs + 128] for k in range(NCH)],
                               [stream[k][:, t0:t1] for k in range(NCH)],
                               mkev,
                               extra_wait=[slab_ready, all_loaded] if (which == 0 and ti == 0) else ())
            for tc in range(32):
                def mkev(e, ap, tc=tc):
                    return e.tensor_copy(vtok[:, tc * 128:(tc + 1) * 128], ap)
                proj.group([stream[k][:, GUARD + tc * 128:GUARD + (tc + 1) * 128]
                            for k in range(NCH)],
                           [wsl[k][bufi][:, 256:384] for k in range(NCH)],
                           mkev, width=128,
                           extra_mm=[(ones1[:, :], bvrow[:, pr * 128:(pr + 1) * 128])])
            slab_free.append(proj.sMM.w())
            vtok_done = proj.drain()

            units = []
            for hl in range(2):
                for ib in range(32):
                    if ib == 0:
                        j0, jn, mk = 0, 256, 0
                    elif ib == 31:
                        j0, jn, mk = 30 * 128, 256, 2
                    else:
                        j0, jn, mk = (ib - 1) * 128, 384, 1
                    u = dict(
                        jn=jn, nj=jn // 128, ident=ident[:, :],
                        s_lhsT=qT[hl * 64:(hl + 1) * 64, ib * 128:(ib + 1) * 128],
                        s_rhs=kT[hl * 64:(hl + 1) * 64, j0:j0 + jn],
                        m_lhsT=ident[:, :], m_rhs=masks[mk][:, 0:jn],
                        pv_lhsT=(lambda c, j0=j0, hl=hl:
                                 vtok[:, (j0 // 128 + c) * 128 + hl * 64:
                                      (j0 // 128 + c) * 128 + hl * 64 + 64]),
                        out_dst=yT[pr][hl * 64:hl * 64 + 64, ib * 128:(ib + 1) * 128],
                    )
                    if ib <= 1 and hl == 0:
                        u["wait"] = [vtok_done]
                    units.append(u)
            attn.add(units)
            attn.flush()

        attn_done = attn.flush()

        for ti in range(8):
            for fo in range(NCH):
                def mkev(e, ap, fo=fo, ti=ti):
                    return e.scalar_tensor_tensor(
                        stream[fo][:, GUARD + ti * TT:GUARD + (ti + 1) * TT],
                        ap, bias["bp"][:, fo:fo + 1],
                        stream[fo][:, GUARD + ti * TT:GUARD + (ti + 1) * TT],
                        ALU.add, ALU.add)
                proj.group([wp16[k][:, fo * 128:(fo + 1) * 128] for k in range(NCH)],
                           [yT[k][:, ti * TT:(ti + 1) * TT] for k in range(NCH)],
                           mkev,
                           extra_wait=[attn_done] if (ti == 0 and fo == 0) else ())
        resid_done = proj.drain()

        ln_bufs = dict(sq=sqb, psR=psR, srow=srow, qrow=qrow,
                       rstd=rstd, rstd16=rstd16, murs16=murs16,
                       onesc=onesc[:, :], ones1=ones1[:, :])
        ln_done = emit_ln(p, stream, ln_bufs, "1", pre_wait=[resid_done])

        if fz is None:
            for c in range(NCH):
                p.op("gpsimd", lambda e, c=c: e.dma_start(
                    out=dr["out"][c * 128:(c + 1) * 128, :],
                    in_=stream[c][:, GUARD:GUARD + NT]),
                    waits=[ln_done] if c == 0 else (), inc=sSt, inc_by=16)
            p.wait("gpsimd", sSt.w())
            p.alloc_sems(ctx)
            block = e(nc.Block())
            p.emit(block)
        else:
            for c in range(NCH):
                p.op("gpsimd", lambda e, c=c: e.dma_start(
                    out=fz["xa_scr"][c * 128:(c + 1) * 128, :],
                    in_=stream[c][:, GUARD:GUARD + NT]),
                    waits=[ln_done] if c == 0 else (), inc=sSt, inc_by=16)
            fz["xa_done"] = sSt.w()
            fz["prev"] = ln_done
    return nc



def emit_ln_stats(p, stream, bufs, sems, shifted, pre_wait, rstd16f, murs16f,
                  eps=1e-5):
    sq, psR = bufs["sq"], bufs["psR"]
    srow, qrow, rstd = bufs["srow"], bufs["qrow"], bufs["rstd"]
    onesc = bufs["onesc"]
    nch = len(stream)
    sSq = p.sem("lsSq")
    sMM = p.sem("lsMM")
    sVs = p.sem("lsVs")
    sSc = p.sem("lsSc")
    sRdy = p.sem("lsRdy")
    bMM, bVs = sMM.n, sVs.n

    def sview(c, t0, t1):
        if shifted and c >= nch // 2:
            return stream[c][:, GUARD - 1 + t0:GUARD - 1 + t1]
        return stream[c][:, GUARD + t0:GUARD + t1]

    for ti in range(NT // TT):
        t0, t1 = ti * TT, (ti + 1) * TT
        for c in range(nch):
            g = ti * nch + c
            sqb = sq[g % 2]
            waits = list(pre_wait) if g == 0 else []
            if g >= 2:
                waits.append((sMM, bMM + 2 * (g - 1)))

            def mkq(e, c=c, t0=t0, t1=t1, sqb=sqb):
                return e.tensor_tensor(sqb[:, :], sview(c, t0, t1), sview(c, t0, t1),
                                       ALU.mult)
            p.op("vector", mkq, waits=waits, inc=sSq)
            sq_done = sSq.w()
            waits2 = list(pre_wait) if g == 0 else []
            if c == 0 and ti >= 1:
                waits2.append((sVs, bVs + 7 * ti))

            def mmS(e, c=c, t0=t0, t1=t1):
                return e.matmul(psR[0][0:1, :], onesc, sview(c, t0, t1),
                                start=(c == 0), stop=(c == nch - 1))
            p.op("tensor", mmS, waits=waits2, inc=sMM)

            def mmQ(e, c=c, sqb=sqb):
                return e.matmul(psR[1][0:1, :], onesc, sqb[:, :],
                                start=(c == 0), stop=(c == nch - 1))
            p.op("tensor", mmQ, waits=[sq_done], inc=sMM)
        mm_done = sMM.w()
        p.op("vector", lambda e: e.tensor_copy(srow[:, :], psR[0][0:1, :]),
             waits=[mm_done], inc=sVs)
        p.op("vector", lambda e: e.tensor_copy(qrow[:, :], psR[1][0:1, :]), inc=sVs)
        p.op("vector", lambda e: e.tensor_scalar_mul(srow[:, :], srow[:, :], 1.0 / D),
             waits=[sVs.w()], inc=sVs)
        p.op("vector", lambda e: e.tensor_tensor(rstd[:, :], srow[:, :], srow[:, :],
                                                 ALU.mult), waits=[sVs.w()], inc=sVs)
        p.op("vector", lambda e: e.scalar_tensor_tensor(rstd[:, :], qrow[:, :], 1.0 / D,
                                                        rstd[:, :], ALU.mult,
                                                        ALU.subtract),
             waits=[sVs.w()], inc=sVs)
        p.op("vector", lambda e: e.tensor_scalar_add(rstd[:, :], rstd[:, :], eps),
             waits=[sVs.w()], inc=sVs)
        p.op("scalar", lambda e: e.sqrt(rstd[:, :], rstd[:, :]), waits=[sVs.w()],
             inc=sSc)
        p.op("vector", lambda e: e.reciprocal(rstd[:, :], rstd[:, :]),
             waits=[sSc.w()], inc=sVs)
        p.op("vector", lambda e, t0=t0, t1=t1: e.tensor_copy(rstd16f[:, t0:t1],
                                                             rstd[:, :]),
             waits=[sVs.w()], inc=sRdy)
        p.op("vector", lambda e, t0=t0, t1=t1: e.tensor_tensor(murs16f[:, t0:t1],
                                                               srow[:, :], rstd[:, :],
                                                               ALU.mult),
             inc=sRdy)
    return sRdy.w()


def emit_ln_apply_tile(p, stream, sems_d, ti, rstd16f, murs16f, psB, ones1,
                       out, affine, shifted, shadow=None, pre_wait=()):
    sBc, sAp = sems_d["sBc"], sems_d["sAp"]
    t0, t1 = ti * TT, (ti + 1) * TT
    nch = len(stream)

    def mb1(e):
        return e.matmul(psB[0][:, :], ones1, rstd16f[:, t0:t1], start=True, stop=True)
    p.op("tensor", mb1, waits=list(pre_wait) + ([(sAp, sAp.n)] if sAp.n else []),
         inc=sBc)

    def mb2(e):
        return e.matmul(psB[1][:, :], ones1, murs16f[:, t0:t1], start=True, stop=True)
    p.op("tensor", mb2, inc=sBc)
    bc_done = sBc.w()
    for c in range(nch):
        dsl = out[c][:, 0:TT]
        if shifted and c >= nch // 2:
            sv_main = stream[c][:, GUARD - 1 + t0 + 1:GUARD - 1 + t1]
            if ti == 0:
                sv_first = stream[c][:, GUARD - 1:GUARD]
            else:
                sh, col = shadow
                sv_first = sh[:, (c - nch // 2) * 8 + ti - 1:(c - nch // 2) * 8 + ti]
        else:
            sv_main = None
            sv_full = stream[c][:, GUARD + t0:GUARD + t1]
        if sv_main is None:
            def ap1(e, c=c, sv=sv_full, dsl=dsl):
                return e.tensor_tensor(dsl, sv, psB[0][:, :], ALU.mult)
            p.op("vector", ap1, waits=[bc_done] if c == 0 else ())
        else:
            def ap1a(e, sv=sv_first, dsl=dsl):
                return e.tensor_tensor(dsl[:, 0:1], sv, psB[0][:, 0:1], ALU.mult)
            p.op("vector", ap1a, waits=[bc_done] if c == 0 else ())

            def ap1b(e, sv=sv_main, dsl=dsl):
                return e.tensor_tensor(dsl[:, 1:TT], sv, psB[0][:, 1:TT], ALU.mult)
            p.op("vector", ap1b)
        last = (c == nch - 1)

        def ap2(e, dsl=dsl):
            return e.tensor_tensor(dsl, dsl, psB[1][:, :], ALU.subtract)
        p.op("vector", ap2, inc=sAp if (last and affine is None) else None)
        if affine is not None:
            g_col, b_col = affine(c)

            def ap3(e, dsl=dsl, g_col=g_col, b_col=b_col):
                return e.tensor_scalar(dsl, dsl, g_col, b_col, ALU.mult, ALU.add)
            p.op("vector", ap3, inc=sAp if last else None)
    return sAp.w()




def _rotary(p, proj, sVr, src, tb1, tb2, prot, bones2, sel2, nr, nr16, scr,
            c0, pw, qrdy, prev_done):
    sq16, u1 = scr
    u2 = sq16
    sv = src[:, c0:c0 + pw]

    def mksq(e, sv=sv, pw=pw):
        return e.tensor_tensor(sq16[:, 0:pw], sv, sv, ALU.mult)
    p.op("vector", mksq, waits=[prev_done], inc=sVr)
    sq_done = sVr.w()

    def evn(e, ap, pw=pw):
        return e.tensor_copy(nr[:, 0:pw], ap)
    proj.group([bones2[:, :]], [sq16[:, 0:pw]], evn, width=pw, rows=2,
               extra_wait=[sq_done])
    p.op("scalar", lambda e, pw=pw: e.sqrt(nr[:, 0:pw], nr[:, 0:pw]),
         waits=[proj.drain()], inc=sVr)
    p.op("vector", lambda e, pw=pw: e.tensor_scalar_max(nr[:, 0:pw], nr[:, 0:pw],
                                                        1e-12),
         waits=[sVr.w()], inc=sVr)
    p.op("vector", lambda e, pw=pw: e.reciprocal(nr[:, 0:pw], nr[:, 0:pw]),
         waits=[sVr.w()], inc=sVr)
    p.op("vector", lambda e, pw=pw: e.tensor_copy(nr16[:, 0:pw], nr[:, 0:pw]),
         waits=[sVr.w()], inc=sVr)
    nrm_ready = sVr.w()

    def mku1(e, sv=sv, pw=pw, c0=c0):
        return e.tensor_tensor(u1[:, 0:pw], sv, tb1[:, c0:c0 + pw], ALU.mult)
    p.op("vector", mku1, inc=sVr)

    def evu2(e, ap, pw=pw, c0=c0):
        return e.tensor_tensor(u2[:, 0:pw], ap, tb2[:, c0:c0 + pw], ALU.mult)
    proj.group([prot[:, :]], [sv], evu2, width=pw, extra_wait=[qrdy])

    def qpre(e, pw=pw):
        return e.tensor_tensor(u1[:, 0:pw], u1[:, 0:pw], u2[:, 0:pw], ALU.add)
    p.op("vector", qpre, waits=[proj.drain()], inc=sVr)
    qpre_done = sVr.w()

    def evrot(e, ap, sv=sv, pw=pw):
        return e.tensor_tensor(sv, u1[:, 0:pw], ap, ALU.mult)
    proj.group([sel2[:, :]], [nr16[:, 0:pw]], evrot, width=pw,
               extra_wait=[nrm_ready], evac_wait=[qpre_done])
    return proj.drain()


def build_stage2(layers, final_ln, skip_ffn=False, dbg=False, fz=None):
    NL = 6
    if fz is not None:
        nc, dr = _NcProxy(fz["nc"], "s2_"), fz["dr"]
        return _stage2_body(nc, dr, layers, final_ln, skip_ffn, dbg, fz)
    nc = bass.Bass("TRN2", target_bir_lowering=False)
    dr = {}
    if dbg:
        for nm in ("dbg_q", "dbg_k", "dbg_v"):
            dr[nm] = nc.dram_tensor(nm, [128, 1280], F32, kind="ExternalOutput").ap()
        dr["dbg_a"] = nc.dram_tensor("dbg_a", [128, 8192], F32, kind="ExternalOutput").ap()
    dr["xT"] = nc.dram_tensor("xT", [D, NT], F32, kind="ExternalInput").ap()
    dr["wqkv"] = nc.dram_tensor("wqkv", [NL * D, 3072], F32, kind="ExternalInput").ap()
    dr["wo"] = nc.dram_tensor("wo", [NL * D, D], F32, kind="ExternalInput").ap()
    dr["w1"] = nc.dram_tensor("w1", [NL * D, 5632], F32, kind="ExternalInput").ap()
    dr["w2"] = nc.dram_tensor("w2", [NL * 2816, D], F32, kind="ExternalInput").ap()
    dr["b1c"] = nc.dram_tensor("b1c", [NL * 128, 44], F32, kind="ExternalInput").ap()
    dr["b2c"] = nc.dram_tensor("b2c", [NL * 128, 8], F32, kind="ExternalInput").ap()
    dr["lngc"] = nc.dram_tensor("lngc", [NL * 128, 8], F32, kind="ExternalInput").ap()
    dr["lnbc"] = nc.dram_tensor("lnbc", [NL * 128, 8], F32, kind="ExternalInput").ap()
    dr["qsc"] = nc.dram_tensor("qsc", [NL * 128, 1], F32, kind="ExternalInput").ap()
    dr["ksc"] = nc.dram_tensor("ksc", [NL * 128, 1], F32, kind="ExternalInput").ap()
    dr["qss"] = nc.dram_tensor("qss", [NL * 128, 1], F32, kind="ExternalInput").ap()
    dr["kss"] = nc.dram_tensor("kss", [NL * 128, 1], F32, kind="ExternalInput").ap()
    for nm in ("tq1", "tq2", "tk1", "tk2"):
        dr[nm] = nc.dram_tensor(nm, [128, NT], F32, kind="ExternalInput").ap()
    for nm, sh in (("prot", [128, 128]), ("ident", [128, 128]), ("ones1", [1, 128]),
                   ("onesc", [128, 1]), ("bones2", [128, 2]), ("sel2", [2, 128]),
                   ("m_a", [128, 512]), ("m_b", [128, 512]), ("m0a", [128, 256]),
                   ("m0b", [128, 256])):
        dr[nm] = nc.dram_tensor(nm, sh, F32, kind="ExternalInput").ap()
    dr["out"] = nc.dram_tensor("outT", [D, NT], F32, kind="ExternalOutput").ap()
    return _stage2_body(nc, dr, layers, final_ln, skip_ffn, dbg, None)


def _stage2_body(nc, dr, layers, final_ln, skip_ffn, dbg, fz):
    NL = 6
    with ExitStack() as ctx:
        e = ctx.enter_context
        if fz is None:
            stream = [e(nc.sbuf_tensor(f"strm{c}", [128, GUARD + NT + 6], F16))
                      for c in range(NCH)]
        else:
            stream = fz["stream"]
        tb = {nm: e(nc.sbuf_tensor(nm + "_s", [128, 1344], BF16))
              for nm in ("tq1", "tq2", "tk1", "tk2")}
        qb = e(nc.sbuf_tensor("qb", [128, 1280], BF16))
        kb = e(nc.sbuf_tensor("kb", [128, 1280], BF16))
        vtok = e(nc.sbuf_tensor("vtok", [128, 1280], F16))
        scr = [e(nc.sbuf_tensor(f"scr{i}", [128, 512], BF16)) for i in range(2)]
        nr = e(nc.sbuf_tensor("nr", [2, 512], F32))
        nr16 = e(nc.sbuf_tensor("nr16", [2, 512], BF16))
        ptail = [e(nc.sbuf_tensor(f"ptail{c}", [128, 264], F16)) for c in range(NCH)]
        Pbuf = [e(nc.sbuf_tensor(f"Pbuf{i}", [128, 512], F32)) for i in range(2)]
        Pn = [e(nc.sbuf_tensor(f"Pn{i}", [128, 512], F16)) for i in range(2)]
        Pt = [e(nc.sbuf_tensor(f"Pt{i}", [128, 512], F16)) for i in range(2)]
        nds = e(nc.sbuf_tensor("nds", [128, 6], F32))
        identH = e(nc.sbuf_tensor("identH", [128, 128], F16))
        prot = e(nc.sbuf_tensor("prot_s", [128, 128], BF16))
        ones1 = e(nc.sbuf_tensor("ones1_s", [1, 128], F16))
        onesc = e(nc.sbuf_tensor("onesc_s", [128, 1], F16))
        bones2 = e(nc.sbuf_tensor("bones2_s", [128, 2], BF16))
        sel2 = e(nc.sbuf_tensor("sel2_s", [2, 128], BF16))
        masks = {"a": e(nc.sbuf_tensor("m_a_s", [128, 512], F16)),
                 "b": e(nc.sbuf_tensor("m_b_s", [128, 512], F16)),
                 "0a": e(nc.sbuf_tensor("m0a_s", [128, 256], F16)),
                 "0b": e(nc.sbuf_tensor("m0b_s", [128, 256], F16))}
        wsl = [e(nc.sbuf_tensor("wsl0", [128, 8 * 384], F16))]
        wosl = [e(nc.sbuf_tensor("wosl0", [128, 8 * 128], F16))]
        w1sl = [e(nc.sbuf_tensor(f"w1sl{i}", [128, 8 * 256], F16)) for i in range(2)]
        w2sl = [e(nc.sbuf_tensor(f"w2sl{i}", [128, 22 * 128], F16)) for i in range(2)]
        b1buf = e(nc.sbuf_tensor("b1buf", [128, 44], F32))
        b2buf = e(nc.sbuf_tensor("b2buf", [128, 8], F32))
        lng = e(nc.sbuf_tensor("lng", [128, 8], F32))
        lnb = e(nc.sbuf_tensor("lnb", [128, 8], F32))
        qscol = e(nc.sbuf_tensor("qscol", [128, 1], F32))
        kscol = e(nc.sbuf_tensor("kscol", [128, 1], F32))
        qscolS = e(nc.sbuf_tensor("qscolS", [128, 1], F32))
        kscolS = e(nc.sbuf_tensor("kscolS", [128, 1], F32))
        xn = [e(nc.sbuf_tensor(f"xn{c}", [128, TT], F16)) for c in range(NCH)]
        hbuf = e(nc.sbuf_tensor("hbuf", [128, 22 * TT], F16))

        def aout(c, x, y):
            return hbuf[:, c * 1024 + x:c * 1024 + y]
        ascr = [e(nc.sbuf_tensor(f"ascr{i}", [128, TT], F16)) for i in range(2)]
        gscr = [e(nc.sbuf_tensor(f"gscr{i}", [128, TT], F16)) for i in range(2)]
        sqb = ascr
        srow = e(nc.sbuf_tensor("srow", [1, TT], F32))
        qrow = e(nc.sbuf_tensor("qrow", [1, TT], F32))
        rstd = e(nc.sbuf_tensor("rstd", [1, TT], F32))
        rstd16f = e(nc.sbuf_tensor("rstd16f", [1, NT], F16))
        murs16f = e(nc.sbuf_tensor("murs16f", [1, NT], F16))
        shdw = e(nc.sbuf_tensor("shdw", [128, 32], F16))
        xtile = gscr

        psA = [e(nc.psum_tensor(f"psA{i}", [128, 512], F32)) for i in range(2)]
        psS = [e(nc.psum_tensor(f"psS{i}", [128, 512], F32)) for i in range(2)]
        psT = [e(nc.psum_tensor(f"psT{i}", [128, 512], F16)) for i in range(2)]
        psO = [e(nc.psum_tensor(f"psO{i}", [64, 128], F32)) for i in range(2)]

        p = fz["p"] if fz else P(nc)
        sLd = p.sem("sLd")
        sTbl = p.sem("sTbl")
        sW = p.sem("sW")
        sWFq = p.sem("sWFq")
        sWFo = p.sem("sWFo")
        sWF1 = p.sem("sWF1")
        sWF2 = p.sem("sWF2")
        sVr = p.sem("sVr")
        sPt = p.sem("sPtl")
        sGlu = p.sem("sGlu")
        sShdw = p.sem("sShdw")
        sSt = p.sem("sSt")
        proj = ProjPipe(p, psA, "proj")
        attn = AttnPipe(p, dict(psS=psS, psT=psT, psO=psO, Pbuf=Pbuf, Pn=Pn,
                                Pt=Pt, nds=nds), "at")
        ln_sbc = p.sem("lnSbc")
        ln_sap = p.sem("lnSap")

        if fz is None:
            for c in range(NCH):
                p.op("gpsimd", lambda e, c=c: e.dma_start(
                    out=stream[c][:, GUARD:GUARD + NT],
                    in_=dr["xT"][c * 128:(c + 1) * 128, :]), inc=sLd, inc_by=16)
        consts = [(identH, "ident"), (prot, "prot"),
                  (ones1, "ones1"), (onesc, "onesc"), (bones2, "bones2"),
                  (sel2, "sel2"), (masks["a"], "m_a"), (masks["b"], "m_b"),
                  (masks["0a"], "m0a"), (masks["0b"], "m0b")]
        for i, (sb, nm) in enumerate(consts):
            p.op("gpsimd", lambda e, sb=sb, nm=nm: e.dma_start(out=sb[:, :], in_=dr[nm]),
                 waits=[fz["prev"]] if (fz and i == 0) else (), inc=sLd, inc_by=16)
        all_loaded = sLd.w()
        if fz is None:
            for c in range(NCH):
                p.op("vector", lambda e, c=c: e.memset(stream[c][:, 0:GUARD], 0.0),
                     waits=[all_loaded] if c == 0 else ())

        def uview(c, tok, width, use_ptail, t0):
            if use_ptail:
                base = tok - (t0 - 257)
                if c >= 4:
                    return ptail[c][:, base - 1:base - 1 + width]
                return ptail[c][:, base:base + width]
            if c >= 4:
                return stream[c][:, GUARD - 1 + tok:GUARD - 1 + tok + width]
            return stream[c][:, GUARD + tok:GUARD + tok + width]

        wslab_g = wo_g = w1_g = w2_g = 0
        qkv_free, wo_free, w1_free, w2_free = [], [], [], []
        w1_done = None

        for li in layers:
            lw = [proj.drain(), ln_sap.w(), (sTbl, sTbl.n), (sGlu, sGlu.n)]
            if fz is not None:
                lw.append(fz["prev"])
            for i, (buf, nm) in enumerate(((qscol, "qsc"), (kscol, "ksc"),
                                           (qscolS, "qss"), (kscolS, "kss"),
                                           (b1buf, "b1c"), (b2buf, "b2c"),
                                           (lng, "lngc"), (lnb, "lnbc"))):
                p.op("sync", lambda e, buf=buf, nm=nm, li=li: e.dma_start(
                    out=buf[:, :], in_=dr[nm][li * 128:(li + 1) * 128, :]),
                    waits=lw if i == 0 else (), inc=sLd, inc_by=16)
            scal_ready = sLd.w()

            for ts in range(4):
                t0, t1 = ts * 1024, (ts + 1) * 1024
                blo = t0 - 256
                ptail_ready = sPt.w()
                plo = max(t0 - 256, 0)
                co = plo - blo
                spn = t1 - plo
                tbl_war = [proj.drain()]
                for nm, col in (("tq1", qscol), ("tq2", qscolS),
                                ("tk1", kscol), ("tk2", kscolS)):
                    p.op("gpsimd", lambda e, nm=nm, plo=plo, co=co, spn=spn: e.dma_start(
                        out=tb[nm][:, co:co + spn],
                        in_=dr[nm][:, plo:plo + spn]),
                        waits=tbl_war, inc=sTbl, inc_by=16)
                    tbl_war = []
                tbl_dma = sTbl.w()
                for i, (nm, col) in enumerate((("tq1", qscol), ("tq2", qscolS),
                                               ("tk1", kscol), ("tk2", kscolS))):
                    p.op("vector", lambda e, nm=nm, col=col, co=co, spn=spn:
                         e.tensor_scalar_mul(tb[nm][:, co:co + spn],
                                             tb[nm][:, co:co + spn], col[:, 0:1]),
                         waits=[tbl_dma, scal_ready] if i == 0 else (), inc=sTbl)
                tbl_ready = sTbl.w()
                for pr in range(8):
                    bufi = 0
                    waits = [qkv_free[-1]] if qkv_free else []
                    p.op("gpsimd", lambda e, li=li, pr=pr, bufi=bufi: e.dma_start(
                        out=wsl[bufi][:, :].rearrange("p (c n) -> p c n", c=NCH),
                        in_=dr["wqkv"][li * D:(li + 1) * D, pr * 384:(pr + 1) * 384]
                        .rearrange("(c p) n -> p c n", p=128)),
                        waits=waits, inc=sW, inc_by=16)
                    wslab_g += 1
                    slab_ready = sW.w()

                    spans = []
                    if ts > 0:
                        spans.append((t0 - 256, 256, True))
                    tok = t0
                    while tok < t1:
                        spans.append((tok, 512, False))
                        tok += 512
                    first = True
                    for which, dst in ((0, qb), (1, kb)):
                        for (tok, wdt, upt) in spans:
                            c0 = tok - blo

                            def mkev(e, ap, dst=dst, c0=c0, wdt=wdt):
                                return e.tensor_copy(dst[:, c0:c0 + wdt], ap)
                            ew = [slab_ready, tbl_ready, ptail_ready] if first else ()
                            first = False
                            proj.group(
                                [wsl[bufi][:, k * 384 + which * 128:
                                           k * 384 + which * 128 + 128]
                                 for k in range(NCH)],
                                [uview(k, tok, wdt, upt, t0) for k in range(NCH)],
                                mkev, width=wdt, extra_wait=ew)
                    for (tok, wdt, upt) in spans:
                        for sub in range(wdt // 128):
                            tk = tok + sub * 128
                            c0 = tk - blo

                            def mkev(e, ap, c0=c0):
                                return e.tensor_copy(vtok[:, c0:c0 + 128], ap)
                            proj.group(
                                [uview(k, tk, 128, upt, t0) for k in range(NCH)],
                                [wsl[bufi][:, k * 384 + 256:k * 384 + 384]
                                 for k in range(NCH)],
                                mkev, width=128)
                    qkv_free.append(proj.sMM.w())
                    qrdy = proj.drain()

                    prev = qrdy
                    for sb_, t1_, t2_ in ((qb, tb["tq1"], tb["tq2"]),
                                          (kb, tb["tk1"], tb["tk2"])):
                        for (tok, wdt, upt) in spans:
                            c0 = tok - blo
                            prev = _rotary(p, proj, sVr, sb_, t1_, t2_, prot,
                                           bones2, sel2, nr, nr16, scr, c0, wdt,
                                           qrdy, prev)
                    rot_done = proj.drain()
                    if dbg and li == layers[0] and ts == 0 and pr == 0:
                        for nm, buf in (("dbg_q", qb), ("dbg_k", kb), ("dbg_v", vtok)):
                            p.op("gpsimd", lambda e, nm=nm, buf=buf: e.dma_start(
                                out=dr[nm], in_=buf[:, :]),
                                waits=[rot_done], inc=sSt, inc_by=16)

                    units = []
                    for hl in range(2):
                        for ibl in range(8):
                            ib = ts * 8 + ibl
                            w = ib // 2
                            if w == 0:
                                jn, j0g = 256, 0
                                mk = masks["0a"] if ib % 2 == 0 else masks["0b"]
                            else:
                                jn, j0g = 512, (w - 1) * 256
                                mk = masks["a"] if ib % 2 == 0 else masks["b"]
                            qc0 = ib * 128 - blo
                            jc0 = j0g - blo
                            u = dict(
                                jn=jn, nj=jn // 128, ident=identH[:, :],
                                s_lhsT=qb[hl * 64:(hl + 1) * 64, qc0:qc0 + 128],
                                s_rhs=kb[hl * 64:(hl + 1) * 64, jc0:jc0 + jn],
                                m_lhsT=identH[:, :], m_rhs=mk[:, 0:jn],
                                pv_lhsT=(lambda c, jc0=jc0, hl=hl:
                                         vtok[:, jc0 + c * 128 + hl * 64:
                                              jc0 + c * 128 + hl * 64 + 64]),
                                out_dst=aout(pr, ibl * 128, (ibl + 1) * 128)
                                [hl * 64:hl * 64 + 64, :],
                            )
                            if hl == 0 and ibl <= 1:
                                u["wait"] = [rot_done]
                            units.append(u)
                    attn.add(units)
                    attn.flush()

                a_done = attn.sEo.w()
                if dbg and li == layers[0] and ts == 0:
                    p.op("gpsimd", lambda e: e.dma_start(
                        out=dr["dbg_a"], in_=hbuf[:, 0:8192]),
                        waits=[a_done], inc=sSt, inc_by=16)
                if ts < 3:
                    for c in range(NCH):
                        p.op("vector", lambda e, c=c, t1=t1: e.tensor_copy(
                            ptail[c][:, 0:257],
                            stream[c][:, GUARD + t1 - 257:GUARD + t1]),
                            waits=[proj.drain(), a_done] if c == 0 else (),
                            inc=sPt if c == NCH - 1 else None)

                for fo in range(NCH):
                    bufi = 0
                    waits = [wo_free[-1]] if wo_free else []
                    p.op("gpsimd", lambda e, li=li, fo=fo, bufi=bufi: e.dma_start(
                        out=wosl[bufi][:, :].rearrange("p (c n) -> p c n", c=NCH),
                        in_=dr["wo"][li * D:(li + 1) * D, fo * 128:(fo + 1) * 128]
                        .rearrange("(c p) n -> p c n", p=128)),
                        waits=waits, inc=sW, inc_by=16)
                    wo_g += 1
                    wo_ready = sW.w()
                    for tsub in range(2):
                        tt0 = t0 + tsub * 512

                        def mkev(e, ap, fo=fo, tt0=tt0):
                            return e.tensor_tensor(
                                stream[fo][:, GUARD + tt0:GUARD + tt0 + 512], ap,
                                stream[fo][:, GUARD + tt0:GUARD + tt0 + 512],
                                ALU.add)
                        proj.group(
                            [wosl[bufi][:, k * 128:(k + 1) * 128] for k in range(NCH)],
                            [aout(k, tsub * 512, (tsub + 1) * 512)
                             for k in range(NCH)],
                            mkev,
                            extra_wait=[wo_ready, a_done])
                    wo_free.append(proj.sMM.w())
            resid_done = proj.drain()
            if skip_ffn:
                continue

            ln_bufs = dict(sq=sqb, psR=psS, srow=srow, qrow=qrow, rstd=rstd,
                           onesc=onesc[:, :])
            stats_done = emit_ln_stats(p, stream, ln_bufs, "", shifted=True,
                                 pre_wait=[resid_done, attn.sEx.w(),
                                           attn.sEo.w(), attn.sTe.w(),
                                           attn.sPV.w(), attn.sPn.w(),
                                           attn.sRm.w(), attn.sRc.w()],
                                 rstd16f=rstd16f, murs16f=murs16f)
            for c in range(4, NCH):
                src = stream[c][:, GUARD:GUARD + NT].rearrange(
                    "p (t s) -> p t s", s=TT)[:, 0:7, TT - 1:TT]
                dst = shdw[:, (c - 4) * 8:(c - 4) * 8 + 7].rearrange(
                    "p (t s) -> p t s", s=1)
                p.op("vector", lambda e, src=src, dst=dst: e.tensor_copy(dst, src),
                     waits=[resid_done] if c == 4 else (),
                     inc=sShdw if c == NCH - 1 else None)
            shdw_done = sShdw.w()

            for tf in range(8):
                tf0 = tf * TT
                af = (lambda c: (lng[:, c:c + 1], lnb[:, c:c + 1]))
                pw = [stats_done, shdw_done]
                if w1_done is not None:
                    pw.append(w1_done)
                ap_done = emit_ln_apply_tile(
                    p, stream, dict(sBc=ln_sbc, sAp=ln_sap), tf, rstd16f, murs16f,
                    psS, ones1[:, :], xn, af, shifted=True, shadow=(shdw, 0),
                    pre_wait=pw)
                for cc in range(22):
                    bufi = w1_g % 2
                    waits = [w1_free[w1_g - 2]] if w1_g >= 2 else []
                    p.op("gpsimd", lambda e, li=li, cc=cc, bufi=bufi: e.dma_start(
                        out=w1sl[bufi][:, :].rearrange("p (c n) -> p c n", c=NCH),
                        in_=dr["w1"][li * D:(li + 1) * D, cc * 256:(cc + 1) * 256]
                        .rearrange("(c p) n -> p c n", p=128)),
                        waits=waits, inc=sW, inc_by=16)
                    w1_g += 1
                    w1_ready = sW.w()
                    sl = cc % 2
                    gwait = [(sGlu, sGlu.n - 1)] if sGlu.n >= 2 else []

                    def eva(e, ap, sl=sl, cc=cc):
                        return e.tensor_scalar(ascr[sl][:, :], ap, b1buf[:, cc:cc + 1],
                                               None, ALU.add)
                    proj.group([w1sl[bufi][:, k * 256:k * 256 + 128]
                                for k in range(NCH)],
                               [xn[k][:, :] for k in range(NCH)],
                               eva,
                               extra_wait=[w1_ready, ap_done],
                               evac_wait=gwait)

                    def evg(e, ap, sl=sl, cc=cc):
                        return e.activation(gscr[sl][:, :], ap, AF.Gelu,
                                            bias=b1buf[:, 22 + cc:23 + cc], scale=1.0)
                    proj.group([w1sl[bufi][:, k * 256 + 128:k * 256 + 256]
                                for k in range(NCH)],
                               [xn[k][:, :] for k in range(NCH)],
                               evg, evac_eng="scalar",
                               extra_wait=[ap_done], evac_wait=gwait)
                    w1_free.append(proj.sMM.w())
                    ab_done = proj.drain()

                    def glu(e, sl=sl, cc=cc):
                        return e.tensor_tensor(hbuf[:, cc * TT:(cc + 1) * TT],
                                               ascr[sl][:, :], gscr[sl][:, :],
                                               ALU.mult)
                    p.op("vector", glu, waits=[ab_done], inc=sGlu)
                w1_done = (proj.sMM, proj.sMM.n)
                glu_done = sGlu.w()

                for fo in range(NCH):
                    bufi = w2_g % 2
                    waits = [w2_free[w2_g - 2]] if w2_g >= 2 else []
                    p.op("gpsimd", lambda e, li=li, fo=fo, bufi=bufi: e.dma_start(
                        out=w2sl[bufi][:, :].rearrange("p (c n) -> p c n", c=22),
                        in_=dr["w2"][li * 2816:(li + 1) * 2816,
                                     fo * 128:(fo + 1) * 128]
                        .rearrange("(c p) n -> p c n", p=128)),
                        waits=waits, inc=sW, inc_by=16)
                    w2_g += 1
                    w2_ready = sW.w()

                    def mkev(e, ap, fo=fo, tf0=tf0):
                        return e.scalar_tensor_tensor(
                            stream[fo][:, GUARD + tf0:GUARD + tf0 + TT],
                            ap, b2buf[:, fo:fo + 1],
                            stream[fo][:, GUARD + tf0:GUARD + tf0 + TT],
                            ALU.add, ALU.add)
                    proj.group([w2sl[bufi][:, k * 128:(k + 1) * 128]
                                for k in range(22)],
                               [hbuf[:, k * TT:(k + 1) * TT] for k in range(22)],
                               mkev,
                               extra_wait=[w2_ready, glu_done])
                    w2_free.append(proj.sMM.w())
        stream_done = proj.drain()

        if final_ln:
            sX = p.sem("sX")
            sXa = p.sem("sXa")
            for c in range(NCH):
                for ti in range(8):
                    g = c * 8 + ti
                    waits = [stream_done] if g == 0 else []
                    if g >= 2:
                        waits.append(sXa.w(g - 1))
                    if fz is not None and g == 0:
                        waits.append(fz["xa_done"])
                    p.op("gpsimd", lambda e, c=c, ti=ti, g=g: e.dma_start(
                        out=xtile[g % 2][:, :],
                        in_=dr["xa" if fz else "xT"][c * 128:(c + 1) * 128,
                                                    ti * TT:(ti + 1) * TT]),
                        waits=waits, inc=sX, inc_by=16)

                    def addx(e, c=c, ti=ti, g=g):
                        return e.tensor_tensor(
                            stream[c][:, GUARD + ti * TT:GUARD + (ti + 1) * TT],
                            stream[c][:, GUARD + ti * TT:GUARD + (ti + 1) * TT],
                            xtile[g % 2][:, :], ALU.add)
                    p.op("vector", addx, waits=[sX.w()], inc=sXa)
            add_done = sXa.w()
            ln_bufs = dict(sq=sqb, psR=psS, srow=srow, qrow=qrow, rstd=rstd,
                           rstd16=rstd16f[:, 0:TT], murs16=murs16f[:, 0:TT],
                           onesc=onesc[:, :], ones1=ones1[:, :])
            ln_done = emit_ln(p, stream, ln_bufs, "fin", pre_wait=[add_done])
        else:
            ln_done = stream_done

        if fz is None:
            for c in range(NCH):
                p.op("gpsimd", lambda e, c=c: e.dma_start(
                    out=dr["out"][c * 128:(c + 1) * 128, :],
                    in_=stream[c][:, GUARD:GUARD + NT]),
                    waits=[ln_done] if c == 0 else (), inc=sSt, inc_by=16)
            p.wait("gpsimd", sSt.w())
            p.alloc_sems(ctx)
            block = e(nc.Block())
            p.emit(block)
        else:
            fz["prev"] = ln_done
    return nc


def _local_masks():
    ii = np.arange(128)[:, None]
    jj = np.arange(512)[None, :]
    m_a = np.where((jj >= ii) & (jj <= ii + 256), 0.0, MASKVAL)
    m_b = np.where((jj >= ii + 128) & (jj <= ii + 384), 0.0, MASKVAL)
    jj2 = np.arange(256)[None, :]
    m0a = np.where(jj2 <= ii, 0.0, MASKVAL)
    m0b = np.where(jj2 <= ii + 128, 0.0, MASKVAL)
    return (m_a.astype(np.float32), m_b.astype(np.float32),
            m0a.astype(np.float32), m0b.astype(np.float32))


def _rot_prot():
    M = np.zeros((128, 128), np.float32)
    for h in range(2):
        o = h * 64
        for dd in range(32):
            M[o + dd, o + 32 + dd] = -1.0
            M[o + 32 + dd, o + dd] = 1.0
    return np.ascontiguousarray(M.T)


def _rot_tables():
    DIM_HEAD, WINDOW = 64, 256
    half = np.arange(0, DIM_HEAD, 2, dtype=np.float64)
    inv_freq = 1.0 / 10000.0 ** (half / DIM_HEAD)
    t = np.arange(NT, dtype=np.float64)
    freqs = t[:, None] * inv_freq[None, :]
    freqs = np.concatenate([freqs, freqs], -1)
    base_scale = (half + 0.4 * DIM_HEAD) / (1.4 * DIM_HEAD)
    power = (t - NT // 2) / (WINDOW // 2)
    scale = base_scale[None, :] ** power[:, None]
    scale = np.concatenate([scale, scale], -1)
    r8 = np.sqrt(8.0)

    def dup(x):
        xt = (x.T * r8).astype(np.float32)
        return np.concatenate([xt, xt], axis=0)
    return (dup(np.cos(freqs) * scale), dup(np.sin(freqs) * scale),
            dup(np.cos(freqs) / scale), dup(np.sin(freqs) / scale))


def stage2_inputs(inputs, b, xA=None):
    f = np.asarray
    NL = 6
    wqkv = f(inputs["L0_ffn_Wqkv"])
    wqkv_r2 = np.zeros((NL * D, 3072), np.float32)
    for li in range(NL):
        for pr in range(8):
            wqkv_r2[li * D:(li + 1) * D, pr * 384:pr * 384 + 128] = wqkv[li][:, pr * 128:(pr + 1) * 128]
            wqkv_r2[li * D:(li + 1) * D, pr * 384 + 128:pr * 384 + 256] = wqkv[li][:, 1024 + pr * 128:1024 + (pr + 1) * 128]
            wqkv_r2[li * D:(li + 1) * D, pr * 384 + 256:pr * 384 + 384] = wqkv[li][:, 2048 + pr * 128:2048 + (pr + 1) * 128]

    w1 = f(inputs["L0_ffn_W1"])
    w2 = f(inputs["L0_ffn_W2"])
    w1p = np.zeros((NL, D, 5632), np.float32)
    w2p = np.zeros((NL, 2816, D), np.float32)
    b1 = f(inputs["L0_ffn_b1"])
    b1p = np.zeros((NL, 128, 44), np.float32)
    for li in range(NL):
        a = w1[li][:, 0:2730]
        g = w1[li][:, 2730:5460]
        ap = np.zeros((D, 2816), np.float32)
        gp = np.zeros((D, 2816), np.float32)
        ap[:, 0:2730] = a
        gp[:, 0:2730] = g
        for cc in range(22):
            w1p[li][:, cc * 256:cc * 256 + 128] = ap[:, cc * 128:(cc + 1) * 128]
            w1p[li][:, cc * 256 + 128:cc * 256 + 256] = gp[:, cc * 128:(cc + 1) * 128]
            b1a = np.zeros(2816, np.float32)
            b1g = np.zeros(2816, np.float32)
            b1a[0:2730] = b1[li][0:2730]
            b1g[0:2730] = b1[li][2730:5460]
            b1p[li][:, cc] = b1a[cc * 128:(cc + 1) * 128]
            b1p[li][:, 22 + cc] = b1g[cc * 128:(cc + 1) * 128]
        w2p[li][0:2730, :] = w2[li]
    tq1, tq2, tk1, tk2 = _rot_tables()
    m_a, m_b, m0a, m0b = _local_masks()
    qs = f(inputs["L0_ffn_q_scale"])
    ks = f(inputs["L0_ffn_k_scale"])
    qsc = np.concatenate([qs, qs], axis=1).reshape(NL * 128, 1)
    ksc = np.concatenate([ks, ks], axis=1).reshape(NL * 128, 1)
    qs_sw = np.concatenate([qs[:, 32:64], qs[:, 0:32]], axis=1)
    ks_sw = np.concatenate([ks[:, 32:64], ks[:, 0:32]], axis=1)
    qss = np.concatenate([qs_sw, qs_sw], axis=1).reshape(NL * 128, 1)
    kss = np.concatenate([ks_sw, ks_sw], axis=1).reshape(NL * 128, 1)
    d = {
        "xT": (np.ascontiguousarray(f(inputs["L0_attn_x"][b]).T) if xA is None
               else np.ascontiguousarray(xA[b].T)),
        "wqkv": wqkv_r2,
        "wo": f(inputs["L0_ffn_Wo"]).reshape(NL * D, D),
        "w1": w1p.reshape(NL * D, 5632),
        "w2": w2p.reshape(NL * 2816, D),
        "b1c": b1p.reshape(NL * 128, 44),
        "b2c": f(inputs["L0_ffn_b2"]).reshape(NL, 8, 128).transpose(0, 2, 1).reshape(NL * 128, 8),
        "lngc": f(inputs["L0_ffn_ln_g"]).reshape(NL, 8, 128).transpose(0, 2, 1).reshape(NL * 128, 8),
        "lnbc": f(inputs["L0_ffn_ln_b"]).reshape(NL, 8, 128).transpose(0, 2, 1).reshape(NL * 128, 8),
        "qsc": qsc, "ksc": ksc, "qss": qss, "kss": kss,
        "tq1": tq1, "tq2": tq2, "tk1": tk1, "tk2": tk2,
        "prot": _rot_prot(),
        "ident": np.eye(128, dtype=np.float32),
        "ones1": np.ones((1, 128), np.float32),
        "onesc": np.ones((128, 1), np.float32),
        "bones2": np.concatenate([np.tile([[1.0, 0.0]], (64, 1)),
                                  np.tile([[0.0, 1.0]], (64, 1))]).astype(np.float32),
        "sel2": np.concatenate([np.tile([[1.0]], (1, 64)) if False else
                                np.hstack([np.ones((1, 64)), np.zeros((1, 64))]),
                                np.hstack([np.zeros((1, 64)), np.ones((1, 64))])]
                               ).astype(np.float32),
        "m_a": m_a, "m_b": m_b, "m0a": m0a, "m0b": m0b,
    }
    return {k: np.ascontiguousarray(np.asarray(v, dtype=np.float32))
            for k, v in d.items()}



def build_stage3(fz=None):
    if fz is None:
        nc = bass.Bass("TRN2", target_bir_lowering=False)
        dr = {}
        dr["xT"] = nc.dram_tensor("xT", [D, NT], F32, kind="ExternalInput").ap()
        for nm in ("qr", "kr", "vr", "wz"):
            dr[nm] = nc.dram_tensor(nm, [D, D], F32, kind="ExternalInput").ap()
        dr["ident"] = nc.dram_tensor("ident", [128, 128], F32, kind="ExternalInput").ap()
        dr["ones1"] = nc.dram_tensor("ones1", [1, 128], F32, kind="ExternalInput").ap()
        dr["onesc"] = nc.dram_tensor("onesc", [128, 1], F32, kind="ExternalInput").ap()
        for k in range(4):
            dr[f"md{k}"] = nc.dram_tensor(f"md{k}", [128, 512], F32, kind="ExternalInput").ap()
        dr["out"] = nc.dram_tensor("outT", [D, NT], F32, kind="ExternalOutput").ap()
    else:
        nc, dr = _NcProxy(fz["nc"], "s3_"), fz["dr"]

    with ExitStack() as ctx:
        e = ctx.enter_context
        if fz is None:
            stream = [e(nc.sbuf_tensor(f"strm{c}", [128, GUARD + NT + 6], F16))
                      for c in range(NCH)]
        else:
            stream = fz["stream"]
        yT = [e(nc.sbuf_tensor(f"yT{c}", [128, NT], F16)) for c in range(NCH)]
        qT = e(nc.sbuf_tensor("qT", [128, NT], F16))
        kT = e(nc.sbuf_tensor("kT", [128, NT], F16))
        vtok = e(nc.sbuf_tensor("vtok", [128, NT], F16))
        wz16 = [e(nc.sbuf_tensor(f"wz16_{c}", [128, D], F16)) for c in range(NCH)]
        wsl = [[e(nc.sbuf_tensor(f"wsl{k}_{i}", [128, 3 * 128], F16)) for i in range(2)]
               for k in range(NCH)]
        Pt = [e(nc.sbuf_tensor(f"Pt{i}", [128, TT], F16)) for i in range(2)]
        mdiag = [e(nc.sbuf_tensor(f"md{k}_s", [128, 512], F16)) for k in range(4)]
        ident = e(nc.sbuf_tensor("ident_s", [128, 128], F16))
        ones1 = e(nc.sbuf_tensor("ones1_s", [1, 128], F16))
        onesc = e(nc.sbuf_tensor("onesc_s", [128, 1], F16))
        zcol = e(nc.sbuf_tensor("zcol", [128, 1], F32))
        ysc = e(nc.sbuf_tensor("ysc", [64, TT], F32))
        dnr = e(nc.sbuf_tensor("dnr", [1, TT], F32))
        dnr16 = e(nc.sbuf_tensor("dnr16", [1, TT], F16))
        sqb = [e(nc.sbuf_tensor(f"sqb{i}", [128, TT], F16)) for i in range(2)]
        srow = e(nc.sbuf_tensor("srow", [1, TT], F32))
        qrow = e(nc.sbuf_tensor("qrow", [1, TT], F32))
        rstd = e(nc.sbuf_tensor("rstd", [1, TT], F32))
        rstd16 = e(nc.sbuf_tensor("rstd16", [1, TT], F16))
        murs16 = e(nc.sbuf_tensor("murs16", [1, TT], F16))

        psA = [e(nc.psum_tensor(f"psA{i}", [128, 512], F32)) for i in range(2)]
        psS = [e(nc.psum_tensor(f"psS{i}", [128, 512], F32)) for i in range(2)]
        psD = e(nc.psum_tensor("psD", [1, 512], F32))
        psY = e(nc.psum_tensor("psY", [64, 512], F32))
        psB = e(nc.psum_tensor("psB", [64, 512], F32))
        psR = psS

        p = fz["p"] if fz else P(nc)
        sLd = p.sem("sLd")
        sSlab = p.sem("sSlab")
        sEx = p.sem("sEx")
        sMk = p.sem("sMk")
        sFl = p.sem("sFl")
        sSc2 = p.sem("sSc2")
        sDen = p.sem("sDen")
        sSt = p.sem("sSt")
        proj = ProjPipe(p, psA, "proj")

        if fz is None:
            for c in range(NCH):
                p.op("gpsimd", lambda e, c=c: e.dma_start(
                    out=stream[c][:, GUARD:GUARD + NT],
                    in_=dr["xT"][c * 128:(c + 1) * 128, :]), inc=sLd, inc_by=16)
        consts = [(ident, "ident"), (ones1, "ones1"), (onesc, "onesc")] + \
                 [(mdiag[k], f"md{k}") for k in range(4)]
        for i, (sb, nm) in enumerate(consts):
            p.op("gpsimd", lambda e, sb=sb, nm=nm: e.dma_start(out=sb[:, :], in_=dr[nm]),
                 waits=[fz["prev"]] if (fz and i == 0) else (), inc=sLd, inc_by=16)
        for c in range(NCH):
            p.op("gpsimd", lambda e, c=c: e.dma_start(
                out=wz16[c][:, :], in_=dr["wz"][c * 128:(c + 1) * 128, :]),
                inc=sLd, inc_by=16)
        all_loaded = sLd.w()
        p.op("vector", lambda e: e.memset(zcol[:, :], 0.0), waits=[all_loaded])

        slab_free = []
        ptfree_g = {0: 0, 1: 0}
        for pr in range(8):
            bufi = pr % 2
            for ck in range(NCH):
                for wi, w in enumerate(("qr", "kr", "vr")):
                    waits = []
                    if pr >= 2 and wi == 0 and ck == 0:
                        waits.append(slab_free[pr - 2])
                    p.op("gpsimd", lambda e, w=w, pr=pr, ck=ck, wi=wi, bufi=bufi: e.dma_start(
                        out=wsl[ck][bufi][:, wi * 128:(wi + 1) * 128],
                        in_=dr[w][ck * 128:(ck + 1) * 128, pr * 128:(pr + 1) * 128]),
                        waits=waits, inc=sSlab, inc_by=16)
            slab_ready = sSlab.w()

            for which, dst in ((0, qT), (1, kT)):
                for ti in range(8):
                    t0, t1 = GUARD + ti * TT, GUARD + (ti + 1) * TT

                    def mkev(e, ap, dst=dst, ti=ti):
                        return e.tensor_copy(dst[:, ti * TT:(ti + 1) * TT], ap)
                    proj.group([wsl[k][bufi][:, which * 128:which * 128 + 128]
                                for k in range(NCH)],
                               [stream[k][:, t0:t1] for k in range(NCH)],
                               mkev,
                               extra_wait=[slab_ready, all_loaded]
                               if (which == 0 and ti == 0) else ())
            for tc in range(32):
                def mkev(e, ap, tc=tc):
                    return e.tensor_copy(vtok[:, tc * 128:(tc + 1) * 128], ap)
                proj.group([stream[k][:, GUARD + tc * 128:GUARD + (tc + 1) * 128]
                            for k in range(NCH)],
                           [wsl[k][bufi][:, 256:384] for k in range(NCH)],
                           mkev, width=128)
            slab_free.append(proj.sMM.w())
            pv_ready = proj.drain()

            ptfree = ptfree_g
            for hl in range(2):
                qrows = qT[hl * 64:(hl + 1) * 64, :]
                krows = kT[hl * 64:(hl + 1) * 64, :]
                for ti in range(8):
                    i0 = ti * TT
                    njc = 4 * ti + 4
                    for jc in range(njc):
                        g = jc % 2
                        si = psS[g]
                        diag = jc >= 4 * ti
                        waits = [pv_ready] if (hl == 0 and ti == 0 and jc <= 1) else []
                        if jc == 0:
                            waits += [(sFl, sFl.n), (sSc2, sSc2.n)]
                        if sEx.n >= 2:
                            waits.append((sEx, sEx.n - 1))

                        def mmS(e, si=si, jc=jc, qrows=qrows, krows=krows, i0=i0):
                            return e.matmul(si[:, :], krows[:, jc * 128:(jc + 1) * 128],
                                            qrows[:, i0:i0 + TT], start=True, stop=True)
                        p.op("tensor", mmS, waits=waits, inc=sDen)
                        s_done = sDen.w()

                        def ex(e, si=si, g=g):
                            return e.activation(Pt[g][:, :], si[:, :], AF.Exp,
                                                bias=zcol[:, 0:1], scale=1.0 / 1024.0)
                        ex_waits = [s_done]
                        if ptfree[g]:
                            ex_waits.append((sDen, ptfree[g]))
                        p.op("scalar", ex, waits=ex_waits, inc=sEx)
                        exd = sEx.w()
                        if diag:
                            k4 = jc - 4 * ti

                            def msk(e, g=g, k4=k4):
                                return e.tensor_tensor(Pt[g][:, :], Pt[g][:, :],
                                                       mdiag[k4][:, :], ALU.mult)
                            p.op("vector", msk, waits=[exd], inc=sMk)
                            pt_ready = (sMk, sMk.n)
                        else:
                            pt_ready = (sEx, sEx.n)

                        def mmD(e, g=g, first=(jc == 0), last=(jc == njc - 1)):
                            return e.matmul(psD[:, :], onesc[:, :], Pt[g][:, :],
                                            start=first, stop=last)
                        p.op("tensor", mmD, waits=[pt_ready], inc=sDen)

                        def mmY(e, g=g, jc=jc, hl=hl, first=(jc == 0),
                                last=(jc == njc - 1)):
                            return e.matmul(psY[:, :],
                                            vtok[:, jc * 128 + hl * 64:
                                                 jc * 128 + hl * 64 + 64],
                                            Pt[g][:, :], start=first, stop=last)
                        p.op("tensor", mmY, inc=sDen)
                        ptfree[g] = sDen.n
                    acc_done = sDen.w()

                    def evd(e):
                        return e.tensor_copy(dnr[:, :], psD[0:1, :])
                    p.op("vector", evd, waits=[acc_done], inc=sFl)
                    p.op("vector", lambda e: e.reciprocal(dnr[:, :], dnr[:, :]),
                         waits=[sFl.w()], inc=sFl)
                    p.op("vector", lambda e: e.tensor_copy(dnr16[:, :], dnr[:, :]),
                         waits=[sFl.w()], inc=sFl)
                    dn_done = sFl.w()

                    def evy(e):
                        return e.copy(ysc[:, :], psY[0:64, :])
                    p.op("scalar", evy, waits=[acc_done, (sFl, sFl.n)], inc=sSc2)

                    def mmB(e):
                        return e.matmul(psB[0:64, :], ones1[:, 0:64], dnr16[:, :],
                                        start=True, stop=True)
                    p.op("tensor", mmB, waits=[dn_done], inc=sDen)
                    bc_done = sDen.w()

                    def evn(e, pr=pr, hl=hl, i0=i0):
                        return e.tensor_tensor(
                            yT[pr][hl * 64:hl * 64 + 64, i0:i0 + TT],
                            ysc[:, :], psB[0:64, :], ALU.mult)
                    p.op("vector", evn, waits=[bc_done, (sSc2, sSc2.n)], inc=sFl)

        attn_done = sFl.w()

        for ti in range(8):
            for fo in range(NCH):
                def mkev(e, ap, fo=fo, ti=ti):
                    return e.scalar_tensor_tensor(
                        stream[fo][:, GUARD + ti * TT:GUARD + (ti + 1) * TT],
                        ap, 1.0 / 16.0,
                        stream[fo][:, GUARD + ti * TT:GUARD + (ti + 1) * TT],
                        ALU.mult, ALU.add)
                proj.group([wz16[k][:, fo * 128:(fo + 1) * 128] for k in range(NCH)],
                           [yT[k][:, ti * TT:(ti + 1) * TT] for k in range(NCH)],
                           mkev,
                           extra_wait=[attn_done] if (ti == 0 and fo == 0) else ())
        resid_done = proj.drain()

        ln_bufs = dict(sq=sqb, psR=psR, srow=srow, qrow=qrow, rstd=rstd,
                       rstd16=rstd16, murs16=murs16,
                       onesc=onesc[:, :], ones1=ones1[:, :])
        ln_done = emit_ln(p, stream, ln_bufs, "3", pre_wait=[resid_done])

        if fz is None:
            for c in range(NCH):
                p.op("gpsimd", lambda e, c=c: e.dma_start(
                    out=dr["out"][c * 128:(c + 1) * 128, :],
                    in_=stream[c][:, GUARD:GUARD + NT]),
                    waits=[ln_done] if c == 0 else (), inc=sSt, inc_by=16)
            p.wait("gpsimd", sSt.w())
            p.alloc_sems(ctx)
            block = e(nc.Block())
            p.emit(block)
        else:
            fz["prev"] = ln_done
    return nc


def stage3_inputs(inputs, b, xB=None):
    f = np.asarray
    Q = f(inputs["L1_attn_Q"])
    Kw = f(inputs["L1_attn_K"])
    V = f(inputs["L1_attn_V"])
    W = f(inputs["L1_attn_W"])
    ii = np.arange(128)[:, None]
    jj = ii
    md = []
    for k4 in range(4):
        m = (np.arange(512)[None, :] >= (np.arange(128)[:, None] + 128 * k4))
        md.append(m.astype(np.float32))
    d = {
        "xT": np.ascontiguousarray(xB[b].T),
        "qr": np.ascontiguousarray(Q.transpose(1, 2, 0).reshape(D, D)),
        "kr": np.ascontiguousarray(Kw.transpose(1, 2, 0).reshape(D, D)),
        "vr": np.ascontiguousarray(V.transpose(1, 2, 0).reshape(D, D)),
        "wz": np.ascontiguousarray(W.transpose(2, 1, 0).reshape(D, D)),
        "ident": np.eye(128, dtype=np.float32),
        "ones1": np.ones((1, 128), np.float32),
        "onesc": np.ones((128, 1), np.float32),
        "md0": md[0], "md1": md[1], "md2": md[2], "md3": md[3],
    }
    return {k: np.ascontiguousarray(np.asarray(v, dtype=np.float32))
            for k, v in d.items()}



def build_stage4(fz=None):
    if fz is None:
        nc = bass.Bass("TRN2", target_bir_lowering=False)
        dr = {}
        dr["xT"] = nc.dram_tensor("xT", [D, NT], F32, kind="ExternalInput").ap()
        dr["w1"] = nc.dram_tensor("w1", [8 * D, 4096], F32, kind="ExternalInput").ap()
        dr["w2"] = nc.dram_tensor("w2", [8 * 4096, D], F32, kind="ExternalInput").ap()
        dr["b1c"] = nc.dram_tensor("b1c", [8 * 128, 32], F32, kind="ExternalInput").ap()
        dr["b2c"] = nc.dram_tensor("b2c", [8 * 128, 8], F32, kind="ExternalInput").ap()
        dr["wr"] = nc.dram_tensor("wr", [D, 8], F32, kind="ExternalInput").ap()
        dr["brb"] = nc.dram_tensor("brb", [128, 8], F32, kind="ExternalInput").ap()
        dr["iotd"] = nc.dram_tensor("iotd", [128, 8], F32, kind="ExternalInput").ap()
        dr["sel8"] = nc.dram_tensor("sel8", [8, 8 * 128], F32, kind="ExternalInput").ap()
        dr["ident"] = nc.dram_tensor("ident", [128, 128], F32, kind="ExternalInput").ap()
        dr["ones1"] = nc.dram_tensor("ones1", [1, 128], F32, kind="ExternalInput").ap()
        dr["onesc"] = nc.dram_tensor("onesc", [128, 1], F32, kind="ExternalInput").ap()
        dr["out"] = nc.dram_tensor("outT", [D, NT], F32, kind="ExternalOutput").ap()
    else:
        nc, dr = _NcProxy(fz["nc"], "s4_"), fz["dr"]

    with ExitStack() as ctx:
        e = ctx.enter_context
        if fz is None:
            stream = [e(nc.sbuf_tensor(f"strm{c}", [128, GUARD + NT + 6], F16))
                      for c in range(NCH)]
        else:
            stream = fz["stream"]
        hbuf = [e(nc.sbuf_tensor(f"hb{c}", [128, 1024], F16)) for c in range(32)]
        zacc = [e(nc.sbuf_tensor(f"za{c}", [128, 1024], F16)) for c in range(NCH)]
        tz = [e(nc.sbuf_tensor(f"tz{i}", [128, 512], F16)) for i in range(2)]
        xf32 = [e(nc.sbuf_tensor(f"xf{i}", [128, 1024], F32)) for i in range(2)]
        wrbuf = e(nc.sbuf_tensor("wrbuf", [128, 64], F32))
        brb = e(nc.sbuf_tensor("brb_s", [128, 8], F32))
        iotd = e(nc.sbuf_tensor("iotd_s", [128, 8], F32))
        sel8 = e(nc.sbuf_tensor("sel8_s", [8, 8 * 128], F16))
        ident = e(nc.sbuf_tensor("ident_s", [128, 128], F16))
        identf = e(nc.sbuf_tensor("identf_s", [128, 128], F32))
        ones1 = e(nc.sbuf_tensor("ones1_s", [1, 128], F16))
        onesc = e(nc.sbuf_tensor("onesc_s", [128, 1], F16))
        lgt = e(nc.sbuf_tensor("lgt", [128, 8], F32))
        sc1 = e(nc.sbuf_tensor("sc1", [128, 8], F32))
        nmx = e(nc.sbuf_tensor("nmx", [128, 4], F32))
        ctok = e(nc.sbuf_tensor("ctok", [128, 256], F32))
        ce8 = e(nc.sbuf_tensor("ce8", [8, NT], F16))
        cb16 = e(nc.sbuf_tensor("cb16", [128, 1024], F16))
        w1sl = [e(nc.sbuf_tensor(f"w1sl{i}", [128, 8 * 256], F16)) for i in range(2)]
        w2sl = [e(nc.sbuf_tensor(f"w2sl{i}", [128, 32 * 128], F16)) for i in range(2)]
        b1buf = e(nc.sbuf_tensor("b1buf", [128, 32], F32))
        b2buf = e(nc.sbuf_tensor("b2buf", [128, 8], F32))
        sqb = [e(nc.sbuf_tensor(f"sqb{i}", [128, TT], F16)) for i in range(2)]
        srow = e(nc.sbuf_tensor("srow", [1, TT], F32))
        qrow = e(nc.sbuf_tensor("qrow", [1, TT], F32))
        rstd = e(nc.sbuf_tensor("rstd", [1, TT], F32))
        rstd16 = e(nc.sbuf_tensor("rstd16", [1, TT], F16))
        murs16 = e(nc.sbuf_tensor("murs16", [1, TT], F16))

        psA = [e(nc.psum_tensor(f"psA{i}", [128, 512], F32)) for i in range(2)]
        psS = [e(nc.psum_tensor(f"psS{i}", [128, 512], F32)) for i in range(2)]
        psR = psS

        p = fz["p"] if fz else P(nc)
        sLd = p.sem("sLd")
        sRt = p.sem("sRt")
        sRs = p.sem("sRs")
        sW = p.sem("sW")
        sGe = p.sem("sGe")
        sZa = p.sem("sZa")
        sCb = p.sem("sCb")
        sSt = p.sem("sSt")
        proj = ProjPipe(p, psA, "proj")

        if fz is None:
            for c in range(NCH):
                p.op("gpsimd", lambda e, c=c: e.dma_start(
                    out=stream[c][:, GUARD:GUARD + NT],
                    in_=dr["xT"][c * 128:(c + 1) * 128, :]), inc=sLd, inc_by=16)
        consts = [(wrbuf, "wr"), (brb, "brb"), (iotd, "iotd"), (sel8, "sel8"),
                  (ident, "ident"), (identf, "ident"), (ones1, "ones1"),
                  (onesc, "onesc")]
        first_g = True
        for sb, nm in consts:
            if nm == "wr":
                p.op("sync", lambda e, sb=sb: e.dma_start(
                    out=sb[:, :].rearrange("p (c n) -> p c n", c=8),
                    in_=dr["wr"].rearrange("(c p) n -> p c n", p=128)),
                    waits=[fz["prev"]] if fz else (), inc=sLd, inc_by=16)
            else:
                p.op("gpsimd", lambda e, sb=sb, nm=nm: e.dma_start(out=sb[:, :],
                                                                   in_=dr[nm]),
                     waits=[fz["prev"]] if (fz and first_g) else (),
                     inc=sLd, inc_by=16)
                first_g = False
        all_loaded = sLd.w()

        for tc in range(32):
            xf = xf32[tc % 2]
            for k in range(NCH):
                waits = [all_loaded] if (tc == 0 and k == 0) else []
                if tc >= 2 and k == 0:
                    waits.append((proj.sMM, proj.sMM.n))

                def cst(e, k=k, tc=tc, xf=xf):
                    return e.tensor_copy(xf[:, k * 128:(k + 1) * 128],
                                         stream[k][:, GUARD + tc * 128:
                                                   GUARD + (tc + 1) * 128])
                p.op("vector", cst, waits=waits, inc=sRt)
            cast_done = sRt.w()

            def evl2(e, ap):
                return e.tensor_add(lgt[:, :], ap, brb[:, :])
            proj.group([xf[:, k * 128:(k + 1) * 128] for k in range(NCH)],
                       [wrbuf[:, k * 8:(k + 1) * 8] for k in range(NCH)],
                       evl2, width=8, extra_wait=[cast_done],
                       evac_wait=[(sRt, sRt.n)])
            lg_done = proj.drain()
            p.op("vector", lambda e: e.reduce_max(nmx[:, 0:1], lgt[:, :], axis=AX.X,
                                                  negate=True),
                 waits=[lg_done], inc=sRt)

            def ex(e):
                return e.activation(sc1[:, :], lgt[:, :], AF.Exp, bias=nmx[:, 0:1],
                                    scale=1.0, accum_out=nmx[:, 1:2])
            p.op("scalar", ex, waits=[sRt.w()], inc=sRs)
            p.op("vector", lambda e: e.reciprocal(nmx[:, 2:3], nmx[:, 1:2]),
                 waits=[sRs.w()], inc=sRt)
            p.op("vector", lambda e: e.tensor_scalar(sc1[:, :], lgt[:, :],
                                                     nmx[:, 0:1], None, ALU.add),
                 waits=[sRt.w()], inc=sRt)
            p.op("vector", lambda e: e.tensor_scalar(sc1[:, :], sc1[:, :], 0.0, None,
                                                     ALU.is_ge),
                 waits=[sRt.w()], inc=sRt)
            p.op("vector", lambda e: e.tensor_tensor(sc1[:, :], sc1[:, :], iotd[:, :],
                                                     ALU.mult),
                 waits=[sRt.w()], inc=sRt)
            p.op("vector", lambda e: e.reduce_max(nmx[:, 3:4], sc1[:, :], axis=AX.X,
                                                  negate=True),
                 waits=[sRt.w()], inc=sRt)
            p.op("vector", lambda e: e.tensor_scalar(sc1[:, :], sc1[:, :],
                                                     nmx[:, 3:4], None, ALU.add),
                 waits=[sRt.w()], inc=sRt)
            p.op("vector", lambda e: e.tensor_scalar(sc1[:, :], sc1[:, :], 0.0, None,
                                                     ALU.is_ge),
                 waits=[sRt.w()], inc=sRt)

            def cm(e, tc=tc):
                return e.tensor_scalar(ctok[:, tc * 8:(tc + 1) * 8], sc1[:, :],
                                       nmx[:, 2:3], None, ALU.mult)
            p.op("vector", cm, waits=[sRt.w()], inc=sRt)
        ctok_done = sRt.w()

        sTr4 = p.sem("sTr4")
        sTe4 = p.sem("sTe4")
        for tc in range(32):
            waits = [ctok_done] if tc == 0 else []
            if tc >= 2:
                waits.append((sTe4, tc - 1))

            def tr(e, tc=tc):
                return e.transpose(psS[tc % 2][0:8, 0:128],
                                   ctok[:, tc * 8:(tc + 1) * 8], identf[:, :])
            p.op("tensor", tr, waits=waits, inc=sTr4)

            def ev(e, tc=tc):
                return e.tensor_copy(ce8[:, tc * 128:(tc + 1) * 128],
                                     psS[tc % 2][0:8, 0:128])
            p.op("vector", ev, waits=[sTr4.w()], inc=sTe4)
        ce_done = sTe4.w()

        w1_g = 0
        w2_g = 0
        w1_free, w2_free = [], []
        h_free = None
        for tf in range(4):
            tf0 = tf * 1024
            for ei in range(8):
                p.op("sync", lambda e, ei=ei: e.dma_start(
                    out=b1buf[:, :], in_=dr["b1c"][ei * 128:(ei + 1) * 128, :]),
                    waits=[proj.drain()], inc=sLd, inc_by=16)
                p.op("sync", lambda e, ei=ei: e.dma_start(
                    out=b2buf[:, :], in_=dr["b2c"][ei * 128:(ei + 1) * 128, :]),
                    inc=sLd, inc_by=16)
                bias_ready = sLd.w()
                for sub in range(2):
                    def evcb(e, ap, sub=sub):
                        return e.tensor_copy(cb16[:, sub * 512:(sub + 1) * 512], ap)
                    proj.group([sel8[:, ei * 128:(ei + 1) * 128]],
                               [ce8[:, tf0 + sub * 512:tf0 + (sub + 1) * 512]],
                               evcb,
                               extra_wait=[ce_done, (sZa, sZa.n)]
                               if sub == 0 else ())
                cb_done = proj.drain()

                for hc in range(32):
                    bufi = w1_g % 2
                    waits = [w1_free[w1_g - 2]] if w1_g >= 2 else []
                    p.op("gpsimd", lambda e, ei=ei, hc=hc, bufi=bufi: e.dma_start(
                        out=w1sl[bufi][:, 0:8 * 128].rearrange("p (c n) -> p c n",
                                                               c=NCH),
                        in_=dr["w1"][ei * D:(ei + 1) * D, hc * 128:(hc + 1) * 128]
                        .rearrange("(c p) n -> p c n", p=128)),
                        waits=waits, inc=sW, inc_by=16)
                    w1_g += 1
                    w1_ready = sW.w()
                    for sub in range(2):
                        def evg(e, ap, hc=hc, sub=sub):
                            return e.activation(hbuf[hc][:, sub * 512:(sub + 1) * 512],
                                                ap, AF.Gelu,
                                                bias=b1buf[:, hc:hc + 1], scale=1.0)
                        ew = []
                        if sub == 0:
                            ew = [w1_ready, bias_ready]
                            if hc == 0:
                                ew.append(cb_done)
                        proj.group([w1sl[bufi][:, k * 128:(k + 1) * 128]
                                    for k in range(NCH)],
                                   [stream[k][:, GUARD + tf0 + sub * 512:
                                              GUARD + tf0 + (sub + 1) * 512]
                                    for k in range(NCH)],
                                   evg, evac_eng="scalar",
                                   extra_wait=ew,
                                   evac_wait=[h_free] if (h_free and sub == 0)
                                   else ())
                    w1_free.append(proj.sMM.w())
                h_done = proj.drain()
                for fo in range(NCH):
                    bufi = w2_g % 2
                    waits = [w2_free[w2_g - 2]] if w2_g >= 2 else []
                    p.op("gpsimd", lambda e, ei=ei, fo=fo, bufi=bufi: e.dma_start(
                        out=w2sl[bufi][:, :].rearrange("p (c n) -> p c n", c=32),
                        in_=dr["w2"][ei * 4096:(ei + 1) * 4096,
                                     fo * 128:(fo + 1) * 128]
                        .rearrange("(c p) n -> p c n", p=128)),
                        waits=waits, inc=sW, inc_by=16)
                    w2_g += 1
                    w2_ready = sW.w()
                    for sub in range(2):
                        sl = (fo * 2 + sub) % 2

                        def ev1(e, ap, ei=ei, fo=fo, sub=sub, sl=sl):
                            return e.scalar_tensor_tensor(
                                tz[sl][:, :], ap, b2buf[:, fo:fo + 1],
                                cb16[:, sub * 512:(sub + 1) * 512],
                                ALU.add, ALU.mult)

                        def ev2(e, ap, ei=ei, fo=fo, tf0=tf0, sub=sub, sl=sl):
                            if ei == 0:
                                return e.tensor_copy(
                                    zacc[fo][:, tf0 % 1024 + sub * 512:
                                             tf0 % 1024 + (sub + 1) * 512]
                                    if False else
                                    zacc[fo][:, sub * 512:(sub + 1) * 512],
                                    tz[sl][:, :])
                            return e.tensor_add(
                                zacc[fo][:, sub * 512:(sub + 1) * 512],
                                zacc[fo][:, sub * 512:(sub + 1) * 512],
                                tz[sl][:, :])
                        proj.group([w2sl[bufi][:, k * 128:(k + 1) * 128]
                                    for k in range(32)],
                                   [hbuf[k][:, sub * 512:(sub + 1) * 512]
                                    for k in range(32)],
                                   [ev1, ev2],
                                   extra_wait=[w2_ready, h_done]
                                   if (sub == 0 and fo == 0)
                                   else ([w2_ready] if sub == 0 else []),
                                   evac_wait=[cb_done])
                    w2_free.append(proj.sMM.w())
                h_free = proj.sMM.w()
            zdone = proj.drain()
            for fo in range(NCH):
                def addz(e, fo=fo, tf0=tf0):
                    return e.tensor_add(
                        stream[fo][:, GUARD + tf0:GUARD + tf0 + 1024],
                        stream[fo][:, GUARD + tf0:GUARD + tf0 + 1024],
                        zacc[fo][:, 0:1024])
                p.op("vector", addz, waits=[zdone] if fo == 0 else (), inc=sZa)

        moe_done = sZa.w()
        ln_bufs = dict(sq=sqb, psR=psR, srow=srow, qrow=qrow, rstd=rstd,
                       rstd16=rstd16, murs16=murs16,
                       onesc=onesc[:, :], ones1=ones1[:, :])
        ln_done = emit_ln(p, stream, ln_bufs, "4", pre_wait=[moe_done])

        for c in range(NCH):
            p.op("gpsimd", lambda e, c=c: e.dma_start(
                out=dr["out"][c * 128:(c + 1) * 128, :],
                in_=stream[c][:, GUARD:GUARD + NT]),
                waits=[ln_done] if c == 0 else (), inc=sSt, inc_by=16)
        p.wait("gpsimd", sSt.w())
        if fz is None:
            p.alloc_sems(ctx)
            block = e(nc.Block())
            p.emit(block)
    return nc


def stage4_inputs(inputs, b, xC=None):
    f = np.asarray
    E = 8
    b1 = f(inputs["L1_ffn_b1"])
    b1c = b1.reshape(E, 32, 128).transpose(0, 2, 1).reshape(E * 128, 32)
    b2 = f(inputs["L1_ffn_b2"])
    b2c = b2.reshape(E, 8, 128).transpose(0, 2, 1).reshape(E * 128, 8)
    sel8 = np.zeros((8, 8 * 128), np.float32)
    for ei in range(8):
        sel8[ei, ei * 128:(ei + 1) * 128] = 1.0
    d = {
        "xT": np.ascontiguousarray(xC[b].T),
        "w1": f(inputs["L1_ffn_W1"]).reshape(E * D, 4096),
        "w2": f(inputs["L1_ffn_W2"]).reshape(E * 4096, D),
        "b1c": b1c, "b2c": b2c,
        "wr": f(inputs["L1_ffn_Wr"]),
        "brb": np.broadcast_to(f(inputs["L1_ffn_br"]).reshape(1, 8), (128, 8)),
        "iotd": np.broadcast_to((8.0 - np.arange(8, dtype=np.float32)).reshape(1, 8),
                                (128, 8)),
        "sel8": sel8,
        "ident": np.eye(128, dtype=np.float32),
        "ones1": np.ones((1, 128), np.float32),
        "onesc": np.ones((128, 1), np.float32),
    }
    return {k: np.ascontiguousarray(np.asarray(v, dtype=np.float32))
            for k, v in d.items()}


def _band_masks():
    ii = np.arange(128)[:, None]
    m_first = np.where(np.abs(ii - np.arange(256)[None, :]) <= 128, 0.0, MASKVAL)
    jj = np.arange(384)[None, :]
    m_int = np.where((jj >= ii) & (jj <= ii + 256), 0.0, MASKVAL)
    m_last = np.ascontiguousarray(m_int[:, 0:256])
    return (m_first.astype(np.float32), m_int.astype(np.float32),
            m_last.astype(np.float32))


def stage1_inputs(inputs, b):
    mf, mi, ml = _band_masks()
    f = np.asarray
    d = {
        "xT": np.ascontiguousarray(f(inputs["L0_attn_x"][b]).T),
        "wq": np.ascontiguousarray(f(inputs["L0_attn_Wq"]).T),
        "wk": np.ascontiguousarray(f(inputs["L0_attn_Wk"]).T),
        "wv": np.ascontiguousarray(f(inputs["L0_attn_Wv"]).T),
        "wp": np.ascontiguousarray(f(inputs["L0_attn_Wp"]).T),
        "bq": np.ascontiguousarray(f(inputs["L0_attn_bq"]).reshape(8, 128).T),
        "bk": np.ascontiguousarray(f(inputs["L0_attn_bk"]).reshape(8, 128).T),
        "bv": np.ascontiguousarray(f(inputs["L0_attn_bv"]).reshape(8, 128).T),
        "bp": np.ascontiguousarray(f(inputs["L0_attn_bp"]).reshape(8, 128).T),
        "ident": np.eye(128, dtype=np.float32),
        "ones1": np.ones((1, 128), np.float32),
        "onesc": np.ones((128, 1), np.float32),
        "mfirst": mf, "mint": mi, "mlast": ml,
        "bvr": f(inputs["L0_attn_bv"]).reshape(1, D),
    }
    return {k: np.ascontiguousarray(np.asarray(v, dtype=np.float32)) for k, v in d.items()}


def build_fused():
    nc = bass.Bass("TRN2", target_bir_lowering=False)

    def A(name, shape):
        return nc.dram_tensor(name, shape, F32, kind="ExternalInput").ap()

    com = {"ident": A("ident", [128, 128]), "ones1": A("ones1", [1, 128]),
           "onesc": A("onesc", [128, 1])}
    d1 = dict(com)
    d1["xT"] = A("xT", [D, NT])
    for w in ("wq", "wk", "wv", "wp"):
        d1[w] = A(w, [D, D])
    for b in ("bq", "bk", "bv", "bp"):
        d1[b] = A(b, [128, 8])
    d1["mfirst"] = A("mfirst", [128, 256])
    d1["mint"] = A("mint", [128, 384])
    d1["mlast"] = A("mlast", [128, 256])
    d1["bvr"] = A("bvr", [1, D])
    NL = 6
    d2 = dict(com)
    d2["wqkv"] = A("wqkv", [NL * D, 3072])
    d2["wo"] = A("wo", [NL * D, D])
    d2["w1"] = A("w1", [NL * D, 5632])
    d2["w2"] = A("w2", [NL * 2816, D])
    d2["b1c"] = A("b1c", [NL * 128, 44])
    d2["b2c"] = A("b2c", [NL * 128, 8])
    d2["lngc"] = A("lngc", [NL * 128, 8])
    d2["lnbc"] = A("lnbc", [NL * 128, 8])
    for nm in ("qsc", "ksc", "qss", "kss"):
        d2[nm] = A(nm, [NL * 128, 1])
    for nm in ("tq1", "tq2", "tk1", "tk2"):
        d2[nm] = A(nm, [128, NT])
    d2["prot"] = A("prot", [128, 128])
    d2["bones2"] = A("bones2", [128, 2])
    d2["sel2"] = A("sel2", [2, 128])
    d2["m_a"] = A("m_a", [128, 512])
    d2["m_b"] = A("m_b", [128, 512])
    d2["m0a"] = A("m0a", [128, 256])
    d2["m0b"] = A("m0b", [128, 256])
    d3 = dict(com)
    for nm in ("qr", "kr", "vr", "wz"):
        d3[nm] = A(nm, [D, D])
    for k in range(4):
        d3[f"md{k}"] = A(f"md{k}", [128, 512])
    d4 = dict(com)
    d4["w1"] = A("e1", [8 * D, 4096])
    d4["w2"] = A("e2", [8 * 4096, D])
    d4["b1c"] = A("eb1", [8 * 128, 32])
    d4["b2c"] = A("eb2", [8 * 128, 8])
    d4["wr"] = A("wr", [D, 8])
    d4["brb"] = A("brb", [128, 8])
    d4["iotd"] = A("iotd", [128, 8])
    d4["sel8"] = A("sel8", [8, 8 * 128])
    d4["out"] = nc.dram_tensor("outT", [D, NT], F32, kind="ExternalOutput").ap()
    xa = nc.dram_tensor("xa_scr", [D, NT], F32, kind="Internal").ap()
    d2["xa"] = xa

    with ExitStack() as ctx:
        e = ctx.enter_context
        stream = [e(nc.sbuf_tensor(f"strm{c}", [128, GUARD + NT + 6], F16))
                  for c in range(NCH)]
        p = P(nc)
        fz = {"nc": nc, "p": p, "stream": stream, "xa_scr": xa,
              "prev": None, "xa_done": None, "dr": d1}
        build_stage1(fz=fz)
        fz["dr"] = d2
        build_stage2(list(range(6)), True, fz=fz)
        fz["dr"] = d3
        build_stage3(fz=fz)
        fz["dr"] = d4
        build_stage4(fz=fz)
        p.alloc_sems(ctx)
        block = e(nc.Block())
        p.emit(block)
    return nc


def fused_inputs(inputs):
    z = np.zeros((1, NT, D), np.float32)
    m = {}
    b2 = stage2_inputs(inputs, 0, xA=z)
    b2.pop("xT")
    m.update(b2)
    b3 = stage3_inputs(inputs, 0, xB=z)
    b3.pop("xT")
    m.update(b3)
    b4 = stage4_inputs(inputs, 0, xC=z)
    b4.pop("xT")
    for a, bnm in (("e1", "w1"), ("e2", "w2"), ("eb1", "b1c"), ("eb2", "b2c")):
        m[a] = b4.pop(bnm)
    m.update(b4)
    m.update(stage1_inputs(inputs, 0))
    return m


def kernel(**inputs):
    nc = build_fused()
    base = fused_inputs(inputs)
    maps = [dict(base, xT=np.ascontiguousarray(np.asarray(inputs["L0_attn_x"][b]).T))
            for b in range(8)]
    res = run_bass_kernel_spmd(nc, maps, core_ids=list(range(8)))
    return np.stack([np.ascontiguousarray(r["outT"].T) for r in res.results])


def kernel_unfused(**inputs):
    cores = list(range(8))
    dummy = np.zeros((8, 1, 1), np.float32)

    def shard_x():
        return [np.ascontiguousarray(np.asarray(inputs["L0_attn_x"][b]).T)
                for b in range(8)]

    cur = shard_x()
    base = stage1_inputs(inputs, 0)
    maps = [dict(base, xT=cur[b]) for b in range(8)]
    res = run_bass_kernel_spmd(build_stage1(), maps, core_ids=cores)
    cur = [np.ascontiguousarray(r["outT"]) for r in res.results]

    base = stage2_inputs(inputs, 0, xA=np.zeros((1, NT, D), np.float32))
    maps = [dict(base, xT=cur[b]) for b in range(8)]
    res = run_bass_kernel_spmd(build_stage2(list(range(6)), True), maps,
                               core_ids=cores)
    cur = [np.ascontiguousarray(r["outT"]) for r in res.results]

    base = stage3_inputs(inputs, 0, xB=np.zeros((1, NT, D), np.float32))
    maps = [dict(base, xT=cur[b]) for b in range(8)]
    res = run_bass_kernel_spmd(build_stage3(), maps, core_ids=cores)
    cur = [np.ascontiguousarray(r["outT"]) for r in res.results]

    base = stage4_inputs(inputs, 0, xC=np.zeros((1, NT, D), np.float32))
    maps = [dict(base, xT=cur[b]) for b in range(8)]
    res = run_bass_kernel_spmd(build_stage4(), maps, core_ids=cores)
    return np.stack([np.ascontiguousarray(r["outT"].T) for r in res.results])


# version 05
# speedup vs baseline: 1.0160x; 1.0160x over previous
import numpy as np
from contextlib import ExitStack
import concourse.bass as bass
import concourse.mybir as mybir
from concourse.bass_utils import run_bass_kernel_spmd

F32 = mybir.dt.float32
F16 = mybir.dt.float16
BF16 = mybir.dt.bfloat16
AF = mybir.ActivationFunctionType
ALU = mybir.AluOpType
AX = mybir.AxisListType

GUARD = 2
NT = 4096
D = 1024
NCH = 8
TT = 512
MASKVAL = -60000.0

ENGINES = ("sync", "scalar", "vector", "tensor", "gpsimd")


class Sem:
    __slots__ = ("name", "n", "h")

    def __init__(self, name):
        self.name = name
        self.n = 0
        self.h = None

    def w(self, val=None):
        return (self, self.n if val is None else val)


class P:
    def __init__(self, nc):
        self.nc = nc
        self.q = {e: [] for e in ENGINES}
        self.sems = []

    def sem(self, name):
        for s in self.sems:
            if s.name == name:
                return s
        s = Sem(name)
        self.sems.append(s)
        return s

    def op(self, eng, build, waits=(), inc=None, inc_by=1):
        if inc is not None:
            inc.n += inc_by
        ws = [(s, v) for (s, v) in waits if v > 0]

        def f(e, build=build, ws=ws, inc=inc, by=inc_by):
            for (s, v) in ws:
                e.wait_ge(s.h, v)
            ins = build(e)
            if inc is not None:
                ins.then_inc(inc.h, by)
        self.q[eng].append(f)

    def wait(self, eng, *waits):
        ws = [(s, v) for (s, v) in waits if v > 0]

        def f(e, ws=ws):
            for (s, v) in ws:
                e.wait_ge(s.h, v)
        self.q[eng].append(f)

    def alloc_sems(self, ctx):
        for i, s in enumerate(self.sems):
            s.h = ctx.enter_context(self.nc.semaphore(f"{s.name}_{i}"))

    def emit(self, block):
        for eng in ENGINES:
            ops = self.q[eng]
            if ops:
                getattr(block, eng)(lambda e, ops=ops: [f(e) for f in ops])


def cdiv(a, b):
    return (a + b - 1) // b


class _NcProxy:

    def __init__(self, nc, tag):
        self._nc = nc
        self._tag = tag

    def sbuf_tensor(self, name, shape, dtype, **kw):
        return self._nc.sbuf_tensor(self._tag + name, shape, dtype, **kw)

    def psum_tensor(self, name, *a, **kw):
        return self._nc.psum_tensor(self._tag + name, *a, **kw)

    def __getattr__(self, k):
        return getattr(self._nc, k)



class ProjPipe:

    def __init__(self, p, psum_banks, name):
        self.p = p
        self.banks = psum_banks
        self.nb = len(psum_banks)
        self.sMM = p.sem(name + "_mm")
        self.sEv = p.sem(name + "_ev")
        self.bEv = self.sEv.n
        self.g = 0

    def group(self, lhsT_list, rhs_list, evac, extra_wait=(), extra_mm=(),
              evac_eng="vector", width=None, rows=None, evac_wait=()):
        g = self.g
        bank = self.banks[g % self.nb]
        ap = bank[0:rows if rows else 128, 0:width if width else bank.shape[1]]
        waits = list(extra_wait)
        if g >= self.nb:
            waits.append(self.sEv.w(self.bEv + g - self.nb + 1))
        n = len(lhsT_list)
        mms = [(lhsT_list[k], rhs_list[k]) for k in range(n)] + list(extra_mm)
        for k, (lh, rh) in enumerate(mms):
            def mk(e, lh=lh, rh=rh, f=(k == 0), l=(k == len(mms) - 1), ap=ap):
                return e.matmul(ap, lh, rh, start=f, stop=l)
            self.p.op("tensor", mk, waits=waits if k == 0 else (),
                      inc=self.sMM if k == len(mms) - 1 else None)
        mm_done = self.sMM.w()
        evacs = evac if isinstance(evac, (list, tuple)) else [evac]
        for i, ef in enumerate(evacs):
            def ev(e, ef=ef, ap=ap):
                return ef(e, ap)
            self.p.op(evac_eng, ev,
                      waits=([mm_done] + list(evac_wait)) if i == 0 else (),
                      inc=self.sEv if i == len(evacs) - 1 else None)
        self.g += 1

    def drain(self):
        return self.sEv.w()


class AttnPipe:

    def __init__(self, p, bufs, name, exp_scale=1.0):
        self.p = p
        self.b = bufs
        self.exp_scale = exp_scale
        self.u = 0
        self.sS = p.sem(name + "S")
        self.sRm = p.sem(name + "Rm")
        self.sEx = p.sem(name + "Ex")
        self.sRc = p.sem(name + "Rc")
        self.sPn = p.sem(name + "Pn")
        self.sTr = p.sem(name + "Tr")
        self.sTe = p.sem(name + "Te")
        self.sPV = p.sem(name + "PV")
        self.sEo = p.sem(name + "Eo")
        self.o = {id(s): s.n for s in (self.sS, self.sRm, self.sEx, self.sRc,
                                       self.sPn, self.sTr, self.sTe, self.sPV,
                                       self.sEo)}
        self._pending = None

    def _r(self, sem, v):
        return (sem, self.o[id(sem)] + v)

    def _part1(self, u, d):
        p, b = self.p, self.b
        si = b["psS"][u % 2]
        waits = list(d.get("wait", ()))
        if u >= 2:
            waits.append(self._r(self.sEx, u - 1))

        def mmS(e, d=d, si=si):
            return e.matmul(si[:, 0:d["jn"]], d["s_lhsT"], d["s_rhs"],
                            start=True, stop=False)
        p.op("tensor", mmS, waits=waits)

        def mmM(e, d=d, si=si):
            return e.matmul(si[:, 0:d["jn"]], d["m_lhsT"], d["m_rhs"],
                            start=False, stop=True)
        p.op("tensor", mmM, inc=self.sS)

    def _vs(self, u, d):
        p, b = self.p, self.b
        si = b["psS"][u % 2]
        sl = u % 2
        nds = b["nds"]
        nm = nds[:, sl:sl + 1]
        dn = nds[:, 2 + sl:3 + sl]
        rc = nds[:, 4 + sl:5 + sl]

        def rmx(e, d=d, si=si, nm=nm):
            return e.reduce_max(nm, si[:, 0:d["jn"]], axis=AX.X, negate=True)
        p.op("vector", rmx, waits=[self._r(self.sS, u + 1), self._r(self.sEx, u - 1)], inc=self.sRm)

        def ex(e, d=d, si=si, nm=nm, dn=dn, sl=sl):
            return e.activation(b["Pbuf"][sl][:, 0:d["jn"]], si[:, 0:d["jn"]], AF.Exp,
                                bias=nm, scale=self.exp_scale, accum_out=dn)
        p.op("scalar", ex, waits=[self._r(self.sRm, u + 1), self._r(self.sPn, u - 1),
                                  self._r(self.sRc, u - 1)], inc=self.sEx)

        def rcp(e, dn=dn, rc=rc):
            return e.reciprocal(rc, dn)
        p.op("vector", rcp, waits=[self._r(self.sEx, u + 1)], inc=self.sRc)

        def pn(e, d=d, sl=sl, rc=rc):
            return e.tensor_scalar_mul(b["Pn"][sl][:, 0:d["jn"]],
                                       b["Pbuf"][sl][:, 0:d["jn"]], rc)
        p.op("vector", pn, waits=[self._r(self.sRc, u + 1), self._r(self.sTr, u - 1)], inc=self.sPn)

    def _part2(self, u, d):
        p, b = self.p, self.b
        sl = u % 2
        for c in range(d["nj"]):
            def tr(e, d=d, sl=sl, c=c):
                return e.transpose(b["psT"][sl][:, c * 128:(c + 1) * 128],
                                   b["Pn"][sl][:, c * 128:(c + 1) * 128], d["ident"])
            p.op("tensor", tr,
                 waits=[self._r(self.sPn, u + 1), self._r(self.sTe, u - 1)] if c == 0 else (),
                 inc=self.sTr if c == d["nj"] - 1 else None)

        def evt(e, d=d, sl=sl):
            return e.tensor_copy(b["Pt"][sl][:, 0:d["jn"]], b["psT"][sl][:, 0:d["jn"]])
        p.op("vector", evt, waits=[self._r(self.sTr, u + 1), self._r(self.sPV, u - 1)], inc=self.sTe)
        oi = b["psO"][sl]
        for c in range(d["nj"]):
            def pv(e, d=d, oi=oi, c=c, sl=sl):
                return e.matmul(oi[0:64, :], d["pv_lhsT"](c),
                                b["Pt"][sl][:, c * 128:(c + 1) * 128],
                                start=(c == 0), stop=(c == d["nj"] - 1))
            p.op("tensor", pv,
                 waits=[self._r(self.sTe, u + 1), self._r(self.sEo, u - 1)] if c == 0 else (),
                 inc=self.sPV if c == d["nj"] - 1 else None)

        def evo(e, d=d, oi=oi):
            return e.tensor_copy(d["out_dst"], oi[0:64, :])
        p.op("vector", evo, waits=[self._r(self.sPV, u + 1)], inc=self.sEo)

    def add(self, units):
        for d in units:
            u = self.u
            self._part1(u, d)
            self._vs(u, d)
            if self._pending is not None:
                self._part2(u - 1, self._pending)
            self._pending = d
            self.u += 1

    def flush(self):
        if self._pending is not None:
            self._part2(self.u - 1, self._pending)
            self._pending = None
        return self.sEo.w()


def emit_ln(p, stream, bufs, sems, affine=None, shifted=False, out=None,
            pre_wait=(), eps=1e-5, ntok=NT):
    sq, psR = bufs["sq"], bufs["psR"]
    srow, qrow, rstd = bufs["srow"], bufs["qrow"], bufs["rstd"]
    rstd16, murs16 = bufs["rstd16"], bufs["murs16"]
    onesc, ones1 = bufs["onesc"], bufs["ones1"]
    nch = len(stream)
    sSq = p.sem("lnSq")
    sMM = p.sem("lnMM")
    sEv = p.sem("lnEv")
    sVs = p.sem("lnVs")
    sSc = p.sem("lnSc")
    sBc = p.sem("lnBc")
    sAp = p.sem("lnAp")
    bMM, bAp = sMM.n, sAp.n

    def sview(c, t0, t1):
        if shifted and c >= nch // 2:
            return stream[c][:, GUARD - 1 + t0:GUARD - 1 + t1]
        return stream[c][:, GUARD + t0:GUARD + t1]

    ntile = ntok // TT
    for ti in range(ntile):
        t0, t1 = ti * TT, (ti + 1) * TT
        for c in range(nch):
            g = ti * nch + c
            sqb = sq[g % 2]
            waits = list(pre_wait) if g == 0 else []
            if g >= 2:
                waits.append((sMM, bMM + 2 * (g - 1)))

            def mkq(e, c=c, t0=t0, t1=t1, sqb=sqb):
                return e.tensor_tensor(sqb[:, :], sview(c, t0, t1), sview(c, t0, t1), ALU.mult)
            p.op("vector", mkq, waits=waits, inc=sSq)
            sq_done = sSq.w()
            waits2 = list(pre_wait) if g == 0 else []
            if c == 0 and ti >= 1:
                waits2.append((sAp, bAp + ti))

            def mmS(e, c=c, t0=t0, t1=t1):
                return e.matmul(psR[0][0:1, :], onesc, sview(c, t0, t1),
                                start=(c == 0), stop=(c == nch - 1))
            p.op("tensor", mmS, waits=waits2, inc=sMM)

            def mmQ(e, c=c, sqb=sqb):
                return e.matmul(psR[1][0:1, :], onesc, sqb[:, :],
                                start=(c == 0), stop=(c == nch - 1))
            p.op("tensor", mmQ, waits=[sq_done], inc=sMM)
        mm_done = sMM.w()
        p.op("vector", lambda e: e.tensor_copy(srow[:, :], psR[0][0:1, :]),
             waits=[mm_done], inc=sVs)
        p.op("vector", lambda e: e.tensor_copy(qrow[:, :], psR[1][0:1, :]), inc=sVs)
        p.op("vector", lambda e: e.tensor_scalar_mul(srow[:, :], srow[:, :], 1.0 / D),
             waits=[sVs.w()], inc=sVs)
        p.op("vector", lambda e: e.tensor_tensor(rstd[:, :], srow[:, :], srow[:, :], ALU.mult),
             waits=[sVs.w()], inc=sVs)
        p.op("vector", lambda e: e.scalar_tensor_tensor(rstd[:, :], qrow[:, :], 1.0 / D,
                                                        rstd[:, :], ALU.mult, ALU.subtract),
             waits=[sVs.w()], inc=sVs)
        p.op("vector", lambda e: e.tensor_scalar_add(rstd[:, :], rstd[:, :], eps),
             waits=[sVs.w()], inc=sVs)
        p.op("scalar", lambda e: e.sqrt(rstd[:, :], rstd[:, :]),
             waits=[sVs.w()], inc=sSc)
        p.op("vector", lambda e: e.reciprocal(rstd[:, :], rstd[:, :]),
             waits=[sSc.w()], inc=sVs)
        p.op("vector", lambda e: e.tensor_copy(rstd16[:, :], rstd[:, :]),
             waits=[sVs.w()], inc=sVs)
        p.op("vector", lambda e: e.tensor_tensor(murs16[:, :], srow[:, :], rstd[:, :], ALU.mult),
             waits=[sVs.w()], inc=sVs)
        prep_done = sVs.w()
        def mb1(e):
            return e.matmul(psR[0][:, :], ones1, rstd16[:, :], start=True, stop=True)
        p.op("tensor", mb1, waits=[prep_done], inc=sBc)

        def mb2(e):
            return e.matmul(psR[1][:, :], ones1, murs16[:, :], start=True, stop=True)
        p.op("tensor", mb2, inc=sBc)
        bc_done = sBc.w()
        for c in range(nch):
            dslice = (stream[c][:, GUARD + t0:GUARD + t1] if out is None
                      else out[c][:, t0:t1])

            def ap1(e, c=c, t0=t0, t1=t1, dslice=dslice):
                return e.tensor_tensor(dslice, sview(c, t0, t1), psR[0][:, :], ALU.mult)
            p.op("vector", ap1, waits=[bc_done] if c == 0 else ())
            last = (c == nch - 1)

            def ap2(e, dslice=dslice):
                return e.tensor_tensor(dslice, dslice, psR[1][:, :], ALU.subtract)
            p.op("vector", ap2, inc=sAp if (last and affine is None) else None)
            if affine is not None:
                g_col, b_col = affine(c)

                def ap3(e, dslice=dslice, g_col=g_col, b_col=b_col):
                    return e.tensor_scalar(dslice, dslice, g_col, b_col, ALU.mult, ALU.add)
                p.op("vector", ap3, inc=sAp if last else None)
    return sAp.w()



def build_stage1(fz=None):
    if fz is None:
        nc = bass.Bass("TRN2", target_bir_lowering=False)
        dr = {}
        dr["xT"] = nc.dram_tensor("xT", [D, NT], F32, kind="ExternalInput").ap()
        for w in ("wq", "wk", "wv", "wp"):
            dr[w] = nc.dram_tensor(w, [D, D], F32, kind="ExternalInput").ap()
        for b in ("bq", "bk", "bv", "bp"):
            dr[b] = nc.dram_tensor(b, [128, 8], F32, kind="ExternalInput").ap()
        dr["ident"] = nc.dram_tensor("ident", [128, 128], F32, kind="ExternalInput").ap()
        dr["ones1"] = nc.dram_tensor("ones1", [1, 128], F32, kind="ExternalInput").ap()
        dr["onesc"] = nc.dram_tensor("onesc", [128, 1], F32, kind="ExternalInput").ap()
        dr["mfirst"] = nc.dram_tensor("mfirst", [128, 256], F32, kind="ExternalInput").ap()
        dr["mint"] = nc.dram_tensor("mint", [128, 384], F32, kind="ExternalInput").ap()
        dr["mlast"] = nc.dram_tensor("mlast", [128, 256], F32, kind="ExternalInput").ap()
        dr["bvr"] = nc.dram_tensor("bvr", [1, D], F32, kind="ExternalInput").ap()
        dr["out"] = nc.dram_tensor("outT", [D, NT], F32, kind="ExternalOutput").ap()
    else:
        nc, dr = _NcProxy(fz["nc"], "s1_"), fz["dr"]

    with ExitStack() as ctx:
        e = ctx.enter_context
        if fz is None:
            stream = [e(nc.sbuf_tensor(f"strm{c}", [128, GUARD + NT + 6], F16))
                      for c in range(NCH)]
        else:
            stream = fz["stream"]
        yT = [e(nc.sbuf_tensor(f"yT{c}", [128, NT], F16)) for c in range(NCH)]
        wp16 = [e(nc.sbuf_tensor(f"wp16_{c}", [128, D], F16)) for c in range(NCH)]
        bias = {b: e(nc.sbuf_tensor(b + "_s", [128, 8], F32)) for b in ("bq", "bk", "bv", "bp")}
        ident = e(nc.sbuf_tensor("ident_s", [128, 128], F16))
        ones1 = e(nc.sbuf_tensor("ones1_s", [1, 128], F16))
        onesc = e(nc.sbuf_tensor("onesc_s", [128, 1], F16))
        masks = {0: e(nc.sbuf_tensor("mfirst_s", [128, 256], F16)),
                 1: e(nc.sbuf_tensor("mint_s", [128, 384], F16)),
                 2: e(nc.sbuf_tensor("mlast_s", [128, 256], F16))}
        wsl = [[e(nc.sbuf_tensor(f"wsl{k}_{i}", [128, 3 * 128], F16)) for i in range(2)]
               for k in range(NCH)]
        qT = e(nc.sbuf_tensor("qT", [128, NT], F16))
        kT = e(nc.sbuf_tensor("kT", [128, NT], F16))
        vtok = e(nc.sbuf_tensor("vtok", [128, NT], F16))
        bvrow = e(nc.sbuf_tensor("bvrow", [1, D], F16))
        Pbuf = [e(nc.sbuf_tensor(f"Pbuf{i}", [128, 384], F32)) for i in range(2)]
        Pn = [e(nc.sbuf_tensor(f"Pn{i}", [128, 384], F16)) for i in range(2)]
        Pt = [e(nc.sbuf_tensor(f"Pt{i}", [128, 384], F16)) for i in range(2)]
        nds = e(nc.sbuf_tensor("nds", [128, 6], F32))
        sqb = [e(nc.sbuf_tensor(f"sqb{i}", [128, TT], F16)) for i in range(2)]
        srow = e(nc.sbuf_tensor("srow", [1, TT], F32))
        qrow = e(nc.sbuf_tensor("qrow", [1, TT], F32))
        rstd = e(nc.sbuf_tensor("rstd", [1, TT], F32))
        rstd16 = e(nc.sbuf_tensor("rstd16", [1, TT], F16))
        murs16 = e(nc.sbuf_tensor("murs16", [1, TT], F16))

        psA = [e(nc.psum_tensor(f"psA{i}", [128, 512], F32)) for i in range(2)]
        psS = [e(nc.psum_tensor(f"psS{i}", [128, 512], F32)) for i in range(2)]
        psT = [e(nc.psum_tensor(f"psT{i}", [128, 512], F16)) for i in range(2)]
        psO = [e(nc.psum_tensor(f"psO{i}", [64, 128], F32)) for i in range(2)]
        psR = psA

        p = fz["p"] if fz else P(nc)
        sLd = p.sem("sLd")
        sSlab = p.sem("sSlab")
        sSlabF = p.sem("sSlabF")
        sSt = p.sem("sSt")
        proj = ProjPipe(p, psA, "proj")
        attn = AttnPipe(p, dict(psS=psS, psT=psT, psO=psO, Pbuf=Pbuf, Pn=Pn,
                                Pt=Pt, nds=nds), "at")

        for c in range(NCH):
            p.op("gpsimd", lambda e, c=c: e.dma_start(
                out=stream[c][:, GUARD:GUARD + NT],
                in_=dr["xT"][c * 128:(c + 1) * 128, :]), inc=sLd, inc_by=16)
        consts = [(ident, "ident"), (ones1, "ones1"), (onesc, "onesc"),
                  (masks[0], "mfirst"), (masks[1], "mint"), (masks[2], "mlast")]
        for sb, nm in consts:
            p.op("gpsimd", lambda e, sb=sb, nm=nm: e.dma_start(out=sb[:, :], in_=dr[nm]),
                 inc=sLd, inc_by=16)
        for c in range(NCH):
            p.op("gpsimd", lambda e, c=c: e.dma_start(
                out=wp16[c][:, :], in_=dr["wp"][c * 128:(c + 1) * 128, :]),
                inc=sLd, inc_by=16)
        for b in ("bq", "bk", "bv", "bp"):
            p.op("sync", lambda e, b=b: e.dma_start(out=bias[b][:, :], in_=dr[b]),
                 inc=sLd, inc_by=16)
        p.op("gpsimd", lambda e: e.dma_start(out=bvrow[:, :], in_=dr["bvr"]),
             inc=sLd, inc_by=16)
        all_loaded = sLd.w()

        for c in range(NCH):
            p.op("vector", lambda e, c=c: e.memset(stream[c][:, 0:GUARD], 0.0),
                 waits=[all_loaded] if c == 0 else ())

        slab_free = []
        for pr in range(8):
            bufi = pr % 2
            for ck in range(NCH):
                for wi, w in enumerate(("wq", "wk", "wv")):
                    waits = []
                    if pr >= 2 and wi == 0 and ck == 0:
                        waits.append(slab_free[pr - 2])
                    p.op("gpsimd", lambda e, w=w, pr=pr, ck=ck, wi=wi, bufi=bufi: e.dma_start(
                        out=wsl[ck][bufi][:, wi * 128:(wi + 1) * 128],
                        in_=dr[w][ck * 128:(ck + 1) * 128, pr * 128:(pr + 1) * 128]),
                        waits=waits, inc=sSlab, inc_by=16)
            slab_ready = sSlab.w()

            for which, (dst, wofs, bcol, scl) in enumerate((
                    (qT, 0, "bq", 1.0 / 32.0), (kT, 128, "bk", None))):
                for ti in range(8):
                    t0, t1 = GUARD + ti * TT, GUARD + (ti + 1) * TT
                    if scl is not None:
                        def mkev(e, ap, dst=dst, ti=ti, bcol=bcol, scl=scl, pr=pr):
                            return e.tensor_scalar(dst[:, ti * TT:(ti + 1) * TT], ap,
                                                   bias[bcol][:, pr:pr + 1], scl,
                                                   ALU.add, ALU.mult)
                    else:
                        def mkev(e, ap, dst=dst, ti=ti, bcol=bcol, pr=pr):
                            return e.tensor_scalar(dst[:, ti * TT:(ti + 1) * TT], ap,
                                                   bias[bcol][:, pr:pr + 1], None, ALU.add)
                    proj.group([wsl[k][bufi][:, wofs:wofs + 128] for k in range(NCH)],
                               [stream[k][:, t0:t1] for k in range(NCH)],
                               mkev,
                               extra_wait=[slab_ready, all_loaded] if (which == 0 and ti == 0) else ())
            for tc in range(32):
                def mkev(e, ap, tc=tc):
                    return e.tensor_copy(vtok[:, tc * 128:(tc + 1) * 128], ap)
                proj.group([stream[k][:, GUARD + tc * 128:GUARD + (tc + 1) * 128]
                            for k in range(NCH)],
                           [wsl[k][bufi][:, 256:384] for k in range(NCH)],
                           mkev, width=128,
                           extra_mm=[(ones1[:, :], bvrow[:, pr * 128:(pr + 1) * 128])])
            slab_free.append(proj.sMM.w())
            vtok_done = proj.drain()

            units = []
            for hl in range(2):
                for ib in range(32):
                    if ib == 0:
                        j0, jn, mk = 0, 256, 0
                    elif ib == 31:
                        j0, jn, mk = 30 * 128, 256, 2
                    else:
                        j0, jn, mk = (ib - 1) * 128, 384, 1
                    u = dict(
                        jn=jn, nj=jn // 128, ident=ident[:, :],
                        s_lhsT=qT[hl * 64:(hl + 1) * 64, ib * 128:(ib + 1) * 128],
                        s_rhs=kT[hl * 64:(hl + 1) * 64, j0:j0 + jn],
                        m_lhsT=ident[:, :], m_rhs=masks[mk][:, 0:jn],
                        pv_lhsT=(lambda c, j0=j0, hl=hl:
                                 vtok[:, (j0 // 128 + c) * 128 + hl * 64:
                                      (j0 // 128 + c) * 128 + hl * 64 + 64]),
                        out_dst=yT[pr][hl * 64:hl * 64 + 64, ib * 128:(ib + 1) * 128],
                    )
                    if ib <= 1 and hl == 0:
                        u["wait"] = [vtok_done]
                    units.append(u)
            attn.add(units)
            attn.flush()

        attn_done = attn.flush()

        for ti in range(8):
            for fo in range(NCH):
                def mkev(e, ap, fo=fo, ti=ti):
                    return e.scalar_tensor_tensor(
                        stream[fo][:, GUARD + ti * TT:GUARD + (ti + 1) * TT],
                        ap, bias["bp"][:, fo:fo + 1],
                        stream[fo][:, GUARD + ti * TT:GUARD + (ti + 1) * TT],
                        ALU.add, ALU.add)
                proj.group([wp16[k][:, fo * 128:(fo + 1) * 128] for k in range(NCH)],
                           [yT[k][:, ti * TT:(ti + 1) * TT] for k in range(NCH)],
                           mkev,
                           extra_wait=[attn_done] if (ti == 0 and fo == 0) else ())
        resid_done = proj.drain()

        ln_bufs = dict(sq=sqb, psR=psR, srow=srow, qrow=qrow,
                       rstd=rstd, rstd16=rstd16, murs16=murs16,
                       onesc=onesc[:, :], ones1=ones1[:, :])
        ln_done = emit_ln(p, stream, ln_bufs, "1", pre_wait=[resid_done])

        if fz is None:
            for c in range(NCH):
                p.op("gpsimd", lambda e, c=c: e.dma_start(
                    out=dr["out"][c * 128:(c + 1) * 128, :],
                    in_=stream[c][:, GUARD:GUARD + NT]),
                    waits=[ln_done] if c == 0 else (), inc=sSt, inc_by=16)
            p.wait("gpsimd", sSt.w())
            p.alloc_sems(ctx)
            block = e(nc.Block())
            p.emit(block)
        else:
            for c in range(NCH):
                p.op("gpsimd", lambda e, c=c: e.dma_start(
                    out=fz["xa_scr"][c * 128:(c + 1) * 128, :],
                    in_=stream[c][:, GUARD:GUARD + NT]),
                    waits=[ln_done] if c == 0 else (), inc=sSt, inc_by=16)
            fz["xa_done"] = sSt.w()
            fz["prev"] = ln_done
    return nc



def emit_ln_stats(p, stream, bufs, sems, shifted, pre_wait, rstd16f, murs16f,
                  eps=1e-5):
    sq, psR = bufs["sq"], bufs["psR"]
    srow, qrow, rstd = bufs["srow"], bufs["qrow"], bufs["rstd"]
    onesc = bufs["onesc"]
    nch = len(stream)
    sSq = p.sem("lsSq")
    sMM = p.sem("lsMM")
    sVs = p.sem("lsVs")
    sSc = p.sem("lsSc")
    sRdy = p.sem("lsRdy")
    bMM, bVs = sMM.n, sVs.n

    def sview(c, t0, t1):
        if shifted and c >= nch // 2:
            return stream[c][:, GUARD - 1 + t0:GUARD - 1 + t1]
        return stream[c][:, GUARD + t0:GUARD + t1]

    for ti in range(NT // TT):
        t0, t1 = ti * TT, (ti + 1) * TT
        for c in range(nch):
            g = ti * nch + c
            sqb = sq[g % 2]
            waits = list(pre_wait) if g == 0 else []
            if g >= 2:
                waits.append((sMM, bMM + 2 * (g - 1)))

            def mkq(e, c=c, t0=t0, t1=t1, sqb=sqb):
                return e.tensor_tensor(sqb[:, :], sview(c, t0, t1), sview(c, t0, t1),
                                       ALU.mult)
            p.op("vector", mkq, waits=waits, inc=sSq)
            sq_done = sSq.w()
            waits2 = list(pre_wait) if g == 0 else []
            if c == 0 and ti >= 1:
                waits2.append((sVs, bVs + 7 * ti))

            def mmS(e, c=c, t0=t0, t1=t1):
                return e.matmul(psR[0][0:1, :], onesc, sview(c, t0, t1),
                                start=(c == 0), stop=(c == nch - 1))
            p.op("tensor", mmS, waits=waits2, inc=sMM)

            def mmQ(e, c=c, sqb=sqb):
                return e.matmul(psR[1][0:1, :], onesc, sqb[:, :],
                                start=(c == 0), stop=(c == nch - 1))
            p.op("tensor", mmQ, waits=[sq_done], inc=sMM)
        mm_done = sMM.w()
        p.op("vector", lambda e: e.tensor_copy(srow[:, :], psR[0][0:1, :]),
             waits=[mm_done], inc=sVs)
        p.op("vector", lambda e: e.tensor_copy(qrow[:, :], psR[1][0:1, :]), inc=sVs)
        p.op("vector", lambda e: e.tensor_scalar_mul(srow[:, :], srow[:, :], 1.0 / D),
             waits=[sVs.w()], inc=sVs)
        p.op("vector", lambda e: e.tensor_tensor(rstd[:, :], srow[:, :], srow[:, :],
                                                 ALU.mult), waits=[sVs.w()], inc=sVs)
        p.op("vector", lambda e: e.scalar_tensor_tensor(rstd[:, :], qrow[:, :], 1.0 / D,
                                                        rstd[:, :], ALU.mult,
                                                        ALU.subtract),
             waits=[sVs.w()], inc=sVs)
        p.op("vector", lambda e: e.tensor_scalar_add(rstd[:, :], rstd[:, :], eps),
             waits=[sVs.w()], inc=sVs)
        p.op("scalar", lambda e: e.sqrt(rstd[:, :], rstd[:, :]), waits=[sVs.w()],
             inc=sSc)
        p.op("vector", lambda e: e.reciprocal(rstd[:, :], rstd[:, :]),
             waits=[sSc.w()], inc=sVs)
        p.op("vector", lambda e, t0=t0, t1=t1: e.tensor_copy(rstd16f[:, t0:t1],
                                                             rstd[:, :]),
             waits=[sVs.w()], inc=sRdy)
        p.op("vector", lambda e, t0=t0, t1=t1: e.tensor_tensor(murs16f[:, t0:t1],
                                                               srow[:, :], rstd[:, :],
                                                               ALU.mult),
             inc=sRdy)
    return sRdy.w()


def emit_ln_apply_tile(p, stream, sems_d, ti, rstd16f, murs16f, psB, ones1,
                       out, affine, shifted, shadow=None, pre_wait=()):
    sBc, sAp = sems_d["sBc"], sems_d["sAp"]
    t0, t1 = ti * TT, (ti + 1) * TT
    nch = len(stream)

    def mb1(e):
        return e.matmul(psB[0][:, :], ones1, rstd16f[:, t0:t1], start=True, stop=True)
    p.op("tensor", mb1, waits=list(pre_wait) + ([(sAp, sAp.n)] if sAp.n else []),
         inc=sBc)

    def mb2(e):
        return e.matmul(psB[1][:, :], ones1, murs16f[:, t0:t1], start=True, stop=True)
    p.op("tensor", mb2, inc=sBc)
    bc_done = sBc.w()
    for c in range(nch):
        dsl = out[c][:, 0:TT]
        if shifted and c >= nch // 2:
            sv_main = stream[c][:, GUARD - 1 + t0 + 1:GUARD - 1 + t1]
            if ti == 0:
                sv_first = stream[c][:, GUARD - 1:GUARD]
            else:
                sh, col = shadow
                sv_first = sh[:, (c - nch // 2) * 8 + ti - 1:(c - nch // 2) * 8 + ti]
        else:
            sv_main = None
            sv_full = stream[c][:, GUARD + t0:GUARD + t1]
        if sv_main is None:
            def ap1(e, c=c, sv=sv_full, dsl=dsl):
                return e.tensor_tensor(dsl, sv, psB[0][:, :], ALU.mult)
            p.op("vector", ap1, waits=[bc_done] if c == 0 else ())
        else:
            def ap1a(e, sv=sv_first, dsl=dsl):
                return e.tensor_tensor(dsl[:, 0:1], sv, psB[0][:, 0:1], ALU.mult)
            p.op("vector", ap1a, waits=[bc_done] if c == 0 else ())

            def ap1b(e, sv=sv_main, dsl=dsl):
                return e.tensor_tensor(dsl[:, 1:TT], sv, psB[0][:, 1:TT], ALU.mult)
            p.op("vector", ap1b)
        last = (c == nch - 1)

        def ap2(e, dsl=dsl):
            return e.tensor_tensor(dsl, dsl, psB[1][:, :], ALU.subtract)
        p.op("vector", ap2, inc=sAp if (last and affine is None) else None)
        if affine is not None:
            g_col, b_col = affine(c)

            def ap3(e, dsl=dsl, g_col=g_col, b_col=b_col):
                return e.tensor_scalar(dsl, dsl, g_col, b_col, ALU.mult, ALU.add)
            p.op("vector", ap3, inc=sAp if last else None)
    return sAp.w()




def _rotary(p, proj, sVr, src, tb1, tb2, prot, bones2, sel2, nr, nr16, scr,
            c0, pw, qrdy, prev_done):
    sq16, u1 = scr
    u2 = sq16
    sv = src[:, c0:c0 + pw]

    def mksq(e, sv=sv, pw=pw):
        return e.tensor_tensor(sq16[:, 0:pw], sv, sv, ALU.mult)
    p.op("vector", mksq, waits=[prev_done], inc=sVr)
    sq_done = sVr.w()

    def evn(e, ap, pw=pw):
        return e.tensor_copy(nr[:, 0:pw], ap)
    proj.group([bones2[:, :]], [sq16[:, 0:pw]], evn, width=pw, rows=2,
               extra_wait=[sq_done])
    p.op("scalar", lambda e, pw=pw: e.sqrt(nr[:, 0:pw], nr[:, 0:pw]),
         waits=[proj.drain()], inc=sVr)
    p.op("vector", lambda e, pw=pw: e.tensor_scalar_max(nr[:, 0:pw], nr[:, 0:pw],
                                                        1e-12),
         waits=[sVr.w()], inc=sVr)
    p.op("vector", lambda e, pw=pw: e.reciprocal(nr[:, 0:pw], nr[:, 0:pw]),
         waits=[sVr.w()], inc=sVr)
    p.op("vector", lambda e, pw=pw: e.tensor_copy(nr16[:, 0:pw], nr[:, 0:pw]),
         waits=[sVr.w()], inc=sVr)
    nrm_ready = sVr.w()

    def mku1(e, sv=sv, pw=pw, c0=c0):
        return e.tensor_tensor(u1[:, 0:pw], sv, tb1[:, c0:c0 + pw], ALU.mult)
    p.op("vector", mku1, inc=sVr)

    def evu2(e, ap, pw=pw, c0=c0):
        return e.tensor_tensor(u2[:, 0:pw], ap, tb2[:, c0:c0 + pw], ALU.mult)
    proj.group([prot[:, :]], [sv], evu2, width=pw, extra_wait=[qrdy])

    def qpre(e, pw=pw):
        return e.tensor_tensor(u1[:, 0:pw], u1[:, 0:pw], u2[:, 0:pw], ALU.add)
    p.op("vector", qpre, waits=[proj.drain()], inc=sVr)
    qpre_done = sVr.w()

    def evrot(e, ap, sv=sv, pw=pw):
        return e.tensor_tensor(sv, u1[:, 0:pw], ap, ALU.mult)
    proj.group([sel2[:, :]], [nr16[:, 0:pw]], evrot, width=pw,
               extra_wait=[nrm_ready], evac_wait=[qpre_done])
    return proj.drain()


def build_stage2(layers, final_ln, skip_ffn=False, dbg=False, fz=None):
    NL = 6
    if fz is not None:
        nc, dr = _NcProxy(fz["nc"], "s2_"), fz["dr"]
        return _stage2_body(nc, dr, layers, final_ln, skip_ffn, dbg, fz)
    nc = bass.Bass("TRN2", target_bir_lowering=False)
    dr = {}
    if dbg:
        for nm in ("dbg_q", "dbg_k", "dbg_v"):
            dr[nm] = nc.dram_tensor(nm, [128, 1280], F32, kind="ExternalOutput").ap()
        dr["dbg_a"] = nc.dram_tensor("dbg_a", [128, 8192], F32, kind="ExternalOutput").ap()
    dr["xT"] = nc.dram_tensor("xT", [D, NT], F32, kind="ExternalInput").ap()
    dr["wqkv"] = nc.dram_tensor("wqkv", [NL * D, 3072], F32, kind="ExternalInput").ap()
    dr["wo"] = nc.dram_tensor("wo", [NL * D, D], F32, kind="ExternalInput").ap()
    dr["w1"] = nc.dram_tensor("w1", [NL * D, 5632], F32, kind="ExternalInput").ap()
    dr["w2"] = nc.dram_tensor("w2", [NL * 2816, D], F32, kind="ExternalInput").ap()
    dr["b1c"] = nc.dram_tensor("b1c", [NL * 128, 44], F32, kind="ExternalInput").ap()
    dr["b2c"] = nc.dram_tensor("b2c", [NL * 128, 8], F32, kind="ExternalInput").ap()
    dr["lngc"] = nc.dram_tensor("lngc", [NL * 128, 8], F32, kind="ExternalInput").ap()
    dr["lnbc"] = nc.dram_tensor("lnbc", [NL * 128, 8], F32, kind="ExternalInput").ap()
    dr["qsc"] = nc.dram_tensor("qsc", [NL * 128, 1], F32, kind="ExternalInput").ap()
    dr["ksc"] = nc.dram_tensor("ksc", [NL * 128, 1], F32, kind="ExternalInput").ap()
    dr["qss"] = nc.dram_tensor("qss", [NL * 128, 1], F32, kind="ExternalInput").ap()
    dr["kss"] = nc.dram_tensor("kss", [NL * 128, 1], F32, kind="ExternalInput").ap()
    for nm in ("tq1", "tq2", "tk1", "tk2"):
        dr[nm] = nc.dram_tensor(nm, [128, NT], F32, kind="ExternalInput").ap()
    for nm, sh in (("prot", [128, 128]), ("ident", [128, 128]), ("ones1", [1, 128]),
                   ("onesc", [128, 1]), ("bones2", [128, 2]), ("sel2", [2, 128]),
                   ("m_a", [128, 512]), ("m_b", [128, 512]), ("m0a", [128, 256]),
                   ("m0b", [128, 256])):
        dr[nm] = nc.dram_tensor(nm, sh, F32, kind="ExternalInput").ap()
    dr["out"] = nc.dram_tensor("outT", [D, NT], F32, kind="ExternalOutput").ap()
    return _stage2_body(nc, dr, layers, final_ln, skip_ffn, dbg, None)


def _stage2_body(nc, dr, layers, final_ln, skip_ffn, dbg, fz):
    NL = 6
    with ExitStack() as ctx:
        e = ctx.enter_context
        if fz is None:
            stream = [e(nc.sbuf_tensor(f"strm{c}", [128, GUARD + NT + 6], F16))
                      for c in range(NCH)]
        else:
            stream = fz["stream"]
        tb = {nm: e(nc.sbuf_tensor(nm + "_s", [128, 1344], BF16))
              for nm in ("tq1", "tq2", "tk1", "tk2")}
        qb = e(nc.sbuf_tensor("qb", [128, 1280], BF16))
        kb = e(nc.sbuf_tensor("kb", [128, 1280], BF16))
        vtok = e(nc.sbuf_tensor("vtok", [128, 1280], F16))
        scr = [e(nc.sbuf_tensor(f"scr{i}", [128, 512], BF16)) for i in range(2)]
        nr = e(nc.sbuf_tensor("nr", [2, 512], F32))
        nr16 = e(nc.sbuf_tensor("nr16", [2, 512], BF16))
        ptail = [e(nc.sbuf_tensor(f"ptail{c}", [128, 264], F16)) for c in range(NCH)]
        Pbuf = [e(nc.sbuf_tensor(f"Pbuf{i}", [128, 512], F32)) for i in range(2)]
        Pn = [e(nc.sbuf_tensor(f"Pn{i}", [128, 512], F16)) for i in range(2)]
        Pt = [e(nc.sbuf_tensor(f"Pt{i}", [128, 512], F16)) for i in range(2)]
        nds = e(nc.sbuf_tensor("nds", [128, 6], F32))
        identH = e(nc.sbuf_tensor("identH", [128, 128], F16))
        prot = e(nc.sbuf_tensor("prot_s", [128, 128], BF16))
        ones1 = e(nc.sbuf_tensor("ones1_s", [1, 128], F16))
        onesc = e(nc.sbuf_tensor("onesc_s", [128, 1], F16))
        bones2 = e(nc.sbuf_tensor("bones2_s", [128, 2], BF16))
        sel2 = e(nc.sbuf_tensor("sel2_s", [2, 128], BF16))
        masks = {"a": e(nc.sbuf_tensor("m_a_s", [128, 512], F16)),
                 "b": e(nc.sbuf_tensor("m_b_s", [128, 512], F16)),
                 "0a": e(nc.sbuf_tensor("m0a_s", [128, 256], F16)),
                 "0b": e(nc.sbuf_tensor("m0b_s", [128, 256], F16))}
        wsl = [e(nc.sbuf_tensor("wsl0", [128, 8 * 384], F16))]
        wosl = [e(nc.sbuf_tensor("wosl0", [128, 8 * 128], F16))]
        w1sl = [e(nc.sbuf_tensor(f"w1sl{i}", [128, 8 * 256], F16)) for i in range(2)]
        w2sl = [e(nc.sbuf_tensor(f"w2sl{i}", [128, 22 * 128], F16)) for i in range(2)]
        b1buf = e(nc.sbuf_tensor("b1buf", [128, 44], F32))
        b2buf = e(nc.sbuf_tensor("b2buf", [128, 8], F32))
        lng = e(nc.sbuf_tensor("lng", [128, 8], F32))
        lnb = e(nc.sbuf_tensor("lnb", [128, 8], F32))
        qscol = e(nc.sbuf_tensor("qscol", [128, 1], F32))
        kscol = e(nc.sbuf_tensor("kscol", [128, 1], F32))
        qscolS = e(nc.sbuf_tensor("qscolS", [128, 1], F32))
        kscolS = e(nc.sbuf_tensor("kscolS", [128, 1], F32))
        xn = [e(nc.sbuf_tensor(f"xn{c}", [128, TT], F16)) for c in range(NCH)]
        hbuf = e(nc.sbuf_tensor("hbuf", [128, 22 * TT], F16))

        def aout(c, x, y):
            return hbuf[:, c * 1024 + x:c * 1024 + y]
        ascr = [e(nc.sbuf_tensor(f"ascr{i}", [128, TT], F16)) for i in range(2)]
        gscr = [e(nc.sbuf_tensor(f"gscr{i}", [128, TT], F16)) for i in range(2)]
        sqb = ascr
        srow = e(nc.sbuf_tensor("srow", [1, TT], F32))
        qrow = e(nc.sbuf_tensor("qrow", [1, TT], F32))
        rstd = e(nc.sbuf_tensor("rstd", [1, TT], F32))
        rstd16f = e(nc.sbuf_tensor("rstd16f", [1, NT], F16))
        murs16f = e(nc.sbuf_tensor("murs16f", [1, NT], F16))
        shdw = e(nc.sbuf_tensor("shdw", [128, 32], F16))
        xtile = gscr

        psA = [e(nc.psum_tensor(f"psA{i}", [128, 512], F32)) for i in range(2)]
        psS = [e(nc.psum_tensor(f"psS{i}", [128, 512], F32)) for i in range(2)]
        psT = [e(nc.psum_tensor(f"psT{i}", [128, 512], F16)) for i in range(2)]
        psO = [e(nc.psum_tensor(f"psO{i}", [64, 128], F32)) for i in range(2)]

        p = fz["p"] if fz else P(nc)
        sLd = p.sem("sLd")
        sTbl = p.sem("sTbl")
        sW = p.sem("sW")
        sWFq = p.sem("sWFq")
        sWFo = p.sem("sWFo")
        sWF1 = p.sem("sWF1")
        sWF2 = p.sem("sWF2")
        sVr = p.sem("sVr")
        sPt = p.sem("sPtl")
        sGlu = p.sem("sGlu")
        sShdw = p.sem("sShdw")
        sSt = p.sem("sSt")
        proj = ProjPipe(p, psA, "proj")
        attn = AttnPipe(p, dict(psS=psS, psT=psT, psO=psO, Pbuf=Pbuf, Pn=Pn,
                                Pt=Pt, nds=nds), "at")
        ln_sbc = p.sem("lnSbc")
        ln_sap = p.sem("lnSap")

        if fz is None:
            for c in range(NCH):
                p.op("gpsimd", lambda e, c=c: e.dma_start(
                    out=stream[c][:, GUARD:GUARD + NT],
                    in_=dr["xT"][c * 128:(c + 1) * 128, :]), inc=sLd, inc_by=16)
        consts = [(identH, "ident"), (prot, "prot"),
                  (ones1, "ones1"), (onesc, "onesc"), (bones2, "bones2"),
                  (sel2, "sel2"), (masks["a"], "m_a"), (masks["b"], "m_b"),
                  (masks["0a"], "m0a"), (masks["0b"], "m0b")]
        for i, (sb, nm) in enumerate(consts):
            p.op("gpsimd", lambda e, sb=sb, nm=nm: e.dma_start(out=sb[:, :], in_=dr[nm]),
                 waits=[fz["prev"]] if (fz and i == 0) else (), inc=sLd, inc_by=16)
        all_loaded = sLd.w()
        if fz is None:
            for c in range(NCH):
                p.op("vector", lambda e, c=c: e.memset(stream[c][:, 0:GUARD], 0.0),
                     waits=[all_loaded] if c == 0 else ())

        def uview(c, tok, width, use_ptail, t0):
            if use_ptail:
                base = tok - (t0 - 257)
                if c >= 4:
                    return ptail[c][:, base - 1:base - 1 + width]
                return ptail[c][:, base:base + width]
            if c >= 4:
                return stream[c][:, GUARD - 1 + tok:GUARD - 1 + tok + width]
            return stream[c][:, GUARD + tok:GUARD + tok + width]

        wslab_g = wo_g = w1_g = w2_g = 0
        qkv_free, wo_free, w1_free, w2_free = [], [], [], []
        w1_done = None

        for li in layers:
            lw = [proj.drain(), ln_sap.w(), (sTbl, sTbl.n), (sGlu, sGlu.n)]
            if fz is not None:
                lw.append(fz["prev"])
            for i, (buf, nm) in enumerate(((qscol, "qsc"), (kscol, "ksc"),
                                           (qscolS, "qss"), (kscolS, "kss"),
                                           (b1buf, "b1c"), (b2buf, "b2c"),
                                           (lng, "lngc"), (lnb, "lnbc"))):
                p.op("sync", lambda e, buf=buf, nm=nm, li=li: e.dma_start(
                    out=buf[:, :], in_=dr[nm][li * 128:(li + 1) * 128, :]),
                    waits=lw if i == 0 else (), inc=sLd, inc_by=16)
            scal_ready = sLd.w()

            for ts in range(4):
                t0, t1 = ts * 1024, (ts + 1) * 1024
                blo = t0 - 256
                ptail_ready = sPt.w()
                plo = max(t0 - 256, 0)
                co = plo - blo
                spn = t1 - plo
                tbl_war = [proj.drain()]
                for nm, col in (("tq1", qscol), ("tq2", qscolS),
                                ("tk1", kscol), ("tk2", kscolS)):
                    p.op("gpsimd", lambda e, nm=nm, plo=plo, co=co, spn=spn: e.dma_start(
                        out=tb[nm][:, co:co + spn],
                        in_=dr[nm][:, plo:plo + spn]),
                        waits=tbl_war, inc=sTbl, inc_by=16)
                    tbl_war = []
                tbl_dma = sTbl.w()
                for i, (nm, col) in enumerate((("tq1", qscol), ("tq2", qscolS),
                                               ("tk1", kscol), ("tk2", kscolS))):
                    p.op("vector", lambda e, nm=nm, col=col, co=co, spn=spn:
                         e.tensor_scalar_mul(tb[nm][:, co:co + spn],
                                             tb[nm][:, co:co + spn], col[:, 0:1]),
                         waits=[tbl_dma, scal_ready] if i == 0 else (), inc=sTbl)
                tbl_ready = sTbl.w()
                for pr in range(8):
                    bufi = 0
                    waits = [qkv_free[-1]] if qkv_free else []
                    p.op("gpsimd", lambda e, li=li, pr=pr, bufi=bufi: e.dma_start(
                        out=wsl[bufi][:, :].rearrange("p (c n) -> p c n", c=NCH),
                        in_=dr["wqkv"][li * D:(li + 1) * D, pr * 384:(pr + 1) * 384]
                        .rearrange("(c p) n -> p c n", p=128)),
                        waits=waits, inc=sW, inc_by=16)
                    wslab_g += 1
                    slab_ready = sW.w()

                    spans = []
                    if ts > 0:
                        spans.append((t0 - 256, 256, True))
                    tok = t0
                    while tok < t1:
                        spans.append((tok, 512, False))
                        tok += 512
                    first = True
                    for which, dst in ((0, qb), (1, kb)):
                        for (tok, wdt, upt) in spans:
                            c0 = tok - blo

                            def mkev(e, ap, dst=dst, c0=c0, wdt=wdt):
                                return e.tensor_copy(dst[:, c0:c0 + wdt], ap)
                            ew = [slab_ready, tbl_ready, ptail_ready] if first else ()
                            first = False
                            proj.group(
                                [wsl[bufi][:, k * 384 + which * 128:
                                           k * 384 + which * 128 + 128]
                                 for k in range(NCH)],
                                [uview(k, tok, wdt, upt, t0) for k in range(NCH)],
                                mkev, width=wdt, extra_wait=ew)
                    for (tok, wdt, upt) in spans:
                        for sub in range(wdt // 128):
                            tk = tok + sub * 128
                            c0 = tk - blo

                            def mkev(e, ap, c0=c0):
                                return e.tensor_copy(vtok[:, c0:c0 + 128], ap)
                            proj.group(
                                [uview(k, tk, 128, upt, t0) for k in range(NCH)],
                                [wsl[bufi][:, k * 384 + 256:k * 384 + 384]
                                 for k in range(NCH)],
                                mkev, width=128)
                    qkv_free.append(proj.sMM.w())
                    qrdy = proj.drain()

                    prev = qrdy
                    for sb_, t1_, t2_ in ((qb, tb["tq1"], tb["tq2"]),
                                          (kb, tb["tk1"], tb["tk2"])):
                        for (tok, wdt, upt) in spans:
                            c0 = tok - blo
                            prev = _rotary(p, proj, sVr, sb_, t1_, t2_, prot,
                                           bones2, sel2, nr, nr16, scr, c0, wdt,
                                           qrdy, prev)
                    rot_done = proj.drain()
                    if dbg and li == layers[0] and ts == 0 and pr == 0:
                        for nm, buf in (("dbg_q", qb), ("dbg_k", kb), ("dbg_v", vtok)):
                            p.op("gpsimd", lambda e, nm=nm, buf=buf: e.dma_start(
                                out=dr[nm], in_=buf[:, :]),
                                waits=[rot_done], inc=sSt, inc_by=16)

                    units = []
                    for hl in range(2):
                        for ibl in range(8):
                            ib = ts * 8 + ibl
                            w = ib // 2
                            if w == 0:
                                jn, j0g = 256, 0
                                mk = masks["0a"] if ib % 2 == 0 else masks["0b"]
                            else:
                                jn, j0g = 512, (w - 1) * 256
                                mk = masks["a"] if ib % 2 == 0 else masks["b"]
                            qc0 = ib * 128 - blo
                            jc0 = j0g - blo
                            u = dict(
                                jn=jn, nj=jn // 128, ident=identH[:, :],
                                s_lhsT=qb[hl * 64:(hl + 1) * 64, qc0:qc0 + 128],
                                s_rhs=kb[hl * 64:(hl + 1) * 64, jc0:jc0 + jn],
                                m_lhsT=identH[:, :], m_rhs=mk[:, 0:jn],
                                pv_lhsT=(lambda c, jc0=jc0, hl=hl:
                                         vtok[:, jc0 + c * 128 + hl * 64:
                                              jc0 + c * 128 + hl * 64 + 64]),
                                out_dst=aout(pr, ibl * 128, (ibl + 1) * 128)
                                [hl * 64:hl * 64 + 64, :],
                            )
                            if hl == 0 and ibl <= 1:
                                u["wait"] = [rot_done]
                            units.append(u)
                    attn.add(units)
                    attn.flush()

                a_done = attn.sEo.w()
                if dbg and li == layers[0] and ts == 0:
                    p.op("gpsimd", lambda e: e.dma_start(
                        out=dr["dbg_a"], in_=hbuf[:, 0:8192]),
                        waits=[a_done], inc=sSt, inc_by=16)
                if ts < 3:
                    for c in range(NCH):
                        p.op("vector", lambda e, c=c, t1=t1: e.tensor_copy(
                            ptail[c][:, 0:257],
                            stream[c][:, GUARD + t1 - 257:GUARD + t1]),
                            waits=[proj.drain(), a_done] if c == 0 else (),
                            inc=sPt if c == NCH - 1 else None)

                for fo in range(NCH):
                    bufi = 0
                    waits = [wo_free[-1]] if wo_free else []
                    p.op("gpsimd", lambda e, li=li, fo=fo, bufi=bufi: e.dma_start(
                        out=wosl[bufi][:, :].rearrange("p (c n) -> p c n", c=NCH),
                        in_=dr["wo"][li * D:(li + 1) * D, fo * 128:(fo + 1) * 128]
                        .rearrange("(c p) n -> p c n", p=128)),
                        waits=waits, inc=sW, inc_by=16)
                    wo_g += 1
                    wo_ready = sW.w()
                    for tsub in range(2):
                        tt0 = t0 + tsub * 512

                        def mkev(e, ap, fo=fo, tt0=tt0):
                            return e.tensor_tensor(
                                stream[fo][:, GUARD + tt0:GUARD + tt0 + 512], ap,
                                stream[fo][:, GUARD + tt0:GUARD + tt0 + 512],
                                ALU.add)
                        proj.group(
                            [wosl[bufi][:, k * 128:(k + 1) * 128] for k in range(NCH)],
                            [aout(k, tsub * 512, (tsub + 1) * 512)
                             for k in range(NCH)],
                            mkev,
                            extra_wait=[wo_ready, a_done])
                    wo_free.append(proj.sMM.w())
            resid_done = proj.drain()
            if skip_ffn:
                continue

            ln_bufs = dict(sq=sqb, psR=psS, srow=srow, qrow=qrow, rstd=rstd,
                           onesc=onesc[:, :])
            stats_done = emit_ln_stats(p, stream, ln_bufs, "", shifted=True,
                                 pre_wait=[resid_done, attn.sEx.w(),
                                           attn.sEo.w(), attn.sTe.w(),
                                           attn.sPV.w(), attn.sPn.w(),
                                           attn.sRm.w(), attn.sRc.w()],
                                 rstd16f=rstd16f, murs16f=murs16f)
            for c in range(4, NCH):
                src = stream[c][:, GUARD:GUARD + NT].rearrange(
                    "p (t s) -> p t s", s=TT)[:, 0:7, TT - 1:TT]
                dst = shdw[:, (c - 4) * 8:(c - 4) * 8 + 7].rearrange(
                    "p (t s) -> p t s", s=1)
                p.op("vector", lambda e, src=src, dst=dst: e.tensor_copy(dst, src),
                     waits=[resid_done] if c == 4 else (),
                     inc=sShdw if c == NCH - 1 else None)
            shdw_done = sShdw.w()

            for tf in range(8):
                tf0 = tf * TT
                af = (lambda c: (lng[:, c:c + 1], lnb[:, c:c + 1]))
                pw = [stats_done, shdw_done]
                if w1_done is not None:
                    pw.append(w1_done)
                ap_done = emit_ln_apply_tile(
                    p, stream, dict(sBc=ln_sbc, sAp=ln_sap), tf, rstd16f, murs16f,
                    psS, ones1[:, :], xn, af, shifted=True, shadow=(shdw, 0),
                    pre_wait=pw)
                for cc in range(22):
                    bufi = w1_g % 2
                    waits = [w1_free[w1_g - 2]] if w1_g >= 2 else []
                    p.op("gpsimd", lambda e, li=li, cc=cc, bufi=bufi: e.dma_start(
                        out=w1sl[bufi][:, :].rearrange("p (c n) -> p c n", c=NCH),
                        in_=dr["w1"][li * D:(li + 1) * D, cc * 256:(cc + 1) * 256]
                        .rearrange("(c p) n -> p c n", p=128)),
                        waits=waits, inc=sW, inc_by=16)
                    w1_g += 1
                    w1_ready = sW.w()
                    sl = cc % 2
                    gwait = [(sGlu, sGlu.n - 1)] if sGlu.n >= 2 else []

                    def eva(e, ap, sl=sl, cc=cc):
                        return e.tensor_scalar(ascr[sl][:, :], ap, b1buf[:, cc:cc + 1],
                                               None, ALU.add)
                    proj.group([w1sl[bufi][:, k * 256:k * 256 + 128]
                                for k in range(NCH)],
                               [xn[k][:, :] for k in range(NCH)],
                               eva,
                               extra_wait=[w1_ready, ap_done],
                               evac_wait=gwait)

                    def evg(e, ap, sl=sl, cc=cc):
                        return e.activation(gscr[sl][:, :], ap, AF.Gelu,
                                            bias=b1buf[:, 22 + cc:23 + cc], scale=1.0)
                    proj.group([w1sl[bufi][:, k * 256 + 128:k * 256 + 256]
                                for k in range(NCH)],
                               [xn[k][:, :] for k in range(NCH)],
                               evg, evac_eng="scalar",
                               extra_wait=[ap_done], evac_wait=gwait)
                    w1_free.append(proj.sMM.w())
                    ab_done = proj.drain()

                    def glu(e, sl=sl, cc=cc):
                        return e.tensor_tensor(hbuf[:, cc * TT:(cc + 1) * TT],
                                               ascr[sl][:, :], gscr[sl][:, :],
                                               ALU.mult)
                    p.op("vector", glu, waits=[ab_done], inc=sGlu)
                w1_done = (proj.sMM, proj.sMM.n)
                glu_done = sGlu.w()

                for fo in range(NCH):
                    bufi = w2_g % 2
                    waits = [w2_free[w2_g - 2]] if w2_g >= 2 else []
                    p.op("gpsimd", lambda e, li=li, fo=fo, bufi=bufi: e.dma_start(
                        out=w2sl[bufi][:, :].rearrange("p (c n) -> p c n", c=22),
                        in_=dr["w2"][li * 2816:(li + 1) * 2816,
                                     fo * 128:(fo + 1) * 128]
                        .rearrange("(c p) n -> p c n", p=128)),
                        waits=waits, inc=sW, inc_by=16)
                    w2_g += 1
                    w2_ready = sW.w()

                    def mkev(e, ap, fo=fo, tf0=tf0):
                        return e.scalar_tensor_tensor(
                            stream[fo][:, GUARD + tf0:GUARD + tf0 + TT],
                            ap, b2buf[:, fo:fo + 1],
                            stream[fo][:, GUARD + tf0:GUARD + tf0 + TT],
                            ALU.add, ALU.add)
                    proj.group([w2sl[bufi][:, k * 128:(k + 1) * 128]
                                for k in range(22)],
                               [hbuf[:, k * TT:(k + 1) * TT] for k in range(22)],
                               mkev,
                               extra_wait=[w2_ready, glu_done])
                    w2_free.append(proj.sMM.w())
        stream_done = proj.drain()

        if final_ln:
            sX = p.sem("sX")
            sXa = p.sem("sXa")
            for c in range(NCH):
                for ti in range(8):
                    g = c * 8 + ti
                    waits = [stream_done] if g == 0 else []
                    if g >= 2:
                        waits.append(sXa.w(g - 1))
                    if fz is not None and g == 0:
                        waits.append(fz["xa_done"])
                    p.op("gpsimd", lambda e, c=c, ti=ti, g=g: e.dma_start(
                        out=xtile[g % 2][:, :],
                        in_=dr["xa" if fz else "xT"][c * 128:(c + 1) * 128,
                                                    ti * TT:(ti + 1) * TT]),
                        waits=waits, inc=sX, inc_by=16)

                    def addx(e, c=c, ti=ti, g=g):
                        return e.tensor_tensor(
                            stream[c][:, GUARD + ti * TT:GUARD + (ti + 1) * TT],
                            stream[c][:, GUARD + ti * TT:GUARD + (ti + 1) * TT],
                            xtile[g % 2][:, :], ALU.add)
                    p.op("vector", addx, waits=[sX.w()], inc=sXa)
            add_done = sXa.w()
            ln_bufs = dict(sq=sqb, psR=psS, srow=srow, qrow=qrow, rstd=rstd,
                           rstd16=rstd16f[:, 0:TT], murs16=murs16f[:, 0:TT],
                           onesc=onesc[:, :], ones1=ones1[:, :])
            ln_done = emit_ln(p, stream, ln_bufs, "fin", pre_wait=[add_done])
        else:
            ln_done = stream_done

        if fz is None:
            for c in range(NCH):
                p.op("gpsimd", lambda e, c=c: e.dma_start(
                    out=dr["out"][c * 128:(c + 1) * 128, :],
                    in_=stream[c][:, GUARD:GUARD + NT]),
                    waits=[ln_done] if c == 0 else (), inc=sSt, inc_by=16)
            p.wait("gpsimd", sSt.w())
            p.alloc_sems(ctx)
            block = e(nc.Block())
            p.emit(block)
        else:
            fz["prev"] = ln_done
    return nc


def _local_masks():
    ii = np.arange(128)[:, None]
    jj = np.arange(512)[None, :]
    m_a = np.where((jj >= ii) & (jj <= ii + 256), 0.0, MASKVAL)
    m_b = np.where((jj >= ii + 128) & (jj <= ii + 384), 0.0, MASKVAL)
    jj2 = np.arange(256)[None, :]
    m0a = np.where(jj2 <= ii, 0.0, MASKVAL)
    m0b = np.where(jj2 <= ii + 128, 0.0, MASKVAL)
    return (m_a.astype(np.float32), m_b.astype(np.float32),
            m0a.astype(np.float32), m0b.astype(np.float32))


def _rot_prot():
    M = np.zeros((128, 128), np.float32)
    for h in range(2):
        o = h * 64
        for dd in range(32):
            M[o + dd, o + 32 + dd] = -1.0
            M[o + 32 + dd, o + dd] = 1.0
    return np.ascontiguousarray(M.T)


def _rot_tables():
    DIM_HEAD, WINDOW = 64, 256
    half = np.arange(0, DIM_HEAD, 2, dtype=np.float64)
    inv_freq = 1.0 / 10000.0 ** (half / DIM_HEAD)
    t = np.arange(NT, dtype=np.float64)
    freqs = t[:, None] * inv_freq[None, :]
    freqs = np.concatenate([freqs, freqs], -1)
    base_scale = (half + 0.4 * DIM_HEAD) / (1.4 * DIM_HEAD)
    power = (t - NT // 2) / (WINDOW // 2)
    scale = base_scale[None, :] ** power[:, None]
    scale = np.concatenate([scale, scale], -1)
    r8 = np.sqrt(8.0)

    def dup(x):
        xt = (x.T * r8).astype(np.float32)
        return np.concatenate([xt, xt], axis=0)
    return (dup(np.cos(freqs) * scale), dup(np.sin(freqs) * scale),
            dup(np.cos(freqs) / scale), dup(np.sin(freqs) / scale))


def stage2_inputs(inputs, b, xA=None):
    f = np.asarray
    NL = 6
    wqkv = f(inputs["L0_ffn_Wqkv"])
    wqkv_r2 = np.zeros((NL * D, 3072), np.float32)
    for li in range(NL):
        for pr in range(8):
            wqkv_r2[li * D:(li + 1) * D, pr * 384:pr * 384 + 128] = wqkv[li][:, pr * 128:(pr + 1) * 128]
            wqkv_r2[li * D:(li + 1) * D, pr * 384 + 128:pr * 384 + 256] = wqkv[li][:, 1024 + pr * 128:1024 + (pr + 1) * 128]
            wqkv_r2[li * D:(li + 1) * D, pr * 384 + 256:pr * 384 + 384] = wqkv[li][:, 2048 + pr * 128:2048 + (pr + 1) * 128]

    w1 = f(inputs["L0_ffn_W1"])
    w2 = f(inputs["L0_ffn_W2"])
    w1p = np.zeros((NL, D, 5632), np.float32)
    w2p = np.zeros((NL, 2816, D), np.float32)
    b1 = f(inputs["L0_ffn_b1"])
    b1p = np.zeros((NL, 128, 44), np.float32)
    for li in range(NL):
        a = w1[li][:, 0:2730]
        g = w1[li][:, 2730:5460]
        ap = np.zeros((D, 2816), np.float32)
        gp = np.zeros((D, 2816), np.float32)
        ap[:, 0:2730] = a
        gp[:, 0:2730] = g
        for cc in range(22):
            w1p[li][:, cc * 256:cc * 256 + 128] = ap[:, cc * 128:(cc + 1) * 128]
            w1p[li][:, cc * 256 + 128:cc * 256 + 256] = gp[:, cc * 128:(cc + 1) * 128]
            b1a = np.zeros(2816, np.float32)
            b1g = np.zeros(2816, np.float32)
            b1a[0:2730] = b1[li][0:2730]
            b1g[0:2730] = b1[li][2730:5460]
            b1p[li][:, cc] = b1a[cc * 128:(cc + 1) * 128]
            b1p[li][:, 22 + cc] = b1g[cc * 128:(cc + 1) * 128]
        w2p[li][0:2730, :] = w2[li]
    tq1, tq2, tk1, tk2 = _rot_tables()
    m_a, m_b, m0a, m0b = _local_masks()
    qs = f(inputs["L0_ffn_q_scale"])
    ks = f(inputs["L0_ffn_k_scale"])
    qsc = np.concatenate([qs, qs], axis=1).reshape(NL * 128, 1)
    ksc = np.concatenate([ks, ks], axis=1).reshape(NL * 128, 1)
    qs_sw = np.concatenate([qs[:, 32:64], qs[:, 0:32]], axis=1)
    ks_sw = np.concatenate([ks[:, 32:64], ks[:, 0:32]], axis=1)
    qss = np.concatenate([qs_sw, qs_sw], axis=1).reshape(NL * 128, 1)
    kss = np.concatenate([ks_sw, ks_sw], axis=1).reshape(NL * 128, 1)
    d = {
        "xT": (np.ascontiguousarray(f(inputs["L0_attn_x"][b]).T) if xA is None
               else np.ascontiguousarray(xA[b].T)),
        "wqkv": wqkv_r2,
        "wo": f(inputs["L0_ffn_Wo"]).reshape(NL * D, D),
        "w1": w1p.reshape(NL * D, 5632),
        "w2": w2p.reshape(NL * 2816, D),
        "b1c": b1p.reshape(NL * 128, 44),
        "b2c": f(inputs["L0_ffn_b2"]).reshape(NL, 8, 128).transpose(0, 2, 1).reshape(NL * 128, 8),
        "lngc": f(inputs["L0_ffn_ln_g"]).reshape(NL, 8, 128).transpose(0, 2, 1).reshape(NL * 128, 8),
        "lnbc": f(inputs["L0_ffn_ln_b"]).reshape(NL, 8, 128).transpose(0, 2, 1).reshape(NL * 128, 8),
        "qsc": qsc, "ksc": ksc, "qss": qss, "kss": kss,
        "tq1": tq1, "tq2": tq2, "tk1": tk1, "tk2": tk2,
        "prot": _rot_prot(),
        "ident": np.eye(128, dtype=np.float32),
        "ones1": np.ones((1, 128), np.float32),
        "onesc": np.ones((128, 1), np.float32),
        "bones2": np.concatenate([np.tile([[1.0, 0.0]], (64, 1)),
                                  np.tile([[0.0, 1.0]], (64, 1))]).astype(np.float32),
        "sel2": np.concatenate([np.tile([[1.0]], (1, 64)) if False else
                                np.hstack([np.ones((1, 64)), np.zeros((1, 64))]),
                                np.hstack([np.zeros((1, 64)), np.ones((1, 64))])]
                               ).astype(np.float32),
        "m_a": m_a, "m_b": m_b, "m0a": m0a, "m0b": m0b,
    }
    return {k: np.ascontiguousarray(np.asarray(v, dtype=np.float32))
            for k, v in d.items()}



def build_stage3(fz=None):
    if fz is None:
        nc = bass.Bass("TRN2", target_bir_lowering=False)
        dr = {}
        dr["xT"] = nc.dram_tensor("xT", [D, NT], F32, kind="ExternalInput").ap()
        for nm in ("qr", "kr", "vr", "wz"):
            dr[nm] = nc.dram_tensor(nm, [D, D], F32, kind="ExternalInput").ap()
        dr["ident"] = nc.dram_tensor("ident", [128, 128], F32, kind="ExternalInput").ap()
        dr["ones1"] = nc.dram_tensor("ones1", [1, 128], F32, kind="ExternalInput").ap()
        dr["onesc"] = nc.dram_tensor("onesc", [128, 1], F32, kind="ExternalInput").ap()
        for k in range(4):
            dr[f"md{k}"] = nc.dram_tensor(f"md{k}", [128, 512], F32, kind="ExternalInput").ap()
        dr["out"] = nc.dram_tensor("outT", [D, NT], F32, kind="ExternalOutput").ap()
    else:
        nc, dr = _NcProxy(fz["nc"], "s3_"), fz["dr"]

    with ExitStack() as ctx:
        e = ctx.enter_context
        if fz is None:
            stream = [e(nc.sbuf_tensor(f"strm{c}", [128, GUARD + NT + 6], F16))
                      for c in range(NCH)]
        else:
            stream = fz["stream"]
        yT = [e(nc.sbuf_tensor(f"yT{c}", [128, NT], F16)) for c in range(NCH)]
        qT = e(nc.sbuf_tensor("qT", [128, NT], F16))
        kT = e(nc.sbuf_tensor("kT", [128, NT], F16))
        vtok = e(nc.sbuf_tensor("vtok", [128, NT], F16))
        wz16 = [e(nc.sbuf_tensor(f"wz16_{c}", [128, D], F16)) for c in range(NCH)]
        wsl = [[e(nc.sbuf_tensor(f"wsl{k}_{i}", [128, 3 * 128], F16)) for i in range(2)]
               for k in range(NCH)]
        Pt = [e(nc.sbuf_tensor(f"Pt{i}", [128, TT], F16)) for i in range(2)]
        mdiag = [e(nc.sbuf_tensor(f"md{k}_s", [128, 512], F16)) for k in range(4)]
        ident = e(nc.sbuf_tensor("ident_s", [128, 128], F16))
        ones1 = e(nc.sbuf_tensor("ones1_s", [1, 128], F16))
        onesc = e(nc.sbuf_tensor("onesc_s", [128, 1], F16))
        zcol = e(nc.sbuf_tensor("zcol", [128, 1], F32))
        ysc = e(nc.sbuf_tensor("ysc", [64, TT], F32))
        dnr = e(nc.sbuf_tensor("dnr", [1, TT], F32))
        dnr16 = e(nc.sbuf_tensor("dnr16", [1, TT], F16))
        sqb = [e(nc.sbuf_tensor(f"sqb{i}", [128, TT], F16)) for i in range(2)]
        srow = e(nc.sbuf_tensor("srow", [1, TT], F32))
        qrow = e(nc.sbuf_tensor("qrow", [1, TT], F32))
        rstd = e(nc.sbuf_tensor("rstd", [1, TT], F32))
        rstd16 = e(nc.sbuf_tensor("rstd16", [1, TT], F16))
        murs16 = e(nc.sbuf_tensor("murs16", [1, TT], F16))

        psA = [e(nc.psum_tensor(f"psA{i}", [128, 512], F32)) for i in range(2)]
        psS = [e(nc.psum_tensor(f"psS{i}", [128, 512], F32)) for i in range(2)]
        psD = e(nc.psum_tensor("psD", [1, 512], F32))
        psY = e(nc.psum_tensor("psY", [64, 512], F32))
        psB = e(nc.psum_tensor("psB", [64, 512], F32))
        psR = psS

        p = fz["p"] if fz else P(nc)
        sLd = p.sem("sLd")
        sSlab = p.sem("sSlab")
        sEx = p.sem("sEx")
        sMk = p.sem("sMk")
        sFl = p.sem("sFl")
        sSc2 = p.sem("sSc2")
        sDen = p.sem("sDen")
        sSt = p.sem("sSt")
        proj = ProjPipe(p, psA, "proj")

        if fz is None:
            for c in range(NCH):
                p.op("gpsimd", lambda e, c=c: e.dma_start(
                    out=stream[c][:, GUARD:GUARD + NT],
                    in_=dr["xT"][c * 128:(c + 1) * 128, :]), inc=sLd, inc_by=16)
        consts = [(ident, "ident"), (ones1, "ones1"), (onesc, "onesc")] + \
                 [(mdiag[k], f"md{k}") for k in range(4)]
        for i, (sb, nm) in enumerate(consts):
            p.op("gpsimd", lambda e, sb=sb, nm=nm: e.dma_start(out=sb[:, :], in_=dr[nm]),
                 waits=[fz["prev"]] if (fz and i == 0) else (), inc=sLd, inc_by=16)
        for c in range(NCH):
            p.op("gpsimd", lambda e, c=c: e.dma_start(
                out=wz16[c][:, :], in_=dr["wz"][c * 128:(c + 1) * 128, :]),
                inc=sLd, inc_by=16)
        all_loaded = sLd.w()
        p.op("vector", lambda e: e.memset(zcol[:, :], 0.0), waits=[all_loaded])

        slab_free = []
        ptfree_g = {0: 0, 1: 0}
        for pr in range(8):
            bufi = pr % 2
            for ck in range(NCH):
                for wi, w in enumerate(("qr", "kr", "vr")):
                    waits = []
                    if pr >= 2 and wi == 0 and ck == 0:
                        waits.append(slab_free[pr - 2])
                    p.op("gpsimd", lambda e, w=w, pr=pr, ck=ck, wi=wi, bufi=bufi: e.dma_start(
                        out=wsl[ck][bufi][:, wi * 128:(wi + 1) * 128],
                        in_=dr[w][ck * 128:(ck + 1) * 128, pr * 128:(pr + 1) * 128]),
                        waits=waits, inc=sSlab, inc_by=16)
            slab_ready = sSlab.w()

            for which, dst in ((0, qT), (1, kT)):
                for ti in range(8):
                    t0, t1 = GUARD + ti * TT, GUARD + (ti + 1) * TT

                    def mkev(e, ap, dst=dst, ti=ti):
                        return e.tensor_copy(dst[:, ti * TT:(ti + 1) * TT], ap)
                    proj.group([wsl[k][bufi][:, which * 128:which * 128 + 128]
                                for k in range(NCH)],
                               [stream[k][:, t0:t1] for k in range(NCH)],
                               mkev,
                               extra_wait=[slab_ready, all_loaded]
                               if (which == 0 and ti == 0) else ())
            for tc in range(32):
                def mkev(e, ap, tc=tc):
                    return e.tensor_copy(vtok[:, tc * 128:(tc + 1) * 128], ap)
                proj.group([stream[k][:, GUARD + tc * 128:GUARD + (tc + 1) * 128]
                            for k in range(NCH)],
                           [wsl[k][bufi][:, 256:384] for k in range(NCH)],
                           mkev, width=128)
            slab_free.append(proj.sMM.w())
            pv_ready = proj.drain()

            ptfree = ptfree_g
            for hl in range(2):
                qrows = qT[hl * 64:(hl + 1) * 64, :]
                krows = kT[hl * 64:(hl + 1) * 64, :]
                for ti in range(8):
                    i0 = ti * TT
                    njc = 4 * ti + 4
                    for jc in range(njc):
                        g = jc % 2
                        si = psS[g]
                        diag = jc >= 4 * ti
                        waits = [pv_ready] if (hl == 0 and ti == 0 and jc <= 1) else []
                        if jc == 0:
                            waits += [(sFl, sFl.n), (sSc2, sSc2.n)]
                        if sEx.n >= 2:
                            waits.append((sEx, sEx.n - 1))

                        def mmS(e, si=si, jc=jc, qrows=qrows, krows=krows, i0=i0):
                            return e.matmul(si[:, :], krows[:, jc * 128:(jc + 1) * 128],
                                            qrows[:, i0:i0 + TT], start=True, stop=True)
                        p.op("tensor", mmS, waits=waits, inc=sDen)
                        s_done = sDen.w()

                        def ex(e, si=si, g=g):
                            return e.activation(Pt[g][:, :], si[:, :], AF.Exp,
                                                bias=zcol[:, 0:1], scale=1.0 / 1024.0)
                        ex_waits = [s_done]
                        if ptfree[g]:
                            ex_waits.append((sDen, ptfree[g]))
                        p.op("scalar", ex, waits=ex_waits, inc=sEx)
                        exd = sEx.w()
                        if diag:
                            k4 = jc - 4 * ti

                            def msk(e, g=g, k4=k4):
                                return e.tensor_tensor(Pt[g][:, :], Pt[g][:, :],
                                                       mdiag[k4][:, :], ALU.mult)
                            p.op("vector", msk, waits=[exd], inc=sMk)
                            pt_ready = (sMk, sMk.n)
                        else:
                            pt_ready = (sEx, sEx.n)

                        def mmD(e, g=g, first=(jc == 0), last=(jc == njc - 1)):
                            return e.matmul(psD[:, :], onesc[:, :], Pt[g][:, :],
                                            start=first, stop=last)
                        p.op("tensor", mmD, waits=[pt_ready], inc=sDen)

                        def mmY(e, g=g, jc=jc, hl=hl, first=(jc == 0),
                                last=(jc == njc - 1)):
                            return e.matmul(psY[:, :],
                                            vtok[:, jc * 128 + hl * 64:
                                                 jc * 128 + hl * 64 + 64],
                                            Pt[g][:, :], start=first, stop=last)
                        p.op("tensor", mmY, inc=sDen)
                        ptfree[g] = sDen.n
                    acc_done = sDen.w()

                    def evd(e):
                        return e.tensor_copy(dnr[:, :], psD[0:1, :])
                    p.op("vector", evd, waits=[acc_done], inc=sFl)
                    p.op("vector", lambda e: e.reciprocal(dnr[:, :], dnr[:, :]),
                         waits=[sFl.w()], inc=sFl)
                    p.op("vector", lambda e: e.tensor_copy(dnr16[:, :], dnr[:, :]),
                         waits=[sFl.w()], inc=sFl)
                    dn_done = sFl.w()

                    def evy(e):
                        return e.copy(ysc[:, :], psY[0:64, :])
                    p.op("scalar", evy, waits=[acc_done, (sFl, sFl.n)], inc=sSc2)

                    def mmB(e):
                        return e.matmul(psB[0:64, :], ones1[:, 0:64], dnr16[:, :],
                                        start=True, stop=True)
                    p.op("tensor", mmB, waits=[dn_done], inc=sDen)
                    bc_done = sDen.w()

                    def evn(e, pr=pr, hl=hl, i0=i0):
                        return e.tensor_tensor(
                            yT[pr][hl * 64:hl * 64 + 64, i0:i0 + TT],
                            ysc[:, :], psB[0:64, :], ALU.mult)
                    p.op("vector", evn, waits=[bc_done, (sSc2, sSc2.n)], inc=sFl)

        attn_done = sFl.w()

        for ti in range(8):
            for fo in range(NCH):
                def mkev(e, ap, fo=fo, ti=ti):
                    return e.scalar_tensor_tensor(
                        stream[fo][:, GUARD + ti * TT:GUARD + (ti + 1) * TT],
                        ap, 1.0 / 16.0,
                        stream[fo][:, GUARD + ti * TT:GUARD + (ti + 1) * TT],
                        ALU.mult, ALU.add)
                proj.group([wz16[k][:, fo * 128:(fo + 1) * 128] for k in range(NCH)],
                           [yT[k][:, ti * TT:(ti + 1) * TT] for k in range(NCH)],
                           mkev,
                           extra_wait=[attn_done] if (ti == 0 and fo == 0) else ())
        resid_done = proj.drain()

        ln_bufs = dict(sq=sqb, psR=psR, srow=srow, qrow=qrow, rstd=rstd,
                       rstd16=rstd16, murs16=murs16,
                       onesc=onesc[:, :], ones1=ones1[:, :])
        ln_done = emit_ln(p, stream, ln_bufs, "3", pre_wait=[resid_done])

        if fz is None:
            for c in range(NCH):
                p.op("gpsimd", lambda e, c=c: e.dma_start(
                    out=dr["out"][c * 128:(c + 1) * 128, :],
                    in_=stream[c][:, GUARD:GUARD + NT]),
                    waits=[ln_done] if c == 0 else (), inc=sSt, inc_by=16)
            p.wait("gpsimd", sSt.w())
            p.alloc_sems(ctx)
            block = e(nc.Block())
            p.emit(block)
        else:
            fz["prev"] = ln_done
    return nc


def stage3_inputs(inputs, b, xB=None):
    f = np.asarray
    Q = f(inputs["L1_attn_Q"])
    Kw = f(inputs["L1_attn_K"])
    V = f(inputs["L1_attn_V"])
    W = f(inputs["L1_attn_W"])
    ii = np.arange(128)[:, None]
    jj = ii
    md = []
    for k4 in range(4):
        m = (np.arange(512)[None, :] >= (np.arange(128)[:, None] + 128 * k4))
        md.append(m.astype(np.float32))
    d = {
        "xT": np.ascontiguousarray(xB[b].T),
        "qr": np.ascontiguousarray(Q.transpose(1, 2, 0).reshape(D, D)),
        "kr": np.ascontiguousarray(Kw.transpose(1, 2, 0).reshape(D, D)),
        "vr": np.ascontiguousarray(V.transpose(1, 2, 0).reshape(D, D)),
        "wz": np.ascontiguousarray(W.transpose(2, 1, 0).reshape(D, D)),
        "ident": np.eye(128, dtype=np.float32),
        "ones1": np.ones((1, 128), np.float32),
        "onesc": np.ones((128, 1), np.float32),
        "md0": md[0], "md1": md[1], "md2": md[2], "md3": md[3],
    }
    return {k: np.ascontiguousarray(np.asarray(v, dtype=np.float32))
            for k, v in d.items()}



def build_stage4(fz=None):
    if fz is None:
        nc = bass.Bass("TRN2", target_bir_lowering=False)
        dr = {}
        dr["xT"] = nc.dram_tensor("xT", [D, NT], F32, kind="ExternalInput").ap()
        dr["w1"] = nc.dram_tensor("w1", [8 * D, 4096], F32, kind="ExternalInput").ap()
        dr["w2"] = nc.dram_tensor("w2", [8 * 4096, D], F32, kind="ExternalInput").ap()
        dr["b1c"] = nc.dram_tensor("b1c", [8 * 128, 32], F32, kind="ExternalInput").ap()
        dr["b2c"] = nc.dram_tensor("b2c", [8 * 128, 8], F32, kind="ExternalInput").ap()
        dr["wr"] = nc.dram_tensor("wr", [D, 8], F32, kind="ExternalInput").ap()
        dr["brb"] = nc.dram_tensor("brb", [128, 8], F32, kind="ExternalInput").ap()
        dr["iotd"] = nc.dram_tensor("iotd", [128, 8], F32, kind="ExternalInput").ap()
        dr["sel8"] = nc.dram_tensor("sel8", [8, 8 * 128], F32, kind="ExternalInput").ap()
        dr["ident"] = nc.dram_tensor("ident", [128, 128], F32, kind="ExternalInput").ap()
        dr["ones1"] = nc.dram_tensor("ones1", [1, 128], F32, kind="ExternalInput").ap()
        dr["onesc"] = nc.dram_tensor("onesc", [128, 1], F32, kind="ExternalInput").ap()
        dr["out"] = nc.dram_tensor("outT", [D, NT], F32, kind="ExternalOutput").ap()
    else:
        nc, dr = _NcProxy(fz["nc"], "s4_"), fz["dr"]

    with ExitStack() as ctx:
        e = ctx.enter_context
        if fz is None:
            stream = [e(nc.sbuf_tensor(f"strm{c}", [128, GUARD + NT + 6], F16))
                      for c in range(NCH)]
        else:
            stream = fz["stream"]
        hbuf = [e(nc.sbuf_tensor(f"hb{c}", [128, 1024], F16)) for c in range(32)]
        zacc = [e(nc.sbuf_tensor(f"za{c}", [128, 1024], F16)) for c in range(NCH)]
        tz = [e(nc.sbuf_tensor(f"tz{i}", [128, 512], F16)) for i in range(2)]
        xf32 = [e(nc.sbuf_tensor(f"xf{i}", [128, 1024], F32)) for i in range(2)]
        wrbuf = e(nc.sbuf_tensor("wrbuf", [128, 64], F32))
        brb = e(nc.sbuf_tensor("brb_s", [128, 8], F32))
        iotd = e(nc.sbuf_tensor("iotd_s", [128, 8], F32))
        sel8 = e(nc.sbuf_tensor("sel8_s", [8, 8 * 128], F16))
        ident = e(nc.sbuf_tensor("ident_s", [128, 128], F16))
        identf = e(nc.sbuf_tensor("identf_s", [128, 128], F32))
        ones1 = e(nc.sbuf_tensor("ones1_s", [1, 128], F16))
        onesc = e(nc.sbuf_tensor("onesc_s", [128, 1], F16))
        lgt = e(nc.sbuf_tensor("lgt", [128, 8], F32))
        sc1 = e(nc.sbuf_tensor("sc1", [128, 8], F32))
        nmx = e(nc.sbuf_tensor("nmx", [128, 4], F32))
        ctok = e(nc.sbuf_tensor("ctok", [128, 256], F32))
        ce8 = e(nc.sbuf_tensor("ce8", [8, NT], F16))
        cb16 = e(nc.sbuf_tensor("cb16", [128, 1024], F16))
        w1sl = [e(nc.sbuf_tensor(f"w1sl{i}", [128, 8 * 256], F16)) for i in range(2)]
        w2sl = [e(nc.sbuf_tensor(f"w2sl{i}", [128, 32 * 128], F16)) for i in range(2)]
        b1buf = e(nc.sbuf_tensor("b1buf", [128, 32], F32))
        b2buf = e(nc.sbuf_tensor("b2buf", [128, 8], F32))
        sqb = [e(nc.sbuf_tensor(f"sqb{i}", [128, TT], F16)) for i in range(2)]
        srow = e(nc.sbuf_tensor("srow", [1, TT], F32))
        qrow = e(nc.sbuf_tensor("qrow", [1, TT], F32))
        rstd = e(nc.sbuf_tensor("rstd", [1, TT], F32))
        rstd16 = e(nc.sbuf_tensor("rstd16", [1, TT], F16))
        murs16 = e(nc.sbuf_tensor("murs16", [1, TT], F16))

        psA = [e(nc.psum_tensor(f"psA{i}", [128, 512], F32)) for i in range(2)]
        psS = [e(nc.psum_tensor(f"psS{i}", [128, 512], F32)) for i in range(2)]
        psR = psS

        p = fz["p"] if fz else P(nc)
        sLd = p.sem("sLd")
        sRt = p.sem("sRt")
        sRs = p.sem("sRs")
        sW = p.sem("sW4")
        sGe = p.sem("sGe")
        sZa = p.sem("sZa")
        sCb = p.sem("sCb")
        sSt = p.sem("sSt")
        proj = ProjPipe(p, psA, "proj")

        if fz is None:
            for c in range(NCH):
                p.op("gpsimd", lambda e, c=c: e.dma_start(
                    out=stream[c][:, GUARD:GUARD + NT],
                    in_=dr["xT"][c * 128:(c + 1) * 128, :]), inc=sLd, inc_by=16)
        consts = [(wrbuf, "wr"), (brb, "brb"), (iotd, "iotd"), (sel8, "sel8"),
                  (ident, "ident"), (identf, "ident"), (ones1, "ones1"),
                  (onesc, "onesc")]
        first_g = True
        for sb, nm in consts:
            if nm == "wr":
                p.op("sync", lambda e, sb=sb: e.dma_start(
                    out=sb[:, :].rearrange("p (c n) -> p c n", c=8),
                    in_=dr["wr"].rearrange("(c p) n -> p c n", p=128)),
                    waits=[fz["prev"]] if fz else (), inc=sLd, inc_by=16)
            else:
                p.op("gpsimd", lambda e, sb=sb, nm=nm: e.dma_start(out=sb[:, :],
                                                                   in_=dr[nm]),
                     waits=[fz["prev"]] if (fz and first_g) else (),
                     inc=sLd, inc_by=16)
                first_g = False
        all_loaded = sLd.w()

        for tc in range(32):
            xf = xf32[tc % 2]
            for k in range(NCH):
                waits = [all_loaded] if (tc == 0 and k == 0) else []
                if tc >= 2 and k == 0:
                    waits.append((proj.sMM, proj.sMM.n))

                def cst(e, k=k, tc=tc, xf=xf):
                    return e.tensor_copy(xf[:, k * 128:(k + 1) * 128],
                                         stream[k][:, GUARD + tc * 128:
                                                   GUARD + (tc + 1) * 128])
                p.op("vector", cst, waits=waits, inc=sRt)
            cast_done = sRt.w()

            def evl2(e, ap):
                return e.tensor_add(lgt[:, :], ap, brb[:, :])
            proj.group([xf[:, k * 128:(k + 1) * 128] for k in range(NCH)],
                       [wrbuf[:, k * 8:(k + 1) * 8] for k in range(NCH)],
                       evl2, width=8, extra_wait=[cast_done],
                       evac_wait=[(sRt, sRt.n)])
            lg_done = proj.drain()
            p.op("vector", lambda e: e.reduce_max(nmx[:, 0:1], lgt[:, :], axis=AX.X,
                                                  negate=True),
                 waits=[lg_done], inc=sRt)

            def ex(e):
                return e.activation(sc1[:, :], lgt[:, :], AF.Exp, bias=nmx[:, 0:1],
                                    scale=1.0, accum_out=nmx[:, 1:2])
            p.op("scalar", ex, waits=[sRt.w()], inc=sRs)
            p.op("vector", lambda e: e.reciprocal(nmx[:, 2:3], nmx[:, 1:2]),
                 waits=[sRs.w()], inc=sRt)
            p.op("vector", lambda e: e.tensor_scalar(sc1[:, :], lgt[:, :],
                                                     nmx[:, 0:1], None, ALU.add),
                 waits=[sRt.w()], inc=sRt)
            p.op("vector", lambda e: e.tensor_scalar(sc1[:, :], sc1[:, :], 0.0, None,
                                                     ALU.is_ge),
                 waits=[sRt.w()], inc=sRt)
            p.op("vector", lambda e: e.tensor_tensor(sc1[:, :], sc1[:, :], iotd[:, :],
                                                     ALU.mult),
                 waits=[sRt.w()], inc=sRt)
            p.op("vector", lambda e: e.reduce_max(nmx[:, 3:4], sc1[:, :], axis=AX.X,
                                                  negate=True),
                 waits=[sRt.w()], inc=sRt)
            p.op("vector", lambda e: e.tensor_scalar(sc1[:, :], sc1[:, :],
                                                     nmx[:, 3:4], None, ALU.add),
                 waits=[sRt.w()], inc=sRt)
            p.op("vector", lambda e: e.tensor_scalar(sc1[:, :], sc1[:, :], 0.0, None,
                                                     ALU.is_ge),
                 waits=[sRt.w()], inc=sRt)

            def cm(e, tc=tc):
                return e.tensor_scalar(ctok[:, tc * 8:(tc + 1) * 8], sc1[:, :],
                                       nmx[:, 2:3], None, ALU.mult)
            p.op("vector", cm, waits=[sRt.w()], inc=sRt)
        ctok_done = sRt.w()

        sTr4 = p.sem("sTr4")
        sTe4 = p.sem("sTe4")
        for tc in range(32):
            waits = [ctok_done] if tc == 0 else []
            if tc >= 2:
                waits.append((sTe4, tc - 1))

            def tr(e, tc=tc):
                return e.transpose(psS[tc % 2][0:8, 0:128],
                                   ctok[:, tc * 8:(tc + 1) * 8], identf[:, :])
            p.op("tensor", tr, waits=waits, inc=sTr4)

            def ev(e, tc=tc):
                return e.tensor_copy(ce8[:, tc * 128:(tc + 1) * 128],
                                     psS[tc % 2][0:8, 0:128])
            p.op("vector", ev, waits=[sTr4.w()], inc=sTe4)
        ce_done = sTe4.w()

        w1_g = 0
        w2_g = 0
        w1_free, w2_free = [], []
        h_free = None
        for tf in range(4):
            tf0 = tf * 1024
            for ei in range(8):
                p.op("sync", lambda e, ei=ei: e.dma_start(
                    out=b1buf[:, :], in_=dr["b1c"][ei * 128:(ei + 1) * 128, :]),
                    waits=[proj.drain()], inc=sLd, inc_by=16)
                p.op("sync", lambda e, ei=ei: e.dma_start(
                    out=b2buf[:, :], in_=dr["b2c"][ei * 128:(ei + 1) * 128, :]),
                    inc=sLd, inc_by=16)
                bias_ready = sLd.w()
                for sub in range(2):
                    def evcb(e, ap, sub=sub):
                        return e.tensor_copy(cb16[:, sub * 512:(sub + 1) * 512], ap)
                    proj.group([sel8[:, ei * 128:(ei + 1) * 128]],
                               [ce8[:, tf0 + sub * 512:tf0 + (sub + 1) * 512]],
                               evcb,
                               extra_wait=[ce_done, (sZa, sZa.n)]
                               if sub == 0 else ())
                cb_done = proj.drain()

                for hc in range(32):
                    bufi = w1_g % 2
                    waits = [w1_free[w1_g - 2]] if w1_g >= 2 else []
                    p.op("gpsimd", lambda e, ei=ei, hc=hc, bufi=bufi: e.dma_start(
                        out=w1sl[bufi][:, 0:8 * 128].rearrange("p (c n) -> p c n",
                                                               c=NCH),
                        in_=dr["w1"][ei * D:(ei + 1) * D, hc * 128:(hc + 1) * 128]
                        .rearrange("(c p) n -> p c n", p=128)),
                        waits=waits, inc=sW, inc_by=16)
                    w1_g += 1
                    w1_ready = sW.w()
                    for sub in range(2):
                        def evg(e, ap, hc=hc, sub=sub):
                            return e.activation(hbuf[hc][:, sub * 512:(sub + 1) * 512],
                                                ap, AF.Gelu,
                                                bias=b1buf[:, hc:hc + 1], scale=1.0)
                        ew = []
                        if sub == 0:
                            ew = [w1_ready, bias_ready]
                            if hc == 0:
                                ew.append(cb_done)
                        proj.group([w1sl[bufi][:, k * 128:(k + 1) * 128]
                                    for k in range(NCH)],
                                   [stream[k][:, GUARD + tf0 + sub * 512:
                                              GUARD + tf0 + (sub + 1) * 512]
                                    for k in range(NCH)],
                                   evg, evac_eng="scalar",
                                   extra_wait=ew,
                                   evac_wait=[h_free] if (h_free and sub == 0)
                                   else ())
                    w1_free.append(proj.sMM.w())
                h_done = proj.drain()
                for fo in range(NCH):
                    bufi = w2_g % 2
                    waits = [w2_free[w2_g - 2]] if w2_g >= 2 else []
                    p.op("gpsimd", lambda e, ei=ei, fo=fo, bufi=bufi: e.dma_start(
                        out=w2sl[bufi][:, :].rearrange("p (c n) -> p c n", c=32),
                        in_=dr["w2"][ei * 4096:(ei + 1) * 4096,
                                     fo * 128:(fo + 1) * 128]
                        .rearrange("(c p) n -> p c n", p=128)),
                        waits=waits, inc=sW, inc_by=16)
                    w2_g += 1
                    w2_ready = sW.w()
                    for sub in range(2):
                        sl = (fo * 2 + sub) % 2

                        def ev1(e, ap, ei=ei, fo=fo, sub=sub, sl=sl):
                            return e.scalar_tensor_tensor(
                                tz[sl][:, :], ap, b2buf[:, fo:fo + 1],
                                cb16[:, sub * 512:(sub + 1) * 512],
                                ALU.add, ALU.mult)

                        def ev2(e, ap, ei=ei, fo=fo, tf0=tf0, sub=sub, sl=sl):
                            if ei == 0:
                                return e.tensor_copy(
                                    zacc[fo][:, tf0 % 1024 + sub * 512:
                                             tf0 % 1024 + (sub + 1) * 512]
                                    if False else
                                    zacc[fo][:, sub * 512:(sub + 1) * 512],
                                    tz[sl][:, :])
                            return e.tensor_add(
                                zacc[fo][:, sub * 512:(sub + 1) * 512],
                                zacc[fo][:, sub * 512:(sub + 1) * 512],
                                tz[sl][:, :])
                        proj.group([w2sl[bufi][:, k * 128:(k + 1) * 128]
                                    for k in range(32)],
                                   [hbuf[k][:, sub * 512:(sub + 1) * 512]
                                    for k in range(32)],
                                   [ev1, ev2],
                                   extra_wait=[w2_ready, h_done]
                                   if (sub == 0 and fo == 0)
                                   else ([w2_ready] if sub == 0 else []),
                                   evac_wait=[cb_done])
                    w2_free.append(proj.sMM.w())
                h_free = proj.sMM.w()
            zdone = proj.drain()
            for fo in range(NCH):
                def addz(e, fo=fo, tf0=tf0):
                    return e.tensor_add(
                        stream[fo][:, GUARD + tf0:GUARD + tf0 + 1024],
                        stream[fo][:, GUARD + tf0:GUARD + tf0 + 1024],
                        zacc[fo][:, 0:1024])
                p.op("vector", addz, waits=[zdone] if fo == 0 else (), inc=sZa)

        moe_done = sZa.w()
        ln_bufs = dict(sq=sqb, psR=psR, srow=srow, qrow=qrow, rstd=rstd,
                       rstd16=rstd16, murs16=murs16,
                       onesc=onesc[:, :], ones1=ones1[:, :])
        ln_done = emit_ln(p, stream, ln_bufs, "4", pre_wait=[moe_done])

        for c in range(NCH):
            p.op("gpsimd", lambda e, c=c: e.dma_start(
                out=dr["out"][c * 128:(c + 1) * 128, :],
                in_=stream[c][:, GUARD:GUARD + NT]),
                waits=[ln_done] if c == 0 else (), inc=sSt, inc_by=16)
        p.wait("gpsimd", sSt.w())
        if fz is None:
            p.alloc_sems(ctx)
            block = e(nc.Block())
            p.emit(block)
    return nc


def stage4_inputs(inputs, b, xC=None):
    f = np.asarray
    E = 8
    b1 = f(inputs["L1_ffn_b1"])
    b1c = b1.reshape(E, 32, 128).transpose(0, 2, 1).reshape(E * 128, 32)
    b2 = f(inputs["L1_ffn_b2"])
    b2c = b2.reshape(E, 8, 128).transpose(0, 2, 1).reshape(E * 128, 8)
    sel8 = np.zeros((8, 8 * 128), np.float32)
    for ei in range(8):
        sel8[ei, ei * 128:(ei + 1) * 128] = 1.0
    d = {
        "xT": np.ascontiguousarray(xC[b].T),
        "w1": f(inputs["L1_ffn_W1"]).reshape(E * D, 4096),
        "w2": f(inputs["L1_ffn_W2"]).reshape(E * 4096, D),
        "b1c": b1c, "b2c": b2c,
        "wr": f(inputs["L1_ffn_Wr"]),
        "brb": np.broadcast_to(f(inputs["L1_ffn_br"]).reshape(1, 8), (128, 8)),
        "iotd": np.broadcast_to((8.0 - np.arange(8, dtype=np.float32)).reshape(1, 8),
                                (128, 8)),
        "sel8": sel8,
        "ident": np.eye(128, dtype=np.float32),
        "ones1": np.ones((1, 128), np.float32),
        "onesc": np.ones((128, 1), np.float32),
    }
    return {k: np.ascontiguousarray(np.asarray(v, dtype=np.float32))
            for k, v in d.items()}


def _band_masks():
    ii = np.arange(128)[:, None]
    m_first = np.where(np.abs(ii - np.arange(256)[None, :]) <= 128, 0.0, MASKVAL)
    jj = np.arange(384)[None, :]
    m_int = np.where((jj >= ii) & (jj <= ii + 256), 0.0, MASKVAL)
    m_last = np.ascontiguousarray(m_int[:, 0:256])
    return (m_first.astype(np.float32), m_int.astype(np.float32),
            m_last.astype(np.float32))


def stage1_inputs(inputs, b):
    mf, mi, ml = _band_masks()
    f = np.asarray
    d = {
        "xT": np.ascontiguousarray(f(inputs["L0_attn_x"][b]).T),
        "wq": np.ascontiguousarray(f(inputs["L0_attn_Wq"]).T),
        "wk": np.ascontiguousarray(f(inputs["L0_attn_Wk"]).T),
        "wv": np.ascontiguousarray(f(inputs["L0_attn_Wv"]).T),
        "wp": np.ascontiguousarray(f(inputs["L0_attn_Wp"]).T),
        "bq": np.ascontiguousarray(f(inputs["L0_attn_bq"]).reshape(8, 128).T),
        "bk": np.ascontiguousarray(f(inputs["L0_attn_bk"]).reshape(8, 128).T),
        "bv": np.ascontiguousarray(f(inputs["L0_attn_bv"]).reshape(8, 128).T),
        "bp": np.ascontiguousarray(f(inputs["L0_attn_bp"]).reshape(8, 128).T),
        "ident": np.eye(128, dtype=np.float32),
        "ones1": np.ones((1, 128), np.float32),
        "onesc": np.ones((128, 1), np.float32),
        "mfirst": mf, "mint": mi, "mlast": ml,
        "bvr": f(inputs["L0_attn_bv"]).reshape(1, D),
    }
    return {k: np.ascontiguousarray(np.asarray(v, dtype=np.float32)) for k, v in d.items()}


def build_fused():
    nc = bass.Bass("TRN2", target_bir_lowering=False)

    def A(name, shape):
        return nc.dram_tensor(name, shape, F32, kind="ExternalInput").ap()

    com = {"ident": A("ident", [128, 128]), "ones1": A("ones1", [1, 128]),
           "onesc": A("onesc", [128, 1])}
    d1 = dict(com)
    d1["xT"] = A("xT", [D, NT])
    for w in ("wq", "wk", "wv", "wp"):
        d1[w] = A(w, [D, D])
    for b in ("bq", "bk", "bv", "bp"):
        d1[b] = A(b, [128, 8])
    d1["mfirst"] = A("mfirst", [128, 256])
    d1["mint"] = A("mint", [128, 384])
    d1["mlast"] = A("mlast", [128, 256])
    d1["bvr"] = A("bvr", [1, D])
    NL = 6
    d2 = dict(com)
    d2["wqkv"] = A("wqkv", [NL * D, 3072])
    d2["wo"] = A("wo", [NL * D, D])
    d2["w1"] = A("w1", [NL * D, 5632])
    d2["w2"] = A("w2", [NL * 2816, D])
    d2["b1c"] = A("b1c", [NL * 128, 44])
    d2["b2c"] = A("b2c", [NL * 128, 8])
    d2["lngc"] = A("lngc", [NL * 128, 8])
    d2["lnbc"] = A("lnbc", [NL * 128, 8])
    for nm in ("qsc", "ksc", "qss", "kss"):
        d2[nm] = A(nm, [NL * 128, 1])
    for nm in ("tq1", "tq2", "tk1", "tk2"):
        d2[nm] = A(nm, [128, NT])
    d2["prot"] = A("prot", [128, 128])
    d2["bones2"] = A("bones2", [128, 2])
    d2["sel2"] = A("sel2", [2, 128])
    d2["m_a"] = A("m_a", [128, 512])
    d2["m_b"] = A("m_b", [128, 512])
    d2["m0a"] = A("m0a", [128, 256])
    d2["m0b"] = A("m0b", [128, 256])
    d3 = dict(com)
    for nm in ("qr", "kr", "vr", "wz"):
        d3[nm] = A(nm, [D, D])
    for k in range(4):
        d3[f"md{k}"] = A(f"md{k}", [128, 512])
    d4 = dict(com)
    d4["w1"] = A("e1", [8 * D, 4096])
    d4["w2"] = A("e2", [8 * 4096, D])
    d4["b1c"] = A("eb1", [8 * 128, 32])
    d4["b2c"] = A("eb2", [8 * 128, 8])
    d4["wr"] = A("wr", [D, 8])
    d4["brb"] = A("brb", [128, 8])
    d4["iotd"] = A("iotd", [128, 8])
    d4["sel8"] = A("sel8", [8, 8 * 128])
    d4["out"] = nc.dram_tensor("outT", [D, NT], F32, kind="ExternalOutput").ap()
    xa = nc.dram_tensor("xa_scr", [D, NT], F32, kind="Internal").ap()
    d2["xa"] = xa

    with ExitStack() as ctx:
        e = ctx.enter_context
        stream = [e(nc.sbuf_tensor(f"strm{c}", [128, GUARD + NT + 6], F16))
                  for c in range(NCH)]
        p = P(nc)
        fz = {"nc": nc, "p": p, "stream": stream, "xa_scr": xa,
              "prev": None, "xa_done": None, "dr": d1}
        build_stage1(fz=fz)
        fz["dr"] = d2
        build_stage2(list(range(6)), True, fz=fz)
        fz["dr"] = d3
        build_stage3(fz=fz)
        fz["dr"] = d4
        build_stage4(fz=fz)
        p.alloc_sems(ctx)
        block = e(nc.Block())
        p.emit(block)
    return nc


def fused_inputs(inputs):
    z = np.zeros((1, NT, D), np.float32)
    m = {}
    b2 = stage2_inputs(inputs, 0, xA=z)
    b2.pop("xT")
    m.update(b2)
    b3 = stage3_inputs(inputs, 0, xB=z)
    b3.pop("xT")
    m.update(b3)
    b4 = stage4_inputs(inputs, 0, xC=z)
    b4.pop("xT")
    for a, bnm in (("e1", "w1"), ("e2", "w2"), ("eb1", "b1c"), ("eb2", "b2c")):
        m[a] = b4.pop(bnm)
    m.update(b4)
    m.update(stage1_inputs(inputs, 0))
    return m


def kernel(**inputs):
    nc = build_fused()
    base = fused_inputs(inputs)
    maps = [dict(base, xT=np.ascontiguousarray(np.asarray(inputs["L0_attn_x"][b]).T))
            for b in range(8)]
    res = run_bass_kernel_spmd(nc, maps, core_ids=list(range(8)))
    return np.stack([np.ascontiguousarray(r["outT"].T) for r in res.results])


def kernel_unfused(**inputs):
    cores = list(range(8))
    dummy = np.zeros((8, 1, 1), np.float32)

    def shard_x():
        return [np.ascontiguousarray(np.asarray(inputs["L0_attn_x"][b]).T)
                for b in range(8)]

    cur = shard_x()
    base = stage1_inputs(inputs, 0)
    maps = [dict(base, xT=cur[b]) for b in range(8)]
    res = run_bass_kernel_spmd(build_stage1(), maps, core_ids=cores)
    cur = [np.ascontiguousarray(r["outT"]) for r in res.results]

    base = stage2_inputs(inputs, 0, xA=np.zeros((1, NT, D), np.float32))
    maps = [dict(base, xT=cur[b]) for b in range(8)]
    res = run_bass_kernel_spmd(build_stage2(list(range(6)), True), maps,
                               core_ids=cores)
    cur = [np.ascontiguousarray(r["outT"]) for r in res.results]

    base = stage3_inputs(inputs, 0, xB=np.zeros((1, NT, D), np.float32))
    maps = [dict(base, xT=cur[b]) for b in range(8)]
    res = run_bass_kernel_spmd(build_stage3(), maps, core_ids=cores)
    cur = [np.ascontiguousarray(r["outT"]) for r in res.results]

    base = stage4_inputs(inputs, 0, xC=np.zeros((1, NT, D), np.float32))
    maps = [dict(base, xT=cur[b]) for b in range(8)]
    res = run_bass_kernel_spmd(build_stage4(), maps, core_ids=cores)
    return np.stack([np.ascontiguousarray(r["outT"].T) for r in res.results])
